# version 06
# baseline (speedup 1.0000x reference)
import contextlib
import numpy as np
import ml_dtypes
import concourse.bass as bass
import concourse.mybir as mybir
from concourse.bass_utils import run_bass_kernel_spmd

F32 = mybir.dt.float32
F32R = mybir.dt.float32r
BF16 = mybir.dt.bfloat16
AF = mybir.ActivationFunctionType
ALU = mybir.AluOpType
AX = mybir.AxisListType.X

NCORE = 8
TOK = 1024
D = 2048
NCH = 16
NTOK = 8192
DSL = 256
LN_EPS = 1e-5
NEXP = 9
NI = 11
_EL1 = None
_POOL_OFF = True
_DBG_L0 = False



def _host_consts():
    c = {}
    c["ones_col"] = np.ones((128, 1), np.float32)
    c["ones_row"] = np.ones((1, 128), np.float32)
    c["ident"] = np.eye(128, dtype=np.float32)
    p = np.arange(128)
    c["m16e"] = (p[None, :] // 16 == np.arange(8)[:, None]).astype(np.float32)
    oh8 = np.zeros((8, 8 * 128), np.float32)
    for e in range(8):
        oh8[e, e * 128:(e + 1) * 128] = 1.0
    c["oh8"] = oh8
    e_b = np.zeros((32, 128), np.float32)
    e_c = np.zeros((32, 128), np.float32)
    for k in range(16):
        e_b[k] = (p % 16 == k)
        e_c[k + 16] = (p % 16 == k)
    c["e_b"], c["e_c"] = e_b, e_c
    g = np.zeros((16, 128, 128), np.float32)
    for gl in range(16):
        for pp in range(128):
            g[gl, gl * 8 + pp // 16, pp] = 1.0
    c["g_pat"] = np.ascontiguousarray(g.reshape(16 * 128, 128))
    c["r_pat"] = (p[:, None] // 16 == np.arange(8)[None, :]).astype(np.float32)
    return c


def _prep(inputs):
    h = dict(_host_consts())
    f32 = np.float32
    x = np.asarray(inputs["L0_attn_x"], f32).reshape(NTOK, D)
    h["xT"] = np.ascontiguousarray(x.T)

    dw = np.asarray(inputs["L0_ffn_down_w"], f32)
    h["dnT"] = np.ascontiguousarray(dw.transpose(2, 0, 1).reshape(D, 128))
    uw = np.asarray(inputs["L0_ffn_up_w"], f32)
    h["upT"] = np.ascontiguousarray(uw.transpose(0, 2, 1).reshape(128, D))
    h["rtT"] = np.ascontiguousarray(np.asarray(inputs["L0_ffn_router_w"], f32).T)

    h["xprojT"] = np.ascontiguousarray(np.asarray(inputs["L0_attn_x_proj_w"], f32).T)
    dtT = np.ascontiguousarray(np.asarray(inputs["L0_attn_dt_proj_w"], f32).T)
    h["dtprojT_sl"] = [np.ascontiguousarray(dtT[:, c * DSL:(c + 1) * DSL]) for c in range(NCORE)]
    dtb = np.asarray(inputs["L0_attn_dt_proj_b"], f32)
    h["dtb_sl"] = [np.ascontiguousarray(dtb[c * DSL:(c + 1) * DSL].reshape(2, 128).T)
                   for c in range(NCORE)]
    h["D_sb"] = np.ascontiguousarray(np.asarray(inputs["L0_attn_D"], f32).reshape(16, 128).T)
    al = np.asarray(inputs["L0_attn_A_log"], f32)
    alp = np.zeros((NCORE, 128, 32), f32)
    for core in range(NCORE):
        for g in range(32):
            for pp in range(128):
                alp[core, pp, g] = al[core * DSL + g * 8 + pp // 16, pp % 16]
    h["alogp"] = alp

    bf = ml_dtypes.bfloat16
    for nm in ["wq", "wk", "wv", "wo"]:
        h[nm + "T"] = np.ascontiguousarray(
            np.asarray(inputs["L1_attn_" + nm], f32).T.astype(bf))
    h["wmqT"] = np.ascontiguousarray(
        np.asarray(inputs["L1_attn_wmap_q"], f32).T.astype(bf))
    h["wmkT"] = np.ascontiguousarray(
        np.asarray(inputs["L1_attn_wmap_k"], f32).T.astype(bf))
    for nm in ["bq", "bk", "bo"]:
        h[nm] = np.ascontiguousarray(
            np.asarray(inputs["L1_attn_" + nm], f32).reshape(16, 128).T)
    h["bvR"] = np.asarray(inputs["L1_attn_bv"], f32).reshape(1, D).astype(bf)

    w1 = np.asarray(inputs["L1_ffn_w1"]); w3 = np.asarray(inputs["L1_ffn_w3"])
    w2 = np.asarray(inputs["L1_ffn_w2"])
    h["w1T"] = np.ascontiguousarray(
        np.concatenate([w1, np.asarray(inputs["L1_ffn_sw1"])[None]], 0).transpose(0, 2, 1))
    h["w3T"] = np.ascontiguousarray(
        np.concatenate([w3, np.asarray(inputs["L1_ffn_sw3"])[None]], 0).transpose(0, 2, 1))
    h["w2T"] = np.ascontiguousarray(
        np.concatenate([w2, np.asarray(inputs["L1_ffn_sw2"])[None]], 0).transpose(0, 2, 1))
    h["gateT"] = np.ascontiguousarray(np.asarray(inputs["L1_ffn_gate_w"], f32).T)
    h["gate_b"] = np.asarray(inputs["L1_ffn_gate_b"], f32).reshape(1, 8)
    return h


_DRAM_SPECS = {
    "xT": ([D, TOK], F32), "dnT": ([D, 128], F32), "upT": ([128, D], F32),
    "rtT": ([D, 8], F32), "m16e": ([8, 128], F32),
    "xprojT": ([D, 160], F32), "dtprojT_sl": ([128, DSL], F32),
    "dtb_sl": ([128, 2], F32), "alogp": ([128, 32], F32), "D_sb": ([128, 16], F32),
    "g_pat": ([16 * 128, 128], F32), "e_b": ([32, 128], F32), "e_c": ([32, 128], F32),
    "r_pat": ([128, 8], F32),
    "wqT": ([D, D], BF16), "wkT": ([D, D], BF16), "wvT": ([D, D], BF16),
    "woT": ([D, D], BF16), "wmqT": ([128, 64], BF16), "wmkT": ([128, 64], BF16),
    "bq": ([128, 16], F32), "bk": ([128, 16], F32), "bo": ([128, 16], F32),
    "bvR": ([1, D], BF16),
    "w1T": ([NEXP, D, 1408], BF16), "w3T": ([NEXP, D, 1408], BF16),
    "w2T": ([NEXP, 1408, D], BF16), "gateT": ([D, 8], F32), "gate_b": ([1, 8], F32),
    "oh8": ([8, 8 * 128], F32),
    "ones_col": ([128, 1], F32), "ones_row": ([1, 128], F32), "ident": ([128, 128], F32),
}
_PERCORE = {"dtprojT_sl", "dtb_sl", "alogp"}

_NEED = {
    "l0ffn": ["dnT", "upT", "rtT", "m16e"],
    "l0attn": ["xprojT", "dtprojT_sl", "dtb_sl", "alogp", "D_sb", "g_pat",
               "e_b", "e_c", "r_pat"],
    "l1ffn": ["w1T", "w3T", "w2T", "gateT", "gate_b", "oh8"],
    "l1attn": ["wqT", "wkT", "wvT", "woT", "wmqT", "wmkT", "bq", "bk", "bo", "bvR"],
}
_CONST_KEYS = ["ones_col", "ones_row", "ident"]



class Sem:
    def __init__(self, nc, name):
        self.h = nc.semaphore(name).__enter__()
        self.n = 0

    def inc(self, k=1):
        self.n += k
        return self.h


class Ctx:
    def __init__(self, nc):
        self.nc = nc
        self._lp = nc.allow_low_precision(reason="fp32r matmul operand tiles")
        self._lp.__enter__()
        self.P = [nc.alloc_psum_tensor(f"psP{i}", [128, 1024], F32) for i in range(4)]
        self.jC = nc.alloc_sbuf_tensor("ctx_jnk", [128, 64], F32)
        self.sems = {}

    def sem(self, name):
        if name not in self.sems:
            self.sems[name] = Sem(self.nc, name)
        return self.sems[name]


def sb_scoped(es, nc, name, shape, dt):
    return es.enter_context(nc.sbuf_tensor(name, shape, dt))


def dmai(eng, out, in_, s):
    eng.dma_start(out=out, in_=in_).then_inc(s.inc(16), 16)


def emit_load_consts(C, dd, sy, s_ld):
    nc = C.nc
    C.consts = {}
    C.consts["ones_col"] = nc.alloc_sbuf_tensor("c_onescol", [128, 1], F32R)
    dmai(sy, C.consts["ones_col"][:], dd["ones_col"].bitcast(F32R), s_ld)
    C.consts["ones_row"] = nc.alloc_sbuf_tensor("c_onesrow", [1, 128], F32R)
    dmai(sy, C.consts["ones_row"][:], dd["ones_row"].bitcast(F32R), s_ld)
    C.consts["ident"] = nc.alloc_sbuf_tensor("c_ident", [128, 128], F32)
    dmai(sy, C.consts["ident"][:], dd["ident"], s_ld)


def emit_ln(C, X, OUT, pfx):
    nc = C.nc
    P = C.P
    consts = C.consts
    s_sq = C.sem("ln_sq")
    s_sqf = C.sem("ln_sqf")
    b_sq, b_sqf = s_sq.n, s_sqf.n
    les = contextlib.ExitStack()
    sq = [sb_scoped(les, nc, f"{pfx}_sq{i}", [128, 1024], F32R) for i in range(2)]
    m_sb = sb_scoped(les, nc, f"{pfx}_m", [1, 1024], F32R)
    r_sb = sb_scoped(les, nc, f"{pfx}_r", [1, 1024], F32R)
    t_sb = sb_scoped(les, nc, f"{pfx}_t", [1, 1024], F32)
    v_sb = sb_scoped(les, nc, f"{pfx}_v", [1, 1024], F32)
    sd_sb = sb_scoped(les, nc, f"{pfx}_sd", [1, 1024], F32)
    nt = sb_scoped(les, nc, f"{pfx}_nt", [128, 1024], F32)

    with nc.Block() as blk:
        @blk.scalar
        def _(sc):
            for c in range(NCH):
                if c >= 2:
                    sc.wait_ge(s_sqf.h, b_sqf + c - 1)
                sc.activation(sq[c % 2][:], X[c][:].bitcast(F32), AF.Square) \
                    .then_inc(s_sq.inc(), 1)

        @blk.tensor
        def _(tn):
            for c in range(NCH):
                for hh in range(2):
                    cl = slice(hh * 512, (hh + 1) * 512)
                    tn.matmul(P[2][0:1, cl], consts["ones_col"][:], X[c][:, cl],
                              start=(c == 0), stop=(c == NCH - 1))
                tn.wait_ge(s_sq.h, b_sq + c + 1)
                for hh in range(2):
                    cl = slice(hh * 512, (hh + 1) * 512)
                    ins = tn.matmul(P[3][0:1, cl], consts["ones_col"][:], sq[c % 2][:, cl],
                                    start=(c == 0), stop=(c == NCH - 1))
                ins.then_inc(s_sqf.inc(), 1)

    with nc.Block() as blk:
        @blk.vector
        def _(v):
            v.tensor_scalar(m_sb[:], P[2][0:1, :], 1.0 / D, None, ALU.mult)
            v.tensor_scalar(v_sb[:], P[3][0:1, :], 1.0 / D, None, ALU.mult)
            v.tensor_tensor(t_sb[:], m_sb[:].bitcast(F32), m_sb[:].bitcast(F32), ALU.mult)
            v.tensor_tensor(v_sb[:], v_sb[:], t_sb[:], ALU.subtract)
            v.tensor_scalar(v_sb[:], v_sb[:], LN_EPS, None, ALU.add)
            v.memset(C.jC[:], 0.0)

    with nc.Block() as blk:
        @blk.scalar
        def _(sc):
            sc.activation(sd_sb[:], v_sb[:], AF.Sqrt)
            sc.memzero(C.jC[:])

    with nc.Block() as blk:
        @blk.vector
        def _(v):
            v.reciprocal(r_sb[:], sd_sb[:])
            v.memset(C.jC[:], 0.0)
            v.memset(C.jC[:], 0.0)

    with nc.Block() as blk:
        @blk.tensor
        def _(tn):
            for hh in range(2):
                cl = slice(hh * 512, (hh + 1) * 512)
                tn.matmul(P[2][:, cl], consts["ones_row"][:], m_sb[0:1, cl],
                          start=True, stop=True)
                tn.matmul(P[3][:, cl], consts["ones_row"][:], r_sb[0:1, cl],
                          start=True, stop=True)

    with nc.Block() as blk:
        @blk.vector
        def _(v):
            for c in range(NCH):
                v.tensor_tensor(nt[:], X[c][:].bitcast(F32), P[2][:], ALU.subtract)
                v.tensor_tensor(OUT[c][:], nt[:], P[3][:], ALU.mult)
            v.memset(C.jC[:], 0.0)
    les.close()


def emit_load_stream(C, X, d_x, sy, s_ld):
    xv = d_x.rearrange("(c p) t -> c p t", p=128)
    for c in range(NCH):
        dmai(sy, X[c][:], xv[c].bitcast(F32R), s_ld)


def emit_store_stream(C, X, d_y, s_st, waits=()):
    nc = C.nc
    with nc.Block() as blk:
        @blk.sync
        def _(sy):
            for s, val in waits:
                sy.wait_ge(s.h, val)
            yv = d_y.rearrange("(c p) t -> c p t", p=128)
            for c in range(NCH):
                dmai(sy, yv[c], X[c][:].bitcast(F32), s_st)
            sy.wait_ge(s_st.h, s_st.n)



def emit_l0ffn(C, X, dd, pfx):
    nc = C.nc
    P = C.P
    es = contextlib.ExitStack()
    sbs = lambda n, shp, dt: sb_scoped(es, nc, pfx + n, shp, dt)

    XN = [sbs(f"xn{c}", [128, 1024], F32R) for c in range(NCH)]
    dn_sb = sbs("dn", [128, 16 * 128], F32R)
    up_sb = sbs("up", [128, D], F32R)
    rt_sb = sbs("rt", [128, 16 * 8], F32R)
    m16_sb = sbs("m16", [8, 128], F32R)
    a_sb = sbs("a", [128, 1024], F32R)
    ag_sb = sbs("ag", [128, 1024], F32R)
    lg_sb = sbs("lg", [8, 1024], F32)
    sct = sbs("sct", [128, 64], F32)
    et = sbs("et", [128, 64], F32)
    ex_t = sbs("ext", [128, 64], F32)
    mx8 = sbs("mx8", [128, 8], F32)
    sm8 = sbs("sm8", [128, 8], F32)
    r8 = sbs("r8", [128, 8], F32)
    jnk = sbs("jnk", [128, 8], F32)
    m1 = sbs("m1", [128, 64], F32)
    m2 = sbs("m2", [128, 64], F32)
    sup = sbs("sup", [128, 64], F32)
    gate_t = sbs("gatet", [128, 64], F32)
    gate_e = sbs("gatee", [8, 1024], F32R)

    s_ld = C.sem("f0_ld")
    s_up = C.sem("f0_up")
    s_upf = C.sem("f0_upf")
    b_up, b_upf = s_up.n, s_upf.n

    with nc.Block() as blk:
        @blk.sync
        def _(sy):
            dmai(sy, dn_sb[:].rearrange("p (c m) -> p c m", c=16),
                 dd["dnT"].rearrange("(c p) m -> p c m", p=128).bitcast(F32R), s_ld)
            dmai(sy, up_sb[:], dd["upT"].bitcast(F32R), s_ld)
            dmai(sy, rt_sb[:].rearrange("p (c m) -> p c m", c=16),
                 dd["rtT"].rearrange("(c p) m -> p c m", p=128).bitcast(F32R), s_ld)
            dmai(sy, m16_sb[:], dd["m16e"].bitcast(F32R), s_ld)
            sy.wait_ge(s_ld.h, s_ld.n)

    emit_ln(C, X, XN, pfx + "ln")

    with nc.Block() as blk:
        @blk.tensor
        def _(tn):
            for c in range(NCH):
                for hh in range(2):
                    cl = slice(hh * 512, (hh + 1) * 512)
                    tn.matmul(P[0][0:8, cl], rt_sb[:, c * 8:(c + 1) * 8], XN[c][:, cl],
                              start=(c == 0), stop=(c == NCH - 1))
                    tn.matmul(P[1][:, cl], dn_sb[:, c * 128:(c + 1) * 128], XN[c][:, cl],
                              start=(c == 0), stop=(c == NCH - 1))

    with nc.Block() as blk:
        @blk.scalar
        def _(sc):
            sc.activation(a_sb[:], P[1][:], AF.Silu)
            sc.memzero(C.jC[:])

        @blk.vector
        def _(v):
            v.tensor_copy(lg_sb[:], P[0][0:8, :])
            v.memset(C.jC[:], 0.0)

    with nc.Block() as blk:
        @blk.tensor
        def _(tn):
            for t8 in range(8):
                tn.transpose(P[2][:, t8 * 8:(t8 + 1) * 8],
                             lg_sb[0:8, t8 * 128:(t8 + 1) * 128],
                             C.consts["ident"][0:8, 0:8])

    S8 = [slice(t8 * 8, (t8 + 1) * 8) for t8 in range(8)]
    with nc.Block() as blk:
        @blk.vector
        def _(v):
            s_hz = C.sem("hz")
            v.tensor_copy(sct[:], P[2][:, 0:64])
            for t8 in range(8):
                v.reduce_max(mx8[:, t8:t8 + 1], sct[:, S8[t8]], axis=AX) \
                    .then_inc(s_hz.inc(), 1)
            v.wait_ge(s_hz.h, s_hz.n)
            for t8 in range(8):
                v.tensor_scalar(et[:, S8[t8]], sct[:, S8[t8]], mx8[:, t8:t8 + 1],
                                None, ALU.subtract)
            v.memset(jnk[:], 0.0)

    with nc.Block() as blk:
        @blk.scalar
        def _(sc):
            sc.activation(ex_t[:], et[:], AF.Exp)
            sc.memzero(C.jC[:])

    with nc.Block() as blk:
        @blk.vector
        def _(v):
            s_hz = C.sem("hz")
            for t8 in range(8):
                v.reduce_sum(sm8[:, t8:t8 + 1], ex_t[:, S8[t8]], axis=AX) \
                    .then_inc(s_hz.inc(), 1)
            v.wait_ge(s_hz.h, s_hz.n)
            v.reciprocal(r8[:], sm8[:]).then_inc(s_hz.inc(), 1)
            v.wait_ge(s_hz.h, s_hz.n)
            for t8 in range(8):
                v.tensor_scalar(sct[:, S8[t8]], ex_t[:, S8[t8]], r8[:, t8:t8 + 1],
                                None, ALU.mult)
            for t8 in range(8):
                v.reduce_max(mx8[:, t8:t8 + 1], sct[:, S8[t8]], axis=AX) \
                    .then_inc(s_hz.inc(), 1)
            v.wait_ge(s_hz.h, s_hz.n)
            for t8 in range(8):
                v.tensor_scalar(m1[:, S8[t8]], sct[:, S8[t8]], mx8[:, t8:t8 + 1],
                                None, ALU.is_ge)
            for t8 in range(8):
                v.scalar_tensor_tensor(sup[:, S8[t8]], m1[:, S8[t8]], -1e30,
                                       sct[:, S8[t8]], ALU.mult, ALU.add)
            for t8 in range(8):
                v.reduce_max(mx8[:, t8:t8 + 1], sup[:, S8[t8]], axis=AX) \
                    .then_inc(s_hz.inc(), 1)
            v.wait_ge(s_hz.h, s_hz.n)
            for t8 in range(8):
                v.tensor_scalar(m2[:, S8[t8]], sup[:, S8[t8]], mx8[:, t8:t8 + 1],
                                None, ALU.is_ge)
            for t8 in range(8):
                v.tensor_tensor(m1[:, S8[t8]], m1[:, S8[t8]], m2[:, S8[t8]], ALU.add)
            for t8 in range(8):
                v.tensor_tensor(gate_t[:, S8[t8]], sct[:, S8[t8]], m1[:, S8[t8]], ALU.mult)
            v.memset(jnk[:], 0.0)

    with nc.Block() as blk:
        @blk.tensor
        def _(tn):
            for t8 in range(8):
                tn.transpose(P[2][0:8, t8 * 128:(t8 + 1) * 128],
                             gate_t[:, t8 * 8:(t8 + 1) * 8],
                             C.consts["ident"][:, 0:128])

    with nc.Block() as blk:
        @blk.vector
        def _(v):
            v.tensor_copy(gate_e[:], P[2][0:8, :])
            v.memset(C.jC[:], 0.0)

    with nc.Block() as blk:
        @blk.tensor
        def _(tn):
            for hh in range(2):
                cl = slice(hh * 512, (hh + 1) * 512)
                tn.matmul(P[3][:, cl], m16_sb[:], gate_e[0:8, cl], start=True, stop=True)

    with nc.Block() as blk:
        @blk.vector
        def _(v):
            v.tensor_tensor(ag_sb[:], a_sb[:].bitcast(F32), P[3][:], ALU.mult)
            v.memset(C.jC[:], 0.0)

    with nc.Block() as blk:
        @blk.tensor
        def _(tn):
            for m in range(NCH):
                if m >= 2:
                    tn.wait_ge(s_upf.h, b_upf + m - 1)
                for hh in range(2):
                    cl = slice(hh * 512, (hh + 1) * 512)
                    ins = tn.matmul(P[m % 2][:, cl], up_sb[:, m * 128:(m + 1) * 128],
                                    ag_sb[:, cl], start=True, stop=True)
                ins.then_inc(s_up.inc(), 1)

        @blk.vector
        def _(v):
            for m in range(NCH):
                v.wait_ge(s_up.h, b_up + m + 1)
                v.scalar_tensor_tensor(X[m][:], P[m % 2][:], 0.5, X[m][:].bitcast(F32),
                                       ALU.mult, ALU.add).then_inc(s_upf.inc(), 1)

    es.close()
    return [(s_upf, s_upf.n)]



def emit_l0attn(C, X, dd, pfx):
    nc = C.nc
    P = C.P
    es0 = contextlib.ExitStack()
    sbs = lambda n, shp, dt: sb_scoped(es0, nc, pfx + n, shp, dt)

    agx_in = nc.dram_tensor(pfx + "agx_in", [D, TOK], F32)
    agx_out = nc.dram_tensor(pfx + "agx_out", [NCORE * D, TOK], F32, addr_space="Shared")
    agd_in = nc.dram_tensor(pfx + "agd_in", [160, TOK], F32)
    agd_out = nc.dram_tensor(pfx + "agd_out", [NCORE * 160, TOK], F32, addr_space="Shared")
    y_mine = nc.dram_tensor(pfx + "y_mine", [8 * DSL, 1024], F32)
    y_full = nc.dram_tensor(pfx + "y_full", [8 * NCORE * DSL, 1024], F32,
                            addr_space="Shared")

    gp_sb = sbs("gp", [128, 16 * 128], F32R)
    eb_sb = sbs("eb", [32, 128], F32R)
    ec_sb = sbs("ec", [32, 128], F32R)
    rp_sb = sbs("rp", [128, 8], F32R)
    dtp_sb = sbs("dtp", [128, DSL], F32R)
    dtb_sb = sbs("dtb", [128, 2], F32)
    alog_sb = sbs("alog", [128, 32], F32)
    apat_sb = sbs("apat", [128, 32], F32)
    D_sb = sbs("Dsb", [128, 16], F32)
    carry = sbs("carry", [128, 32], F32)

    s_ld = C.sem("m_ld")
    s_ag = C.sem("m_ag")
    s_cc = C.sem("m_cc")
    s_u = C.sem("m_u"); s_uf = C.sem("m_uf")
    s_dr = C.sem("m_dr")
    s_gdp = C.sem("m_gdp"); s_gda = C.sem("m_gda")
    s_gup = C.sem("m_gup"); s_gbu = C.sem("m_gbu")
    s_gsc = C.sem("m_gsc"); s_gt0 = C.sem("m_gt0"); s_gt1 = C.sem("m_gt1")
    s_sc1 = C.sem("m_sc1")
    s_gym = C.sem("m_gym"); s_gye = C.sem("m_gye"); s_gyd = C.sem("m_gyd")
    s_yl = C.sem("m_yl")
    s_fin = C.sem("m_fin")
    b = {s: s.n for s in (s_u, s_uf, s_gdp, s_gda, s_gup, s_gbu, s_gsc, s_gt0,
                          s_gt1, s_sc1, s_gym, s_gye, s_gyd, s_yl, s_cc, s_fin)}

    with nc.Block() as blk:
        @blk.sync
        def _(sy):
            dmai(sy, gp_sb[:].rearrange("d (g p) -> d g p", g=16),
                 dd["g_pat"].rearrange("(g d) p -> d g p", d=128).bitcast(F32R), s_ld)
            dmai(sy, eb_sb[:], dd["e_b"].bitcast(F32R), s_ld)
            dmai(sy, ec_sb[:], dd["e_c"].bitcast(F32R), s_ld)
            dmai(sy, rp_sb[:], dd["r_pat"].bitcast(F32R), s_ld)
            dmai(sy, dtp_sb[:], dd["dtprojT_sl"].bitcast(F32R), s_ld)
            dmai(sy, dtb_sb[:], dd["dtb_sl"], s_ld)
            dmai(sy, alog_sb[:], dd["alogp"], s_ld)
            dmai(sy, D_sb[:], dd["D_sb"], s_ld)
            sy.wait_ge(s_ld.h, s_ld.n)

    with nc.Block() as blk:
        @blk.scalar
        def _(sc):
            sc.activation(apat_sb[:], alog_sb[:], AF.Exp)
            sc.memzero(C.jC[:])

    with nc.Block() as blk:
        @blk.vector
        def _(v):
            v.tensor_scalar(apat_sb[:], apat_sb[:], -1.0, None, ALU.mult)
            v.memset(C.jC[:], 0.0)

    es1 = contextlib.ExitStack()
    XN = [sb_scoped(es1, nc, f"{pfx}xn{c}", [128, 1024], F32R) for c in range(NCH)]
    emit_ln(C, X, XN, pfx + "ln")

    xpj_sb = sb_scoped(es1, nc, pfx + "xpj", [128, 16 * 160], F32R)
    dr_sb = sb_scoped(es1, nc, pfx + "drs", [128, 1024], F32R)
    bc_sb = sb_scoped(es1, nc, pfx + "bcs", [32, 1024], F32R)

    with nc.Block() as blk:
        @blk.sync
        def _(sy):
            dmai(sy, xpj_sb[:].rearrange("p (c m) -> p c m", c=16),
                 dd["xprojT"].rearrange("(c p) m -> p c m", p=128).bitcast(F32R), s_ld)
            sy.wait_ge(s_ld.h, s_ld.n)

    with nc.Block() as blk:
        @blk.tensor
        def _(tn):
            for c in range(NCH):
                for hh in range(2):
                    cl = slice(hh * 512, (hh + 1) * 512)
                    tn.matmul(P[0][:, cl], xpj_sb[:, c * 160:c * 160 + 128],
                              XN[c][:, cl], start=(c == 0), stop=(c == NCH - 1))
                    tn.matmul(P[1][0:32, cl], xpj_sb[:, c * 160 + 128:c * 160 + 160],
                              XN[c][:, cl], start=(c == 0), stop=(c == NCH - 1))


    with nc.Block() as blk:
        @blk.vector
        def _(v):
            v.tensor_copy(dr_sb[:], P[0][:])
            v.tensor_copy(bc_sb[:], P[1][0:32, :])
            v.memset(C.jC[:], 0.0)

    with nc.Block() as blk:
        @blk.sync
        def _(sy):
            xv = agx_in.ap().rearrange("(c p) t -> c p t", p=128)
            for c in range(NCH):
                dmai(sy, xv[c], XN[c][:].bitcast(F32), s_ag)
            dmai(sy, agd_in.ap()[0:128, :], dr_sb[:].bitcast(F32), s_ag)
            dmai(sy, agd_in.ap()[128:160, :], bc_sb[:].bitcast(F32), s_ag)

        @blk.gpsimd
        def _(g):
            g.wait_ge(s_ag.h, s_ag.n)
            g.collective_compute("AllGather", ALU.bypass,
                                 replica_groups=[list(range(NCORE))],
                                 ins=[agx_in.ap()], outs=[agx_out.ap()]) \
                .then_inc(s_cc.inc(), 1)
            g.collective_compute("AllGather", ALU.bypass,
                                 replica_groups=[list(range(NCORE))],
                                 ins=[agd_in.ap()], outs=[agd_out.ap()]) \
                .then_inc(s_cc.inc(), 1)
            g.wait_ge(s_cc.h, b[s_cc] + 2)

    es1.close()

    es2 = contextlib.ExitStack()
    sbt = lambda n, shp, dt: sb_scoped(es2, nc, pfx + n, shp, dt)
    u_t = [[sbt(f"u{s}_{dh}", [128, 1024], F32) for dh in range(2)] for s in range(2)]
    drt = [sbt(f"drt{s}", [128, 1024], F32R) for s in range(2)]
    bct = [sbt(f"bct{s}", [32, 1024], F32R) for s in range(2)]
    zt = [sbt(f"zt{dh}", [128, 1024], F32) for dh in range(2)]
    ab2 = [sbt(f"ab2{dh}", [128, 1024], F32) for dh in range(2)]
    el = [sbt(f"el{dh}", [128, 1024], F32) for dh in range(2)]
    delta = [sbt(f"dl{dh}", [128, 1024], F32R) for dh in range(2)]
    du = [sbt(f"du{dh}", [128, 1024], F32R) for dh in range(2)]
    Bp = sbt("Bp", [128, 1024], F32)
    Cp = sbt("Cp", [128, 1024], F32)
    dA = [sbt(f"dA{s}", [128, 1024], F32) for s in range(2)]
    dBu = [sbt(f"dBu{s}", [128, 1024], F32) for s in range(2)]
    h_t = [sbt(f"h{s}", [128, 1024], F32) for s in range(3)]
    tmp = [sbt(f"tmp{s}", [128, 1024], F32R) for s in range(3)]
    y_sb = [sbt(f"ysb{s}", [8, 1024], F32) for s in range(2)]

    NTC = 8
    for tc in range(NTC):
        for arm in nc.switch_core_id(engines=[nc.sync], n=128):
            try:
                rank = int(arm.logical) % NCORE
            except Exception:
                rank = 0
            if tc >= 2:
                nc.sync.wait_ge(s_uf.h, b[s_uf] + tc - 1)
            for dh in range(2):
                nc.sync.dma_start(
                    out=u_t[tc % 2][dh][:],
                    in_=agx_out.ap()[tc * D + rank * DSL + dh * 128:
                                     tc * D + rank * DSL + (dh + 1) * 128, :]) \
                    .then_inc(s_u.h, 16)
        s_u.n += 32

        with nc.Block() as blk:
            @blk.sync
            def _(sy):
                dmai(sy, drt[tc % 2][:], agd_out.ap()[tc * 160:tc * 160 + 128, :]
                     .bitcast(F32R), s_dr)
                dmai(sy, bct[tc % 2][:], agd_out.ap()[tc * 160 + 128:tc * 160 + 160, :]
                     .bitcast(F32R), s_dr)
                sy.wait_ge(s_dr.h, s_dr.n)

        with nc.Block() as blk:
            @blk.tensor
            def _(tn):
                for dh in range(2):
                    for hh in range(2):
                        cl = slice(hh * 512, (hh + 1) * 512)
                        tn.matmul(P[dh][:, cl], dtp_sb[:, dh * 128:(dh + 1) * 128],
                                  drt[tc % 2][:, cl], start=True, stop=True)
                for hh in range(2):
                    cl = slice(hh * 512, (hh + 1) * 512)
                    tn.matmul(P[2][:, cl], eb_sb[:], bct[tc % 2][0:32, cl],
                              start=True, stop=True)
                    tn.matmul(P[3][:, cl], ec_sb[:], bct[tc % 2][0:32, cl],
                              start=True, stop=True)

        with nc.Block() as blk:
            @blk.scalar
            def _(sc):
                for dh in range(2):
                    sc.activation(zt[dh][:], P[dh][:], AF.Identity,
                                  bias=dtb_sb[:, dh:dh + 1])
                sc.memzero(C.jC[:])

        with nc.Block() as blk:
            @blk.vector
            def _(v):
                for dh in range(2):
                    v.scalar_tensor_tensor(ab2[dh][:], zt[dh][:], -1.0, zt[dh][:],
                                           ALU.mult, ALU.max)
                v.tensor_copy(Bp[:], P[2][:])
                v.tensor_copy(Cp[:], P[3][:])
                v.memset(C.jC[:], 0.0)

        with nc.Block() as blk:
            @blk.scalar
            def _(sc):
                for dh in range(2):
                    sc.activation(el[dh][:], ab2[dh][:], AF.Exp, scale=-1.0)
                for dh in range(2):
                    sc.activation(ab2[dh][:], el[dh][:], AF.Ln, bias=1.0)
                sc.memzero(C.jC[:])

        with nc.Block() as blk:
            @blk.vector
            def _(v):
                v.wait_ge(s_u.h, b[s_u] + 32 * (tc + 1))
                for dh in range(2):
                    v.scalar_tensor_tensor(delta[dh][:], zt[dh][:], 0.0, ab2[dh][:],
                                           ALU.max, ALU.add)
                    ins = v.tensor_tensor(du[dh][:], delta[dh][:].bitcast(F32),
                                          u_t[tc % 2][dh][:], ALU.mult)
                    if dh == 1:
                        ins.then_inc(s_uf.inc(), 1)
                v.memset(C.jC[:], 0.0)

        with nc.Block() as blk:
            @blk.tensor
            def _(tn):
                for g in range(33):
                    gq = tc * 32 + g
                    if g < 32:
                        dh = g // 16
                        gl = g % 16
                        for hh in range(2):
                            q2 = 2 * gq + hh
                            cl = slice(hh * 512, (hh + 1) * 512)
                            if q2 >= 2:
                                tn.wait_ge(s_gda.h, b[s_gda] + q2 - 1)
                            tn.matmul(P[0][:, cl], gp_sb[:, gl * 128:(gl + 1) * 128],
                                      delta[dh][:, cl], start=True, stop=True) \
                                .then_inc(s_gdp.inc(), 1)
                            if q2 >= 2:
                                tn.wait_ge(s_gbu.h, b[s_gbu] + q2 - 1)
                            tn.matmul(P[1][:, cl], gp_sb[:, gl * 128:(gl + 1) * 128],
                                      du[dh][:, cl], start=True, stop=True) \
                                .then_inc(s_gup.inc(), 1)
                    if g >= 1:
                        gy = gq - 1
                        if gy >= 2:
                            tn.wait_ge(s_gye.h, b[s_gye] + gy - 1)
                        tn.wait_ge(s_gt1.h, b[s_gt1] + gy + 1)
                        for hh in range(2):
                            cl = slice(hh * 512, (hh + 1) * 512)
                            if hh == 0:
                                tn.wait_ge(s_gt0.h, b[s_gt0] + gy + 1)
                            ins = tn.matmul(P[2 + (gy % 2)][0:8, cl], rp_sb[:],
                                            tmp[gy % 3][:, cl], start=True, stop=True)
                        ins.then_inc(s_gym.inc(), 1)

            @blk.scalar
            def _(sc):
                for g in range(33):
                    gq = tc * 32 + g
                    if g < 32:
                        for hh in range(2):
                            q2 = 2 * gq + hh
                            cl = slice(hh * 512, (hh + 1) * 512)
                            sc.wait_ge(s_gdp.h, b[s_gdp] + q2 + 1)
                            if gq >= 2:
                                sc.wait_ge(s_sc1.h, b[s_sc1] + gq - 1)
                            sc.activation(dA[gq % 2][:, cl], P[0][:, cl], AF.Exp,
                                          scale=apat_sb[:, g:g + 1]) \
                                .then_inc(s_gda.inc(), 1)
                    if g >= 1:
                        gy = gq - 1
                        sc.wait_ge(s_gym.h, b[s_gym] + gy + 1)
                        if gy >= 2:
                            sc.wait_ge(s_gyd.h, b[s_gyd] + 16 * (gy - 1))
                        sc.activation(y_sb[gy % 2][:], P[2 + (gy % 2)][0:8, :],
                                      AF.Identity).then_inc(s_gye.inc(), 1)

            @blk.vector
            def _(v):
                for g in range(33):
                    gq = tc * 32 + g
                    if g < 32:
                        for hh in range(2):
                            q2 = 2 * gq + hh
                            cl = slice(hh * 512, (hh + 1) * 512)
                            v.wait_ge(s_gup.h, b[s_gup] + q2 + 1)
                            if gq >= 2:
                                v.wait_ge(s_sc1.h, b[s_sc1] + gq - 1)
                            v.tensor_tensor(dBu[gq % 2][:, cl], P[1][:, cl], Bp[:, cl],
                                            ALU.mult).then_inc(s_gbu.inc(), 1)
                    if g >= 1:
                        gp_ = gq - 1
                        v.wait_ge(s_sc1.h, b[s_sc1] + gp_ + 1)
                        if gp_ >= 3:
                            v.wait_ge(s_gym.h, b[s_gym] + gp_ - 2)
                        v.tensor_tensor(tmp[gp_ % 3][:, 0:256], h_t[gp_ % 3][:, 0:256],
                                        Cp[:, 0:256], ALU.mult).then_inc(s_gt0.inc(), 1)
                    if g < 32:
                        v.wait_ge(s_gda.h, b[s_gda] + 2 * gq + 2)
                        if gq >= 3:
                            v.wait_ge(s_gt1.h, b[s_gt1] + gq - 2)
                        if tc not in (0, 4):
                            v.wait_ge(s_gsc.h, b[s_gsc] + gq - 31)
                        ini = 0.0 if tc in (0, 4) else carry[:, g:g + 1]
                        v.tensor_tensor_scan(h_t[gq % 3][:], dA[gq % 2][:],
                                             dBu[gq % 2][:], ini, ALU.mult, ALU.add) \
                            .then_inc(s_sc1.inc(), 1)

            @blk.gpsimd
            def _(gp):
                for g in range(32):
                    gq = tc * 32 + g
                    gp.wait_ge(s_sc1.h, b[s_sc1] + gq + 1)
                    gp.tensor_copy(carry[:, g:g + 1], h_t[gq % 3][:, 1023:1024]) \
                        .then_inc(s_gsc.inc(), 1)
                    if gq >= 3:
                        gp.wait_ge(s_gym.h, b[s_gym] + gq - 2)
                    gp.tensor_tensor(tmp[gq % 3][:, 256:1024], h_t[gq % 3][:, 256:1024],
                                     Cp[:, 256:1024], ALU.mult).then_inc(s_gt1.inc(), 1)
                gp.wait_ge(s_gyd.h, b[s_gyd] + (tc + 1) * 512)
                gp.collective_compute("AllGather", ALU.bypass,
                                      replica_groups=[list(range(NCORE))],
                                      ins=[y_mine.ap()[tc * DSL:(tc + 1) * DSL, :]],
                                      outs=[y_full.ap()[tc * NCORE * DSL:
                                                        (tc + 1) * NCORE * DSL, :]]) \
                    .then_inc(s_cc.inc(), 1)

            @blk.sync
            def _(sy):
                for g in range(32):
                    gq = tc * 32 + g
                    sy.wait_ge(s_gye.h, b[s_gye] + gq + 1)
                    sy.dma_start(out=y_mine.ap()[tc * DSL + 8 * g:tc * DSL + 8 * g + 8, :],
                                 in_=y_sb[gq % 2][:]).then_inc(s_gyd.inc(16), 16)

    with nc.Block() as blk:
        @blk.gpsimd
        def _(g):
            g.wait_ge(s_cc.h, b[s_cc] + 10)

    es2.close()
    es3 = contextlib.ExitStack()
    yloc = [sb_scoped(es3, nc, f"{pfx}yl{c}", [128, 1024], F32) for c in range(NCH)]
    xnl = [sb_scoped(es3, nc, f"{pfx}xnl{c}", [128, 1024], F32) for c in range(NCH)]

    for arm in nc.switch_core_id(engines=[nc.sync], n=128):
        try:
            rank = int(arm.logical) % NCORE
        except Exception:
            rank = 0
        for c in range(NCH):
            nc.sync.dma_start(out=yloc[c][:],
                              in_=y_full.ap()[rank * NCORE * DSL + c * 128:
                                              rank * NCORE * DSL + (c + 1) * 128, :]) \
                .then_inc(s_yl.h, 16)
            nc.sync.dma_start(out=xnl[c][:],
                              in_=agx_out.ap()[rank * D + c * 128:
                                               rank * D + (c + 1) * 128, :]) \
                .then_inc(s_yl.h, 16)
    s_yl.n += 32 * NCH

    yf = sb_scoped(es3, nc, pfx + "yf", [128, 1024], F32)
    with nc.Block() as blk:
        @blk.vector
        def _(v):
            v.wait_ge(s_yl.h, s_yl.n)
            for c in range(NCH):
                v.scalar_tensor_tensor(yf[:], xnl[c][:], D_sb[:, c:c + 1],
                                       yloc[c][:], ALU.mult, ALU.add)
                v.tensor_tensor(X[c][:], X[c][:].bitcast(F32), yf[:], ALU.add) \
                    .then_inc(s_fin.inc(), 1)

    es3.close()
    es0.close()
    return [(s_fin, s_fin.n)]



def emit_l1ffn(C, X, dd, pfx):
    nc = C.nc
    P = C.P
    es = contextlib.ExitStack()
    sbs = lambda n, shp, dt: sb_scoped(es, nc, pfx + n, shp, dt)

    XNb = [sbs(f"xnb{c}", [128, 1024], BF16) for c in range(NCH)]
    gt_sb = sbs("gt", [128, 16 * 8], BF16)
    gb_sb = sbs("gb", [1, 8], F32R)
    oh8_sb = sbs("oh8s", [8, 8 * 128], F32R)
    sc_sb = sbs("scs", [8, 1024], F32)
    sct = sbs("sct", [128, 64], F32)
    ab = sbs("ab", [128, 64], F32)
    lb = sbs("lb", [128, 64], F32)
    ssq = sbs("ssq", [128, 64], F32)
    bt = sbs("bt", [128, 64], F32)
    m1 = sbs("m1", [128, 64], F32)
    m2 = sbs("m2", [128, 64], F32)
    sup = sbs("sup", [128, 64], F32)
    mx8 = sbs("mx8", [128, 8], F32)
    cw_t = sbs("cwt", [128, 64], F32)
    cwe_sb = sbs("cwe", [8, 1024], F32R)
    cwb_sb = sbs("cwb", [128, 1024], F32)
    th = [sbs(f"th{i}", [128, 1024], F32) for i in range(2)]
    hbt = sbs("hbt", [128, 1024], F32)
    hb = [sbs(f"hb{i}", [128, 1024], BF16) for i in range(NI)]
    w13 = [sbs(f"w13_{i}", [128, 16 * 128], BF16) for i in range(4)]
    w2b = [sbs(f"w2_{i}", [128, NI * 128], BF16) for i in range(2)]

    s_ld = C.sem("g_ld")
    s_wl = C.sem("g_wl")
    s_wf13 = C.sem("g_wf13")
    s_sl = C.sem("g_sl")
    s_tt1 = C.sem("g_tt1")
    s_hb = C.sem("g_hb")
    s_o = C.sem("g_o")
    s_ox = C.sem("g_ox")
    s_cwm = C.sem("g_cwm")
    s_cwb = C.sem("g_cwb")
    bS = {s: s.n for s in (s_wl, s_wf13, s_sl, s_tt1, s_hb, s_o, s_ox, s_cwm, s_cwb)}

    with nc.Block() as blk:
        @blk.sync
        def _(sy):
            dmai(sy, gb_sb[:], dd["gate_b"].bitcast(F32R), s_ld)
            dmai(sy, oh8_sb[:], dd["oh8"].bitcast(F32R), s_ld)
            sy.wait_ge(s_ld.h, s_ld.n)

        @blk.gpsimd
        def _(g):
            g.dma_start(out=gt_sb[:].rearrange("p (c m) -> p c m", c=16),
                        in_=dd["gateT"].rearrange("(c p) m -> p c m", p=128)) \
                .then_inc(s_ld.inc(16), 16)
            g.wait_ge(s_ld.h, s_ld.n)

    emit_ln(C, X, XNb, pfx + "ln")

    with nc.Block() as blk:
        @blk.tensor
        def _(tn):
            for c in range(NCH):
                for hh in range(2):
                    cl = slice(hh * 512, (hh + 1) * 512)
                    tn.matmul(P[0][0:8, cl], gt_sb[:, c * 8:(c + 1) * 8], XNb[c][:, cl],
                              start=(c == 0), stop=(c == NCH - 1))

    with nc.Block() as blk:
        @blk.vector
        def _(v):
            v.tensor_copy(sc_sb[:], P[0][0:8, :])
            v.memset(C.jC[:], 0.0)

    with nc.Block() as blk:
        @blk.tensor
        def _(tn):
            for t8 in range(8):
                tn.transpose(P[1][:, t8 * 8:(t8 + 1) * 8],
                             sc_sb[0:8, t8 * 128:(t8 + 1) * 128],
                             C.consts["ident"][0:8, 0:8])
            tn.matmul(P[2][:, 0:8], C.consts["ones_row"][:], gb_sb[:], start=True, stop=True)

    with nc.Block() as blk:
        @blk.vector
        def _(v):
            v.tensor_copy(sct[:], P[1][:, 0:64])
            v.scalar_tensor_tensor(ab[:], sct[:], -1.0, sct[:], ALU.mult, ALU.max)
            v.memset(C.jC[:], 0.0)

    with nc.Block() as blk:
        @blk.scalar
        def _(sc):
            sc.activation(lb[:], ab[:], AF.Exp, scale=-1.0)
            sc.memzero(C.jC[:])

    with nc.Block() as blk:
        @blk.scalar
        def _(sc):
            sc.activation(ab[:], lb[:], AF.Ln, bias=1.0)
            sc.memzero(C.jC[:])

    with nc.Block() as blk:
        @blk.vector
        def _(v):
            v.scalar_tensor_tensor(lb[:], sct[:], 0.0, ab[:], ALU.max, ALU.add)
            v.memset(C.jC[:], 0.0)

    with nc.Block() as blk:
        @blk.scalar
        def _(sc):
            sc.activation(ssq[:], lb[:], AF.Sqrt)
            sc.memzero(C.jC[:])

    S8 = [slice(t8 * 8, (t8 + 1) * 8) for t8 in range(8)]
    with nc.Block() as blk:
        @blk.vector
        def _(v):
            s_hz = C.sem("hz")
            for t8 in range(8):
                v.tensor_tensor(bt[:, S8[t8]], ssq[:, S8[t8]], P[2][:, 0:8], ALU.add)
            for t8 in range(8):
                v.reduce_max(mx8[:, t8:t8 + 1], bt[:, S8[t8]], axis=AX) \
                    .then_inc(s_hz.inc(), 1)
            v.wait_ge(s_hz.h, s_hz.n)
            for t8 in range(8):
                v.tensor_scalar(m1[:, S8[t8]], bt[:, S8[t8]], mx8[:, t8:t8 + 1],
                                None, ALU.is_ge)
            for t8 in range(8):
                v.scalar_tensor_tensor(sup[:, S8[t8]], m1[:, S8[t8]], -1e30,
                                       bt[:, S8[t8]], ALU.mult, ALU.add)
            for t8 in range(8):
                v.reduce_max(mx8[:, t8:t8 + 1], sup[:, S8[t8]], axis=AX) \
                    .then_inc(s_hz.inc(), 1)
            v.wait_ge(s_hz.h, s_hz.n)
            for t8 in range(8):
                v.tensor_scalar(m2[:, S8[t8]], sup[:, S8[t8]], mx8[:, t8:t8 + 1],
                                None, ALU.is_ge)
            for t8 in range(8):
                v.tensor_tensor(m1[:, S8[t8]], m1[:, S8[t8]], m2[:, S8[t8]], ALU.add)
            for t8 in range(8):
                v.scalar_tensor_tensor(cw_t[:, S8[t8]], m1[:, S8[t8]], 0.5,
                                       ssq[:, S8[t8]], ALU.mult, ALU.mult)
            v.memset(C.jC[:], 0.0)

    with nc.Block() as blk:
        @blk.tensor
        def _(tn):
            for t8 in range(8):
                tn.transpose(P[1][0:8, t8 * 128:(t8 + 1) * 128],
                             cw_t[:, t8 * 8:(t8 + 1) * 8],
                             C.consts["ident"][:, 0:128])

    with nc.Block() as blk:
        @blk.vector
        def _(v):
            v.tensor_copy(cwe_sb[:], P[1][0:8, :])
            v.memset(C.jC[:], 0.0)

    elist = list(range(NEXP)) if _EL1 is None else list(_EL1)
    nrouted_before = [sum(1 for x in elist[:k] if x < 8) for k in range(len(elist) + 1)]
    with nc.Block() as blk:
        @blk.sync
        def _(sy):
            for ei, e in enumerate(elist):
                for i in range(NI):
                    for wd, nm in ((dd["w1T"], 0), (dd["w3T"], 1)):
                        j13 = ei * 22 + 2 * i + nm
                        if j13 >= 4:
                            sy.wait_ge(s_wf13.h, bS[s_wf13] + j13 - 3)
                        dmai(sy, w13[j13 % 4][:].rearrange("p (c m) -> p c m", c=16),
                             wd[e, :, i * 128:(i + 1) * 128]
                             .rearrange("(c p) m -> p c m", p=128), s_wl)
                for ddx in range(NCH):
                    j2 = ei * 16 + ddx
                    if j2 >= 2:
                        sy.wait_ge(s_o.h, bS[s_o] + j2 - 1)
                    dmai(sy, w2b[j2 % 2][:].rearrange("p (c m) -> p c m", c=NI),
                         dd["w2T"][e, :, ddx * 128:(ddx + 1) * 128]
                         .rearrange("(c p) m -> p c m", p=128), s_wl)

        @blk.tensor
        def _(tn):
            jg = 0
            for ei, e in enumerate(elist):
                routed = e < 8
                if routed:
                    if ei > 0:
                        tn.wait_ge(s_hb.h, bS[s_hb] + 11 * ei)
                        tn.wait_ge(s_ox.h, bS[s_ox] + 16 * ei)
                    for hh in range(2):
                        cl = slice(hh * 512, (hh + 1) * 512)
                        ins = tn.matmul(P[2][:, cl], oh8_sb[0:8, e * 128:(e + 1) * 128],
                                        cwe_sb[0:8, cl], start=True, stop=True)
                    ins.then_inc(s_cwm.inc(), 1)
                elif ei > 0:
                    tn.wait_ge(s_hb.h, bS[s_hb] + 11 * ei)
                    tn.wait_ge(s_ox.h, bS[s_ox] + 16 * ei)
                for i in range(NI):
                    for nm in range(2):
                        tn.wait_ge(s_wl.h, bS[s_wl] + 16 * (jg + 1))
                        slot = w13[(ei * 22 + 2 * i + nm) % 4]
                        dst = P[2 * (i % 2) + nm]
                        if i >= 2 and nm == 0:
                            tn.wait_ge(s_sl.h, bS[s_sl] + 11 * ei + i - 1)
                            tn.wait_ge(s_tt1.h, bS[s_tt1] + 11 * ei + i - 1)
                        if i == 1 and nm == 0 and routed:
                            tn.wait_ge(s_cwb.h, bS[s_cwb] + nrouted_before[ei] + 1)
                        for k in range(NCH):
                            for hh in range(2):
                                cl = slice(hh * 512, (hh + 1) * 512)
                                ins = tn.matmul(dst[:, cl], slot[:, k * 128:(k + 1) * 128],
                                                XNb[k][:, cl], start=(k == 0), stop=(k == NCH - 1))
                        ins.then_inc(s_wf13.inc(), 1)
                        jg += 1
                tn.wait_ge(s_hb.h, bS[s_hb] + 11 * (ei + 1))
                for ddx in range(NCH):
                    tn.wait_ge(s_wl.h, bS[s_wl] + 16 * (jg + 1))
                    if ddx >= 2:
                        tn.wait_ge(s_ox.h, bS[s_ox] + 16 * ei + ddx - 1)
                    slot = w2b[(ei * 16 + ddx) % 2]
                    dst = P[ddx % 2]
                    for k in range(NI):
                        for hh in range(2):
                            cl = slice(hh * 512, (hh + 1) * 512)
                            ins = tn.matmul(dst[:, cl], slot[:, k * 128:(k + 1) * 128],
                                            hb[k][:, cl], start=(k == 0), stop=(k == NI - 1))
                    ins.then_inc(s_o.inc(), 1)
                    jg += 1

        @blk.scalar
        def _(sc):
            for ei, e in enumerate(elist):
                for i in range(NI):
                    gi = 11 * ei + i
                    sc.wait_ge(s_wf13.h, bS[s_wf13] + 22 * ei + 2 * i + 1)
                    if gi >= 2:
                        sc.wait_ge(s_tt1.h, bS[s_tt1] + gi - 1)
                    sc.activation(th[gi % 2][:], P[2 * (i % 2)][:], AF.Silu) \
                        .then_inc(s_sl.inc(), 1)

        @blk.vector
        def _(v):
            for ei, e in enumerate(elist):
                routed = e < 8
                if routed:
                    v.wait_ge(s_cwm.h, bS[s_cwm] + nrouted_before[ei] + 1)
                    v.tensor_copy(cwb_sb[:], P[2][:]).then_inc(s_cwb.inc(), 1)
                for i in range(NI):
                    gi = 11 * ei + i
                    v.wait_ge(s_sl.h, bS[s_sl] + gi + 1)
                    v.wait_ge(s_wf13.h, bS[s_wf13] + 22 * ei + 2 * i + 2)
                    v.tensor_tensor(hbt[:], th[gi % 2][:], P[2 * (i % 2) + 1][:],
                                    ALU.mult).then_inc(s_tt1.inc(), 1)
                    if routed:
                        v.tensor_tensor(hb[i][:], hbt[:], cwb_sb[:], ALU.mult) \
                            .then_inc(s_hb.inc(), 1)
                    else:
                        v.tensor_scalar(hb[i][:], hbt[:], 0.5, None, ALU.mult) \
                            .then_inc(s_hb.inc(), 1)
                for ddx in range(NCH):
                    v.wait_ge(s_o.h, bS[s_o] + 16 * ei + ddx + 1)
                    v.tensor_tensor(X[ddx][:], X[ddx][:].bitcast(F32), P[ddx % 2][:],
                                    ALU.add).then_inc(s_ox.inc(), 1)

    es.close()
    return [(s_ox, s_ox.n)]



def emit_l1attn(C, X, dd, pfx):
    nc = C.nc
    P = C.P
    es = contextlib.ExitStack()
    sbs = lambda n, shp, dt: sb_scoped(es, nc, pfx + n, shp, dt)

    ar_in = nc.dram_tensor(pfx + "ar_in", [128, 1040], F32)
    ar_out = nc.dram_tensor(pfx + "ar_out", [128, 1040], F32)

    qe = [sbs(f"qe{j}", [128, 1024], BF16) for j in range(8)]
    ke = [sbs(f"ke{j}", [128, 1024], BF16) for j in range(8)]
    kv_sb = sbs("kv", [128, 1024], F32R)
    ks_bd = sbs("ksbd", [128, 16], F32R)
    kst = sbs("kst", [128, 8], F32)
    rcp_t = sbs("rcpt", [1, 1024], F32)
    rcp = sbs("rcp", [1, 1024], F32R)
    wmq_sb = sbs("wmq", [128, 64], BF16)
    wmk_sb = sbs("wmk", [128, 64], BF16)
    bq_sb = sbs("bqs", [128, 16], F32)
    bk_sb = sbs("bks", [128, 16], F32)
    bo_sb = sbs("bos", [128, 16], F32)
    bv_bf = sbs("bvbf", [1, D], BF16)
    orow_bf = sbs("orowbf", [1, 128], BF16)
    ident_bf = sbs("identbf", [128, 128], BF16)

    s_ld = C.sem("a_ld")
    s_w = C.sem("a_w")
    s_pj = C.sem("a_pj"); s_bh = C.sem("a_bh")
    s_qm = C.sem("a_qm"); s_e1 = C.sem("a_e1"); s_e2 = C.sem("a_e2")
    s_elu = C.sem("a_elu")
    s_vm = C.sem("a_vm"); s_ve = C.sem("a_ve"); s_vw = C.sem("a_vw")
    s_tp = C.sem("a_tp"); s_te = C.sem("a_te")
    s_cc = C.sem("a_cc"); s_ar = C.sem("a_ar")
    s_nm = C.sem("a_nm"); s_nr = C.sem("a_nr")
    s_am = C.sem("a_am"); s_ae = C.sem("a_ae"); s_ad = C.sem("a_ad")
    s_om = C.sem("a_om"); s_oe = C.sem("a_oe"); s_od = C.sem("a_od")
    bA = {s: s.n for s in (s_vm, s_ve, s_vw, s_tp, s_te, s_cc, s_nm, s_nr,
                           s_am, s_ae, s_ad, s_om, s_oe, s_od)}

    with nc.Block() as blk:
        @blk.sync
        def _(sy):
            for dst, srcd in ((bq_sb, dd["bq"]), (bk_sb, dd["bk"]), (bo_sb, dd["bo"])):
                dmai(sy, dst[:], srcd, s_ld)
            sy.wait_ge(s_ld.h, s_ld.n)

        @blk.gpsimd
        def _(g):
            g.dma_start(out=wmq_sb[:], in_=dd["wmqT"]).then_inc(s_ld.inc(16), 16)
            g.dma_start(out=wmk_sb[:], in_=dd["wmkT"]).then_inc(s_ld.inc(16), 16)
            g.dma_start(out=bv_bf[:], in_=dd["bvR"]).then_inc(s_ld.inc(16), 16)
            g.wait_ge(s_ld.h, s_ld.n)

    with nc.Block() as blk:
        @blk.gpsimd
        def _(g):
            g.dma_start(out=orow_bf[:], in_=C.consts["ones_row"][:].bitcast(F32)) \
                .then_inc(s_ld.inc(16), 16)
            g.dma_start(out=ident_bf[:], in_=C.consts["ident"][:]).then_inc(s_ld.inc(16), 16)
            g.wait_ge(s_ld.h, s_ld.n)

        @blk.vector
        def _(v):
            v.memset(C.jC[:], 0.0)
            v.memset(C.jC[:], 0.0)
            v.tensor_scalar(ks_bd[:], C.jC[:, 0:16], 0.0, None, ALU.mult)

    es_xn = contextlib.ExitStack()
    XNb = [sb_scoped(es_xn, nc, f"{pfx}xnb{c}", [128, 1024], BF16) for c in range(NCH)]

    emit_ln(C, X, XNb, pfx + "ln")

    es_qk = contextlib.ExitStack()
    wr = [sb_scoped(es_qk, nc, f"{pfx}wr{i}", [128, 16 * 128], BF16) for i in range(4)]
    qh = sb_scoped(es_qk, nc, pfx + "qh", [128, 1024], BF16)
    el1 = sb_scoped(es_qk, nc, pfx + "el1", [128, 1024], F32)
    el2 = sb_scoped(es_qk, nc, pfx + "el2", [128, 1024], F32)

    for which, d_w, wmap, bias_sb, out_tiles in (
            ("q", dd["wqT"], wmq_sb, bq_sb, qe), ("k", dd["wkT"], wmk_sb, bk_sb, ke)):
        b_w = s_w.n
        b_pj, b_bh, b_qm = s_pj.n, s_bh.n, s_qm.n
        b_e1, b_e2, b_el = s_e1.n, s_e2.n, s_elu.n

        with nc.Block() as blk:
            @blk.gpsimd
            def _(g):
                for f in range(NCH):
                    if f >= 4:
                        g.wait_ge(s_pj.h, b_pj + f - 3)
                    g.dma_start(out=wr[f % 4][:].rearrange("p (c m) -> p c m", c=16),
                                in_=d_w[:, f * 128:(f + 1) * 128]
                                .rearrange("(c p) m -> p c m", p=128)) \
                        .then_inc(s_w.inc(16), 16)

            @blk.tensor
            def _(tn):
                for f in range(NCH):
                    j = f // 2
                    tn.wait_ge(s_w.h, b_w + 16 * (f + 1))
                    if f >= 2:
                        tn.wait_ge(s_bh.h, b_bh + f - 1)
                    for k in range(NCH):
                        for hh in range(2):
                            cl = slice(hh * 512, (hh + 1) * 512)
                            ins = tn.matmul(P[f % 2][:, cl],
                                            wr[f % 4][:, k * 128:(k + 1) * 128],
                                            XNb[k][:, cl], start=(k == 0), stop=(k == NCH - 1))
                    ins.then_inc(s_pj.inc(), 1)
                    tn.wait_ge(s_bh.h, b_bh + f + 1)
                    if f >= 4:
                        tn.wait_ge(s_elu.h, b_el + j - 1)
                    rows = slice(64 * (f % 2), 64 * (f % 2) + 64)
                    for hh in range(2):
                        cl = slice(hh * 512, (hh + 1) * 512)
                        ins = tn.matmul(P[2 + (j % 2)][rows, cl], wmap[:], qh[:, cl],
                                        start=True, stop=True)
                    ins.then_inc(s_qm.inc(), 1)

            @blk.scalar
            def _(sc):
                for f in range(NCH):
                    j = f // 2
                    sc.wait_ge(s_pj.h, b_pj + f + 1)
                    if f >= 1:
                        sc.wait_ge(s_qm.h, b_qm + f)
                    sc.activation(qh[:], P[f % 2][:], AF.Identity,
                                  bias=bias_sb[:, f:f + 1]).then_inc(s_bh.inc(), 1)
                    if f % 2 == 1:
                        sc.wait_ge(s_e1.h, b_e1 + j + 1)
                        if j >= 1:
                            sc.wait_ge(s_elu.h, b_el + j)
                        sc.activation(el2[:], el1[:], AF.Exp).then_inc(s_e2.inc(), 1)

            @blk.vector
            def _(v):
                s_hz = C.sem("hz")
                for j in range(8):
                    v.wait_ge(s_qm.h, b_qm + 2 * j + 2)
                    if j >= 1:
                        v.wait_ge(s_e2.h, b_e2 + j)
                    v.tensor_scalar(el1[:], P[2 + (j % 2)][:], 0.0, None, ALU.min) \
                        .then_inc(s_e1.inc(), 1)
                    v.wait_ge(s_e2.h, b_e2 + j + 1)
                    v.scalar_tensor_tensor(out_tiles[j][:], P[2 + (j % 2)][:], 0.0,
                                           el2[:], ALU.max, ALU.add) \
                        .then_inc(s_elu.inc(), 1)
                    if which == "k":
                        v.reduce_sum(kst[:, j:j + 1], out_tiles[j][:], axis=AX) \
                            .then_inc(s_hz.inc(), 1)
                v.memset(C.jC[:], 0.0)

    es_qk.close()

    with nc.Block() as blk:
        @blk.vector
        def _(v):
            v.wait_ge(C.sem("hz").h, C.sem("hz").n)
            for j in range(8):
                v.tensor_copy(ks_bd[0:64, 2 * j:2 * j + 1], kst[0:64, j:j + 1])
                v.tensor_copy(ks_bd[64:128, 2 * j + 1:2 * j + 2], kst[64:128, j:j + 1])
            v.memset(C.jC[:], 0.0)

    es_v = contextlib.ExitStack()
    vwb = [sb_scoped(es_v, nc, f"{pfx}vw{i}", [128, 16 * 512], BF16)
           for i in range(2)]
    vT = [sb_scoped(es_v, nc, f"{pfx}vT{t}", [128, D], BF16) for t in range(4)]
    emkT = [sb_scoped(es_v, nc, f"{pfx}emkT{t}", [128, 1024], BF16) for t in range(2)]

    for sblk in range(2):
        with nc.Block() as blk:
            @blk.gpsimd
            def _(g):
                for nq in range(4):
                    bb = sblk * 4 + nq
                    if bb >= 2:
                        g.wait_ge(s_vm.h, bA[s_vm] + 4 * (bb - 1))
                    for k in range(NCH):
                        g.dma_start(out=vwb[bb % 2][:, k * 512:(k + 1) * 512],
                                    in_=dd["wvT"][k * 128:(k + 1) * 128,
                                                  nq * 512:(nq + 1) * 512]) \
                            .then_inc(s_vw.inc(16), 16)

            @blk.tensor
            def _(tn):
                for nq in range(4):
                    bb = sblk * 4 + nq
                    tn.wait_ge(s_vw.h, bA[s_vw] + 16 * 16 * (bb + 1))
                    for tb in range(4):
                        tbg = sblk * 4 + tb
                        it = bb * 4 + tb
                        slot = it % 4
                        dst = P[slot // 2][:, (slot % 2) * 512:(slot % 2) * 512 + 512]
                        if it >= 4:
                            tn.wait_ge(s_ve.h, bA[s_ve] + it - 3)
                        for k in range(NCH):
                            tn.matmul(dst, XNb[k][:, tbg * 128:(tbg + 1) * 128],
                                      vwb[bb % 2][:, k * 512:(k + 1) * 512],
                                      start=(k == 0), stop=False)
                        ins = tn.matmul(dst, orow_bf[:],
                                        bv_bf[:, nq * 512:(nq + 1) * 512],
                                        start=False, stop=True)
                        ins.then_inc(s_vm.inc(), 1)

            @blk.scalar
            def _(sc):
                for nq in range(4):
                    for tb in range(4):
                        it = (sblk * 4 + nq) * 4 + tb
                        slot = it % 4
                        srcp = P[slot // 2][:, (slot % 2) * 512:(slot % 2) * 512 + 512]
                        sc.wait_ge(s_vm.h, bA[s_vm] + it + 1)
                        sc.activation(vT[tb][:, nq * 512:(nq + 1) * 512], srcp,
                                      AF.Identity).then_inc(s_ve.inc(), 1)

        with nc.Block() as blk:
            @blk.tensor
            def _(tn):
                for tb in range(4):
                    tbg = sblk * 4 + tb
                    for h in range(16):
                        it = (sblk * 4 + tb) * 16 + h
                        if it >= 2:
                            tn.wait_ge(s_te.h, bA[s_te] + it - 1)
                        j = h // 2
                        rows = slice(64 * (h % 2), 64 * (h % 2) + 64)
                        ib = 64 * (h % 2)
                        sl_c = (it % 2) * 1024
                        tn.transpose(P[2][:].bitcast(BF16)[:, sl_c:sl_c + 64],
                                     ke[j][rows, tbg * 128:(tbg + 1) * 128],
                                     ident_bf[ib:ib + 64, ib:ib + 64]).then_inc(s_tp.inc(), 1)
                    for h in range(16):
                        rows = slice(64 * (h % 2), 64 * (h % 2) + 64)
                        tn.wait_ge(s_te.h, bA[s_te] + (sblk * 4 + tb) * 16 + h + 1)
                        tn.matmul(P[3][rows, (h // 2) * 128:(h // 2) * 128 + 128],
                                  emkT[tb % 2][:, h * 64:(h + 1) * 64],
                                  vT[tb][:, h * 128:(h + 1) * 128],
                                  start=(tbg == 0), stop=(tbg == 7))

            @blk.vector
            def _(v):
                for tb in range(4):
                    for h in range(16):
                        it = (sblk * 4 + tb) * 16 + h
                        v.wait_ge(s_tp.h, bA[s_tp] + it + 1)
                        v.tensor_copy(emkT[tb % 2][:, h * 64:(h + 1) * 64],
                                      P[2][:].bitcast(BF16)[:, (it % 2) * 1024:(it % 2) * 1024 + 64]) \
                            .then_inc(s_te.inc(), 1)

    with nc.Block() as blk:
        @blk.vector
        def _(v):
            v.tensor_copy(kv_sb[:], P[3][:])
            v.memset(C.jC[:], 0.0)

    es_v.close()
    es_xn.close()

    with nc.Block() as blk:
        @blk.sync
        def _(sy):
            dmai(sy, ar_in.ap()[:, 0:1024], kv_sb[:].bitcast(F32), s_ar)
            dmai(sy, ar_in.ap()[:, 1024:1040], ks_bd[:].bitcast(F32), s_ar)

        @blk.gpsimd
        def _(g):
            g.wait_ge(s_ar.h, s_ar.n)
            g.collective_compute(
                "AllReduce", ALU.add,
                replica_groups=[[0, 1, 2, 3], [4, 5, 6, 7]],
                ins=[ar_in.ap()], outs=[ar_out.ap()],
            ).then_inc(s_cc.inc(), 1)
            g.wait_ge(s_cc.h, bA[s_cc] + 1)

    es_att = contextlib.ExitStack()
    att = [sb_scoped(es_att, nc, f"{pfx}att{h}", [128, 1024], BF16) for h in range(NCH)]
    at_sb = sb_scoped(es_att, nc, pfx + "at", [128, 1024], F32)
    kv_bf = sb_scoped(es_att, nc, pfx + "kvbf", [128, 1024], BF16)
    ksb_pad = sb_scoped(es_att, nc, pfx + "ksbp", [128, 32], BF16)

    with nc.Block() as blk:
        @blk.gpsimd
        def _(g):
            g.dma_start(out=kv_bf[:], in_=ar_out.ap()[:, 0:1024]) \
                .then_inc(s_ar.inc(16), 16)
            g.wait_ge(s_ar.h, s_ar.n)

        @blk.sync
        def _(sy):
            dmai(sy, ks_bd[:], ar_out.ap()[:, 1024:1040].bitcast(F32R), s_ar)
            sy.wait_ge(s_ar.h, s_ar.n)

    with nc.Block() as blk:
        @blk.vector
        def _(v):
            v.tensor_copy(ksb_pad[:].rearrange("p (h two) -> p h two", two=2)
                          [:, :, 0:1], ks_bd[:].bitcast(F32)
                          .rearrange("p (h o) -> p h o", o=1))
            v.memset(C.jC[:], 0.0)

    with nc.Block() as blk:
        @blk.tensor
        def _(tn):
            for h in range(16):
                if h >= 2:
                    tn.wait_ge(s_ad.h, bA[s_ad] + h - 1)
                for hh in range(2):
                    cl = slice(hh * 512, (hh + 1) * 512)
                    ins = tn.matmul(P[2 + (h % 2)][0:1, cl],
                                    ksb_pad[:, 2 * h:2 * h + 1],
                                    qe[h // 2][:, cl], start=True, stop=True)
                ins.then_inc(s_nm.inc(), 1)
                if h >= 2:
                    tn.wait_ge(s_ae.h, bA[s_ae] + h - 1)
                rows = slice(64 * (h % 2), 64 * (h % 2) + 64)
                for hh in range(2):
                    cl = slice(hh * 512, (hh + 1) * 512)
                    tn.matmul(P[h % 2][:, cl],
                              kv_bf[rows, (h // 2) * 128:(h // 2) * 128 + 128],
                              qe[h // 2][rows, cl], start=True, stop=True)
                tn.wait_ge(s_nr.h, bA[s_nr] + h + 1)
                for hh in range(2):
                    cl = slice(hh * 512, (hh + 1) * 512)
                    ins = tn.matmul(P[2 + (h % 2)][:, cl], C.consts["ones_row"][:],
                                    rcp[0:1, cl], start=True, stop=True)
                ins.then_inc(s_am.inc(), 1)

        @blk.vector
        def _(v):
            s_hz = C.sem("hz")
            for h in range(16):
                v.wait_ge(s_nm.h, bA[s_nm] + h + 1)
                v.tensor_scalar(rcp_t[:], P[2 + (h % 2)][0:1, :], 1e-6, None, ALU.add) \
                    .then_inc(s_hz.inc(), 1)
                v.wait_ge(s_hz.h, s_hz.n)
                v.reciprocal(rcp[:], rcp_t[:]).then_inc(s_nr.inc(), 1)
                v.wait_ge(s_ae.h, bA[s_ae] + h + 1)
                v.tensor_tensor(att[h][:], at_sb[:], P[2 + (h % 2)][:], ALU.mult) \
                    .then_inc(s_ad.inc(), 1)

        @blk.scalar
        def _(sc):
            for h in range(16):
                sc.wait_ge(s_am.h, bA[s_am] + h + 1)
                if h >= 1:
                    sc.wait_ge(s_ad.h, bA[s_ad] + h)
                sc.activation(at_sb[:], P[h % 2][:], AF.Identity).then_inc(s_ae.inc(), 1)

    wr2 = [sb_scoped(es_att, nc, f"{pfx}wr2_{i}", [128, 16 * 128], BF16) for i in range(4)]
    ot = sb_scoped(es_att, nc, pfx + "ot", [128, 1024], F32)
    with nc.Block() as blk:
        @blk.gpsimd
        def _(g):
            for ddx in range(NCH):
                if ddx >= 4:
                    g.wait_ge(s_om.h, bA[s_om] + ddx - 3)
                g.dma_start(out=wr2[ddx % 4][:].rearrange("p (c m) -> p c m", c=16),
                            in_=dd["woT"][:, ddx * 128:(ddx + 1) * 128]
                            .rearrange("(c p) m -> p c m", p=128)) \
                    .then_inc(s_w.inc(16), 16)

        @blk.tensor
        def _(tn):
            b_w = s_w.n - 16 * NCH
            for ddx in range(NCH):
                tn.wait_ge(s_w.h, b_w + 16 * (ddx + 1))
                if ddx >= 2:
                    tn.wait_ge(s_oe.h, bA[s_oe] + ddx - 1)
                for k in range(NCH):
                    for hh in range(2):
                        cl = slice(hh * 512, (hh + 1) * 512)
                        ins = tn.matmul(P[ddx % 2][:, cl], wr2[ddx % 4][:, k * 128:(k + 1) * 128],
                                        att[k][:, cl], start=(k == 0), stop=(k == NCH - 1))
                ins.then_inc(s_om.inc(), 1)

        @blk.scalar
        def _(sc):
            for ddx in range(NCH):
                sc.wait_ge(s_om.h, bA[s_om] + ddx + 1)
                if ddx >= 1:
                    sc.wait_ge(s_od.h, bA[s_od] + ddx)
                sc.activation(ot[:], P[ddx % 2][:], AF.Identity,
                              bias=bo_sb[:, ddx:ddx + 1]).then_inc(s_oe.inc(), 1)

        @blk.vector
        def _(v):
            for ddx in range(NCH):
                v.wait_ge(s_oe.h, bA[s_oe] + ddx + 1)
                v.tensor_tensor(X[ddx][:], X[ddx][:].bitcast(F32), ot[:], ALU.add) \
                    .then_inc(s_od.inc(), 1)

    es_att.close()
    es.close()
    return [(s_od, s_od.n)]



_EMIT = {"l0ffn": emit_l0ffn, "l0attn": emit_l0attn,
         "l1ffn": emit_l1ffn, "l1attn": emit_l1attn}
_SEQ = ["l0ffn", "l0attn", "l0ffn", "l1ffn", "l1attn", "l1ffn"]


def _declare(nc, keys):
    dd = {}
    for k in keys:
        shp, dt = _DRAM_SPECS[k]
        dd[k] = nc.dram_tensor(k, shp, dt, kind="ExternalInput").ap()
    return dd


def build_single(which):
    nc = bass.Bass("TRN2", target_bir_lowering=False)
    C = Ctx(nc)
    keys = ["xT"] + _NEED[which] + _CONST_KEYS
    dd = _declare(nc, keys)
    d_y = nc.dram_tensor("yT", [D, TOK], F32, kind="ExternalOutput").ap()
    X = [nc.alloc_sbuf_tensor(f"x{c}", [128, 1024], F32R) for c in range(NCH)]
    s_ld = C.sem("io_ld")
    s_st = C.sem("io_st")
    with nc.Block() as blk:
        @blk.sync
        def _(sy):
            emit_load_stream(C, X, dd["xT"], sy, s_ld)
            emit_load_consts(C, dd, sy, s_ld)
            sy.wait_ge(s_ld.h, s_ld.n)
    waits = _EMIT[which](C, X, dd, which[:2] + "_")
    emit_store_stream(C, X, d_y, s_st, waits=waits)
    return nc, keys


def build_fused():
    nc = bass.Bass("TRN2", target_bir_lowering=False)
    C = Ctx(nc)
    keys = sorted({k for ks in _NEED.values() for k in ks}) + ["xT"] + _CONST_KEYS
    dd = _declare(nc, keys)
    d_y = nc.dram_tensor("yT", [D, TOK], F32, kind="ExternalOutput").ap()
    X = [nc.alloc_sbuf_tensor(f"x{c}", [128, 1024], F32R) for c in range(NCH)]
    s_ld = C.sem("io_ld")
    s_st = C.sem("io_st")
    with nc.Block() as blk:
        @blk.sync
        def _(sy):
            emit_load_stream(C, X, dd["xT"], sy, s_ld)
            emit_load_consts(C, dd, sy, s_ld)
            sy.wait_ge(s_ld.h, s_ld.n)
    waits = []
    for idx, which in enumerate(_SEQ):
        with nc.named_scope(f"s{idx}_{which}"):
            waits = _EMIT[which](C, X, dd, f"s{idx}_")
    emit_store_stream(C, X, d_y, s_st, waits=waits)
    return nc, keys



_CACHE = {}


def kernel(**inputs):
    h = _prep(inputs)
    if "fused" not in _CACHE:
        _CACHE["fused"] = build_fused()
    nc, keys = _CACHE["fused"]
    in_maps = []
    for c in range(NCORE):
        m = {}
        for k in keys:
            if k == "xT":
                m[k] = np.ascontiguousarray(h["xT"][:, c * TOK:(c + 1) * TOK])
            elif k in _PERCORE:
                m[k] = h[k][c]
            else:
                m[k] = h[k]
        in_maps.append(m)
    res = run_bass_kernel_spmd(nc, in_maps, core_ids=list(range(NCORE)))
    full = np.concatenate([res.results[c]["yT"] for c in range(NCORE)], axis=1)
    return np.ascontiguousarray(full.T).reshape(2, 4096, D).astype(np.float32)


# version 07
# speedup vs baseline: 1.0079x; 1.0079x over previous
import contextlib
import numpy as np
import ml_dtypes
import concourse.bass as bass
import concourse.mybir as mybir
from concourse.bass_utils import run_bass_kernel_spmd

F32 = mybir.dt.float32
F32R = mybir.dt.float32r
BF16 = mybir.dt.bfloat16
AF = mybir.ActivationFunctionType
ALU = mybir.AluOpType
AX = mybir.AxisListType.X

NCORE = 8
TOK = 1024
D = 2048
NCH = 16
NTOK = 8192
DSL = 256
LN_EPS = 1e-5
NEXP = 9
NI = 11
_EL1 = None
_POOL_OFF = True
_DBG_L0 = False



def _host_consts():
    c = {}
    c["ones_col"] = np.ones((128, 1), np.float32)
    c["ones_row"] = np.ones((1, 128), np.float32)
    c["ident"] = np.eye(128, dtype=np.float32)
    p = np.arange(128)
    c["m16e"] = (p[None, :] // 16 == np.arange(8)[:, None]).astype(np.float32)
    oh8 = np.zeros((8, 8 * 128), np.float32)
    for e in range(8):
        oh8[e, e * 128:(e + 1) * 128] = 1.0
    c["oh8"] = oh8
    e_b = np.zeros((32, 128), np.float32)
    e_c = np.zeros((32, 128), np.float32)
    for k in range(16):
        e_b[k] = (p % 16 == k)
        e_c[k + 16] = (p % 16 == k)
    c["e_b"], c["e_c"] = e_b, e_c
    g = np.zeros((16, 128, 128), np.float32)
    for gl in range(16):
        for pp in range(128):
            g[gl, gl * 8 + pp // 16, pp] = 1.0
    c["g_pat"] = np.ascontiguousarray(g.reshape(16 * 128, 128))
    c["r_pat"] = (p[:, None] // 16 == np.arange(8)[None, :]).astype(np.float32)
    return c


def _prep(inputs):
    h = dict(_host_consts())
    f32 = np.float32
    x = np.asarray(inputs["L0_attn_x"], f32).reshape(NTOK, D)
    h["xT"] = np.ascontiguousarray(x.T)

    dw = np.asarray(inputs["L0_ffn_down_w"], f32)
    h["dnT"] = np.ascontiguousarray(dw.transpose(2, 0, 1).reshape(D, 128))
    uw = np.asarray(inputs["L0_ffn_up_w"], f32)
    h["upT"] = np.ascontiguousarray(uw.transpose(0, 2, 1).reshape(128, D))
    h["rtT"] = np.ascontiguousarray(np.asarray(inputs["L0_ffn_router_w"], f32).T)

    h["xprojT"] = np.ascontiguousarray(np.asarray(inputs["L0_attn_x_proj_w"], f32).T)
    dtT = np.ascontiguousarray(np.asarray(inputs["L0_attn_dt_proj_w"], f32).T)
    h["dtprojT_sl"] = [np.ascontiguousarray(dtT[:, c * DSL:(c + 1) * DSL]) for c in range(NCORE)]
    dtb = np.asarray(inputs["L0_attn_dt_proj_b"], f32)
    h["dtb_sl"] = [np.ascontiguousarray(dtb[c * DSL:(c + 1) * DSL].reshape(2, 128).T)
                   for c in range(NCORE)]
    h["D_sb"] = np.ascontiguousarray(np.asarray(inputs["L0_attn_D"], f32).reshape(16, 128).T)
    al = np.asarray(inputs["L0_attn_A_log"], f32)
    alp = np.zeros((NCORE, 128, 32), f32)
    for core in range(NCORE):
        for g in range(32):
            for pp in range(128):
                alp[core, pp, g] = al[core * DSL + g * 8 + pp // 16, pp % 16]
    h["alogp"] = alp

    bf = ml_dtypes.bfloat16
    for nm in ["wq", "wk", "wv", "wo"]:
        h[nm + "T"] = np.ascontiguousarray(
            np.asarray(inputs["L1_attn_" + nm], f32).T.astype(bf))
    h["wmqT"] = np.ascontiguousarray(
        np.asarray(inputs["L1_attn_wmap_q"], f32).T.astype(bf))
    h["wmkT"] = np.ascontiguousarray(
        np.asarray(inputs["L1_attn_wmap_k"], f32).T.astype(bf))
    for nm in ["bq", "bk", "bo"]:
        h[nm] = np.ascontiguousarray(
            np.asarray(inputs["L1_attn_" + nm], f32).reshape(16, 128).T)
    h["bvR"] = np.asarray(inputs["L1_attn_bv"], f32).reshape(1, D).astype(bf)

    w1 = np.asarray(inputs["L1_ffn_w1"]); w3 = np.asarray(inputs["L1_ffn_w3"])
    w2 = np.asarray(inputs["L1_ffn_w2"])
    h["w1T"] = np.ascontiguousarray(
        np.concatenate([w1, np.asarray(inputs["L1_ffn_sw1"])[None]], 0).transpose(0, 2, 1))
    h["w3T"] = np.ascontiguousarray(
        np.concatenate([w3, np.asarray(inputs["L1_ffn_sw3"])[None]], 0).transpose(0, 2, 1))
    h["w2T"] = np.ascontiguousarray(
        np.concatenate([w2, np.asarray(inputs["L1_ffn_sw2"])[None]], 0).transpose(0, 2, 1))
    h["gateT"] = np.ascontiguousarray(np.asarray(inputs["L1_ffn_gate_w"], f32).T)
    h["gate_b"] = np.asarray(inputs["L1_ffn_gate_b"], f32).reshape(1, 8)
    return h


_DRAM_SPECS = {
    "xT": ([D, TOK], F32), "dnT": ([D, 128], F32), "upT": ([128, D], F32),
    "rtT": ([D, 8], F32), "m16e": ([8, 128], F32),
    "xprojT": ([D, 160], F32), "dtprojT_sl": ([128, DSL], F32),
    "dtb_sl": ([128, 2], F32), "alogp": ([128, 32], F32), "D_sb": ([128, 16], F32),
    "g_pat": ([16 * 128, 128], F32), "e_b": ([32, 128], F32), "e_c": ([32, 128], F32),
    "r_pat": ([128, 8], F32),
    "wqT": ([D, D], BF16), "wkT": ([D, D], BF16), "wvT": ([D, D], BF16),
    "woT": ([D, D], BF16), "wmqT": ([128, 64], BF16), "wmkT": ([128, 64], BF16),
    "bq": ([128, 16], F32), "bk": ([128, 16], F32), "bo": ([128, 16], F32),
    "bvR": ([1, D], BF16),
    "w1T": ([NEXP, D, 1408], BF16), "w3T": ([NEXP, D, 1408], BF16),
    "w2T": ([NEXP, 1408, D], BF16), "gateT": ([D, 8], F32), "gate_b": ([1, 8], F32),
    "oh8": ([8, 8 * 128], F32),
    "ones_col": ([128, 1], F32), "ones_row": ([1, 128], F32), "ident": ([128, 128], F32),
}
_PERCORE = {"dtprojT_sl", "dtb_sl", "alogp"}

_NEED = {
    "l0ffn": ["dnT", "upT", "rtT", "m16e"],
    "l0attn": ["xprojT", "dtprojT_sl", "dtb_sl", "alogp", "D_sb", "g_pat",
               "e_b", "e_c", "r_pat"],
    "l1ffn": ["w1T", "w3T", "w2T", "gateT", "gate_b", "oh8"],
    "l1attn": ["wqT", "wkT", "wvT", "woT", "wmqT", "wmkT", "bq", "bk", "bo", "bvR"],
}
_CONST_KEYS = ["ones_col", "ones_row", "ident"]



class Sem:
    def __init__(self, nc, name):
        self.h = nc.semaphore(name).__enter__()
        self.n = 0

    def inc(self, k=1):
        self.n += k
        return self.h


class Ctx:
    def __init__(self, nc):
        self.nc = nc
        self._lp = nc.allow_low_precision(reason="fp32r matmul operand tiles")
        self._lp.__enter__()
        self.P = [nc.alloc_psum_tensor(f"psP{i}", [128, 1024], F32) for i in range(4)]
        self.jC = nc.alloc_sbuf_tensor("ctx_jnk", [128, 64], F32)
        self.sems = {}

    def sem(self, name):
        if name not in self.sems:
            self.sems[name] = Sem(self.nc, name)
        return self.sems[name]


def sb_scoped(es, nc, name, shape, dt):
    return es.enter_context(nc.sbuf_tensor(name, shape, dt))


def dmai(eng, out, in_, s):
    eng.dma_start(out=out, in_=in_).then_inc(s.inc(16), 16)


def emit_load_consts(C, dd, sy, s_ld):
    nc = C.nc
    C.consts = {}
    C.consts["ones_col"] = nc.alloc_sbuf_tensor("c_onescol", [128, 1], F32R)
    dmai(sy, C.consts["ones_col"][:], dd["ones_col"].bitcast(F32R), s_ld)
    C.consts["ones_row"] = nc.alloc_sbuf_tensor("c_onesrow", [1, 128], F32R)
    dmai(sy, C.consts["ones_row"][:], dd["ones_row"].bitcast(F32R), s_ld)
    C.consts["ident"] = nc.alloc_sbuf_tensor("c_ident", [128, 128], F32)
    dmai(sy, C.consts["ident"][:], dd["ident"], s_ld)


def emit_ln(C, X, OUT, pfx):
    nc = C.nc
    P = C.P
    consts = C.consts
    s_sq = C.sem("ln_sq")
    s_sqf = C.sem("ln_sqf")
    b_sq, b_sqf = s_sq.n, s_sqf.n
    les = contextlib.ExitStack()
    sq = [sb_scoped(les, nc, f"{pfx}_sq{i}", [128, 1024], F32R) for i in range(2)]
    m_sb = sb_scoped(les, nc, f"{pfx}_m", [1, 1024], F32R)
    r_sb = sb_scoped(les, nc, f"{pfx}_r", [1, 1024], F32R)
    t_sb = sb_scoped(les, nc, f"{pfx}_t", [1, 1024], F32)
    v_sb = sb_scoped(les, nc, f"{pfx}_v", [1, 1024], F32)
    sd_sb = sb_scoped(les, nc, f"{pfx}_sd", [1, 1024], F32)
    nt = sb_scoped(les, nc, f"{pfx}_nt", [128, 1024], F32)

    with nc.Block() as blk:
        @blk.scalar
        def _(sc):
            for c in range(NCH):
                if c >= 2:
                    sc.wait_ge(s_sqf.h, b_sqf + c - 1)
                sc.activation(sq[c % 2][:], X[c][:].bitcast(F32), AF.Square) \
                    .then_inc(s_sq.inc(), 1)

        @blk.tensor
        def _(tn):
            for c in range(NCH):
                for hh in range(2):
                    cl = slice(hh * 512, (hh + 1) * 512)
                    tn.matmul(P[2][0:1, cl], consts["ones_col"][:], X[c][:, cl],
                              start=(c == 0), stop=(c == NCH - 1))
                tn.wait_ge(s_sq.h, b_sq + c + 1)
                for hh in range(2):
                    cl = slice(hh * 512, (hh + 1) * 512)
                    ins = tn.matmul(P[3][0:1, cl], consts["ones_col"][:], sq[c % 2][:, cl],
                                    start=(c == 0), stop=(c == NCH - 1))
                ins.then_inc(s_sqf.inc(), 1)

    with nc.Block() as blk:
        @blk.vector
        def _(v):
            v.tensor_scalar(m_sb[:], P[2][0:1, :], 1.0 / D, None, ALU.mult)
            v.tensor_scalar(v_sb[:], P[3][0:1, :], 1.0 / D, None, ALU.mult)
            v.tensor_tensor(t_sb[:], m_sb[:].bitcast(F32), m_sb[:].bitcast(F32), ALU.mult)
            v.tensor_tensor(v_sb[:], v_sb[:], t_sb[:], ALU.subtract)
            v.tensor_scalar(v_sb[:], v_sb[:], LN_EPS, None, ALU.add)
            v.memset(C.jC[:], 0.0)

    with nc.Block() as blk:
        @blk.scalar
        def _(sc):
            sc.activation(sd_sb[:], v_sb[:], AF.Sqrt)
            sc.memzero(C.jC[:])

    with nc.Block() as blk:
        @blk.vector
        def _(v):
            v.reciprocal(r_sb[:], sd_sb[:])
            v.memset(C.jC[:], 0.0)
            v.memset(C.jC[:], 0.0)

    with nc.Block() as blk:
        @blk.tensor
        def _(tn):
            for hh in range(2):
                cl = slice(hh * 512, (hh + 1) * 512)
                tn.matmul(P[2][:, cl], consts["ones_row"][:], m_sb[0:1, cl],
                          start=True, stop=True)
                tn.matmul(P[3][:, cl], consts["ones_row"][:], r_sb[0:1, cl],
                          start=True, stop=True)

    with nc.Block() as blk:
        @blk.vector
        def _(v):
            for c in range(NCH):
                v.tensor_tensor(nt[:], X[c][:].bitcast(F32), P[2][:], ALU.subtract)
                v.tensor_tensor(OUT[c][:], nt[:], P[3][:], ALU.mult)
            v.memset(C.jC[:], 0.0)
    les.close()


def emit_load_stream(C, X, d_x, sy, s_ld):
    xv = d_x.rearrange("(c p) t -> c p t", p=128)
    for c in range(NCH):
        dmai(sy, X[c][:], xv[c].bitcast(F32R), s_ld)


def emit_store_stream(C, X, d_y, s_st, waits=()):
    nc = C.nc
    with nc.Block() as blk:
        @blk.sync
        def _(sy):
            for s, val in waits:
                sy.wait_ge(s.h, val)
            yv = d_y.rearrange("(c p) t -> c p t", p=128)
            for c in range(NCH):
                dmai(sy, yv[c], X[c][:].bitcast(F32), s_st)
            sy.wait_ge(s_st.h, s_st.n)



def emit_l0ffn(C, X, dd, pfx):
    nc = C.nc
    P = C.P
    es = contextlib.ExitStack()
    sbs = lambda n, shp, dt: sb_scoped(es, nc, pfx + n, shp, dt)

    XN = [sbs(f"xn{c}", [128, 1024], F32R) for c in range(NCH)]
    dn_sb = sbs("dn", [128, 16 * 128], F32R)
    up_sb = sbs("up", [128, D], F32R)
    rt_sb = sbs("rt", [128, 16 * 8], F32R)
    m16_sb = sbs("m16", [8, 128], F32R)
    a_sb = sbs("a", [128, 1024], F32R)
    ag_sb = sbs("ag", [128, 1024], F32R)
    lg_sb = sbs("lg", [8, 1024], F32)
    sct = sbs("sct", [128, 64], F32)
    et = sbs("et", [128, 64], F32)
    ex_t = sbs("ext", [128, 64], F32)
    mx8 = sbs("mx8", [128, 8], F32)
    sm8 = sbs("sm8", [128, 8], F32)
    r8 = sbs("r8", [128, 8], F32)
    jnk = sbs("jnk", [128, 8], F32)
    m1 = sbs("m1", [128, 64], F32)
    m2 = sbs("m2", [128, 64], F32)
    sup = sbs("sup", [128, 64], F32)
    gate_t = sbs("gatet", [128, 64], F32)
    gate_e = sbs("gatee", [8, 1024], F32R)

    s_ld = C.sem("f0_ld")
    s_up = C.sem("f0_up")
    s_upf = C.sem("f0_upf")
    b_up, b_upf = s_up.n, s_upf.n

    with nc.Block() as blk:
        @blk.sync
        def _(sy):
            dmai(sy, dn_sb[:].rearrange("p (c m) -> p c m", c=16),
                 dd["dnT"].rearrange("(c p) m -> p c m", p=128).bitcast(F32R), s_ld)
            dmai(sy, up_sb[:], dd["upT"].bitcast(F32R), s_ld)
            dmai(sy, rt_sb[:].rearrange("p (c m) -> p c m", c=16),
                 dd["rtT"].rearrange("(c p) m -> p c m", p=128).bitcast(F32R), s_ld)
            dmai(sy, m16_sb[:], dd["m16e"].bitcast(F32R), s_ld)
            sy.wait_ge(s_ld.h, s_ld.n)

    emit_ln(C, X, XN, pfx + "ln")

    with nc.Block() as blk:
        @blk.tensor
        def _(tn):
            for c in range(NCH):
                for hh in range(2):
                    cl = slice(hh * 512, (hh + 1) * 512)
                    tn.matmul(P[0][0:8, cl], rt_sb[:, c * 8:(c + 1) * 8], XN[c][:, cl],
                              start=(c == 0), stop=(c == NCH - 1))
                    tn.matmul(P[1][:, cl], dn_sb[:, c * 128:(c + 1) * 128], XN[c][:, cl],
                              start=(c == 0), stop=(c == NCH - 1))

    with nc.Block() as blk:
        @blk.scalar
        def _(sc):
            sc.activation(a_sb[:], P[1][:], AF.Silu)
            sc.memzero(C.jC[:])

        @blk.vector
        def _(v):
            v.tensor_copy(lg_sb[:], P[0][0:8, :])
            v.memset(C.jC[:], 0.0)

    with nc.Block() as blk:
        @blk.tensor
        def _(tn):
            for t8 in range(8):
                tn.transpose(P[2][:, t8 * 8:(t8 + 1) * 8],
                             lg_sb[0:8, t8 * 128:(t8 + 1) * 128],
                             C.consts["ident"][0:8, 0:8])

    S8 = [slice(t8 * 8, (t8 + 1) * 8) for t8 in range(8)]
    with nc.Block() as blk:
        @blk.vector
        def _(v):
            s_hz = C.sem("hz")
            v.tensor_copy(sct[:], P[2][:, 0:64])
            for t8 in range(8):
                v.reduce_max(mx8[:, t8:t8 + 1], sct[:, S8[t8]], axis=AX) \
                    .then_inc(s_hz.inc(), 1)
            v.wait_ge(s_hz.h, s_hz.n)
            for t8 in range(8):
                v.tensor_scalar(et[:, S8[t8]], sct[:, S8[t8]], mx8[:, t8:t8 + 1],
                                None, ALU.subtract)
            v.memset(jnk[:], 0.0)

    with nc.Block() as blk:
        @blk.scalar
        def _(sc):
            sc.activation(ex_t[:], et[:], AF.Exp)
            sc.memzero(C.jC[:])

    with nc.Block() as blk:
        @blk.vector
        def _(v):
            s_hz = C.sem("hz")
            for t8 in range(8):
                v.reduce_sum(sm8[:, t8:t8 + 1], ex_t[:, S8[t8]], axis=AX) \
                    .then_inc(s_hz.inc(), 1)
            v.wait_ge(s_hz.h, s_hz.n)
            v.reciprocal(r8[:], sm8[:]).then_inc(s_hz.inc(), 1)
            v.wait_ge(s_hz.h, s_hz.n)
            for t8 in range(8):
                v.tensor_scalar(sct[:, S8[t8]], ex_t[:, S8[t8]], r8[:, t8:t8 + 1],
                                None, ALU.mult)
            for t8 in range(8):
                v.reduce_max(mx8[:, t8:t8 + 1], sct[:, S8[t8]], axis=AX) \
                    .then_inc(s_hz.inc(), 1)
            v.wait_ge(s_hz.h, s_hz.n)
            for t8 in range(8):
                v.tensor_scalar(m1[:, S8[t8]], sct[:, S8[t8]], mx8[:, t8:t8 + 1],
                                None, ALU.is_ge)
            for t8 in range(8):
                v.scalar_tensor_tensor(sup[:, S8[t8]], m1[:, S8[t8]], -1e30,
                                       sct[:, S8[t8]], ALU.mult, ALU.add)
            for t8 in range(8):
                v.reduce_max(mx8[:, t8:t8 + 1], sup[:, S8[t8]], axis=AX) \
                    .then_inc(s_hz.inc(), 1)
            v.wait_ge(s_hz.h, s_hz.n)
            for t8 in range(8):
                v.tensor_scalar(m2[:, S8[t8]], sup[:, S8[t8]], mx8[:, t8:t8 + 1],
                                None, ALU.is_ge)
            for t8 in range(8):
                v.tensor_tensor(m1[:, S8[t8]], m1[:, S8[t8]], m2[:, S8[t8]], ALU.add)
            for t8 in range(8):
                v.tensor_tensor(gate_t[:, S8[t8]], sct[:, S8[t8]], m1[:, S8[t8]], ALU.mult)
            v.memset(jnk[:], 0.0)

    with nc.Block() as blk:
        @blk.tensor
        def _(tn):
            for t8 in range(8):
                tn.transpose(P[2][0:8, t8 * 128:(t8 + 1) * 128],
                             gate_t[:, t8 * 8:(t8 + 1) * 8],
                             C.consts["ident"][:, 0:128])

    with nc.Block() as blk:
        @blk.vector
        def _(v):
            v.tensor_copy(gate_e[:], P[2][0:8, :])
            v.memset(C.jC[:], 0.0)

    with nc.Block() as blk:
        @blk.tensor
        def _(tn):
            for hh in range(2):
                cl = slice(hh * 512, (hh + 1) * 512)
                tn.matmul(P[3][:, cl], m16_sb[:], gate_e[0:8, cl], start=True, stop=True)

    with nc.Block() as blk:
        @blk.vector
        def _(v):
            v.tensor_tensor(ag_sb[:], a_sb[:].bitcast(F32), P[3][:], ALU.mult)
            v.memset(C.jC[:], 0.0)

    with nc.Block() as blk:
        @blk.tensor
        def _(tn):
            for m in range(NCH):
                if m >= 2:
                    tn.wait_ge(s_upf.h, b_upf + m - 1)
                for hh in range(2):
                    cl = slice(hh * 512, (hh + 1) * 512)
                    ins = tn.matmul(P[m % 2][:, cl], up_sb[:, m * 128:(m + 1) * 128],
                                    ag_sb[:, cl], start=True, stop=True)
                ins.then_inc(s_up.inc(), 1)

        @blk.vector
        def _(v):
            for m in range(NCH):
                v.wait_ge(s_up.h, b_up + m + 1)
                v.scalar_tensor_tensor(X[m][:], P[m % 2][:], 0.5, X[m][:].bitcast(F32),
                                       ALU.mult, ALU.add).then_inc(s_upf.inc(), 1)

    es.close()
    return [(s_upf, s_upf.n)]



def emit_l0attn(C, X, dd, pfx):
    nc = C.nc
    P = C.P
    es0 = contextlib.ExitStack()
    sbs = lambda n, shp, dt: sb_scoped(es0, nc, pfx + n, shp, dt)

    agx_in = nc.dram_tensor(pfx + "agx_in", [D, TOK], F32)
    agx_out = nc.dram_tensor(pfx + "agx_out", [NCORE * D, TOK], F32, addr_space="Shared")
    agd_in = nc.dram_tensor(pfx + "agd_in", [160, TOK], F32)
    agd_out = nc.dram_tensor(pfx + "agd_out", [NCORE * 160, TOK], F32, addr_space="Shared")
    y_mine = nc.dram_tensor(pfx + "y_mine", [8 * DSL, 1024], F32)
    y_full = nc.dram_tensor(pfx + "y_full", [8 * NCORE * DSL, 1024], F32,
                            addr_space="Shared")

    gp_sb = sbs("gp", [128, 16 * 128], F32R)
    eb_sb = sbs("eb", [32, 128], F32R)
    ec_sb = sbs("ec", [32, 128], F32R)
    rp_sb = sbs("rp", [128, 8], F32R)
    dtp_sb = sbs("dtp", [128, DSL], F32R)
    dtb_sb = sbs("dtb", [128, 2], F32)
    alog_sb = sbs("alog", [128, 32], F32)
    apat_sb = sbs("apat", [128, 32], F32)
    D_sb = sbs("Dsb", [128, 16], F32)
    carry = sbs("carry", [128, 32], F32)

    s_ld = C.sem("m_ld")
    s_ag = C.sem("m_ag")
    s_cc = C.sem("m_cc")
    s_u = C.sem("m_u"); s_uf = C.sem("m_uf")
    s_dr = C.sem("m_dr")
    s_gdp = C.sem("m_gdp"); s_gda = C.sem("m_gda")
    s_gup = C.sem("m_gup"); s_gbu = C.sem("m_gbu")
    s_gsc = C.sem("m_gsc"); s_gt0 = C.sem("m_gt0"); s_gt1 = C.sem("m_gt1")
    s_sc1 = C.sem("m_sc1")
    s_gym = C.sem("m_gym"); s_gye = C.sem("m_gye"); s_gyd = C.sem("m_gyd")
    s_yl = C.sem("m_yl")
    s_fin = C.sem("m_fin")
    b = {s: s.n for s in (s_u, s_uf, s_gdp, s_gda, s_gup, s_gbu, s_gsc, s_gt0,
                          s_gt1, s_sc1, s_gym, s_gye, s_gyd, s_yl, s_cc, s_fin)}

    with nc.Block() as blk:
        @blk.sync
        def _(sy):
            dmai(sy, gp_sb[:].rearrange("d (g p) -> d g p", g=16),
                 dd["g_pat"].rearrange("(g d) p -> d g p", d=128).bitcast(F32R), s_ld)
            dmai(sy, eb_sb[:], dd["e_b"].bitcast(F32R), s_ld)
            dmai(sy, ec_sb[:], dd["e_c"].bitcast(F32R), s_ld)
            dmai(sy, rp_sb[:], dd["r_pat"].bitcast(F32R), s_ld)
            dmai(sy, dtp_sb[:], dd["dtprojT_sl"].bitcast(F32R), s_ld)
            dmai(sy, dtb_sb[:], dd["dtb_sl"], s_ld)
            dmai(sy, alog_sb[:], dd["alogp"], s_ld)
            dmai(sy, D_sb[:], dd["D_sb"], s_ld)
            sy.wait_ge(s_ld.h, s_ld.n)

    with nc.Block() as blk:
        @blk.scalar
        def _(sc):
            sc.activation(apat_sb[:], alog_sb[:], AF.Exp)
            sc.memzero(C.jC[:])

    with nc.Block() as blk:
        @blk.vector
        def _(v):
            v.tensor_scalar(apat_sb[:], apat_sb[:], -1.0, None, ALU.mult)
            v.memset(C.jC[:], 0.0)

    es1 = contextlib.ExitStack()
    XN = [sb_scoped(es1, nc, f"{pfx}xn{c}", [128, 1024], F32R) for c in range(NCH)]
    emit_ln(C, X, XN, pfx + "ln")

    xpj_sb = sb_scoped(es1, nc, pfx + "xpj", [128, 16 * 160], F32R)
    dr_sb = sb_scoped(es1, nc, pfx + "drs", [128, 1024], F32R)
    bc_sb = sb_scoped(es1, nc, pfx + "bcs", [32, 1024], F32R)

    with nc.Block() as blk:
        @blk.sync
        def _(sy):
            dmai(sy, xpj_sb[:].rearrange("p (c m) -> p c m", c=16),
                 dd["xprojT"].rearrange("(c p) m -> p c m", p=128).bitcast(F32R), s_ld)
            sy.wait_ge(s_ld.h, s_ld.n)

    with nc.Block() as blk:
        @blk.tensor
        def _(tn):
            for c in range(NCH):
                for hh in range(2):
                    cl = slice(hh * 512, (hh + 1) * 512)
                    tn.matmul(P[0][:, cl], xpj_sb[:, c * 160:c * 160 + 128],
                              XN[c][:, cl], start=(c == 0), stop=(c == NCH - 1))
                    tn.matmul(P[1][0:32, cl], xpj_sb[:, c * 160 + 128:c * 160 + 160],
                              XN[c][:, cl], start=(c == 0), stop=(c == NCH - 1))


    with nc.Block() as blk:
        @blk.vector
        def _(v):
            v.tensor_copy(dr_sb[:], P[0][:])
            v.tensor_copy(bc_sb[:], P[1][0:32, :])
            v.memset(C.jC[:], 0.0)

    with nc.Block() as blk:
        @blk.sync
        def _(sy):
            xv = agx_in.ap().rearrange("(c p) t -> c p t", p=128)
            for c in range(NCH):
                dmai(sy, xv[c], XN[c][:].bitcast(F32), s_ag)
            dmai(sy, agd_in.ap()[0:128, :], dr_sb[:].bitcast(F32), s_ag)
            dmai(sy, agd_in.ap()[128:160, :], bc_sb[:].bitcast(F32), s_ag)

        @blk.gpsimd
        def _(g):
            g.wait_ge(s_ag.h, s_ag.n)
            g.collective_compute("AllGather", ALU.bypass,
                                 replica_groups=[list(range(NCORE))],
                                 ins=[agx_in.ap()], outs=[agx_out.ap()]) \
                .then_inc(s_cc.inc(), 1)
            g.collective_compute("AllGather", ALU.bypass,
                                 replica_groups=[list(range(NCORE))],
                                 ins=[agd_in.ap()], outs=[agd_out.ap()]) \
                .then_inc(s_cc.inc(), 1)
            g.wait_ge(s_cc.h, b[s_cc] + 2)

    es1.close()

    es2 = contextlib.ExitStack()
    sbt = lambda n, shp, dt: sb_scoped(es2, nc, pfx + n, shp, dt)
    u_t = [[sbt(f"u{s}_{dh}", [128, 1024], F32) for dh in range(2)] for s in range(2)]
    drt = [sbt(f"drt{s}", [128, 1024], F32R) for s in range(2)]
    bct = [sbt(f"bct{s}", [32, 1024], F32R) for s in range(2)]
    zt = [sbt(f"zt{dh}", [128, 1024], F32) for dh in range(2)]
    ab2 = [sbt(f"ab2{dh}", [128, 1024], F32) for dh in range(2)]
    el = [sbt(f"el{dh}", [128, 1024], F32) for dh in range(2)]
    delta = [sbt(f"dl{dh}", [128, 1024], F32R) for dh in range(2)]
    du = [sbt(f"du{dh}", [128, 1024], F32R) for dh in range(2)]
    Bp = sbt("Bp", [128, 1024], F32)
    Cp = sbt("Cp", [128, 1024], F32)
    dA = [sbt(f"dA{s}", [128, 1024], F32) for s in range(2)]
    dBu = [sbt(f"dBu{s}", [128, 1024], F32) for s in range(2)]
    h_t = [sbt(f"h{s}", [128, 1024], F32) for s in range(3)]
    tmp = [sbt(f"tmp{s}", [128, 1024], F32R) for s in range(3)]
    y_sb = [sbt(f"ysb{s}", [8, 1024], F32) for s in range(2)]

    NTC = 8
    for tc in range(NTC):
        for arm in nc.switch_core_id(engines=[nc.sync], n=128):
            try:
                rank = int(arm.logical) % NCORE
            except Exception:
                rank = 0
            if tc >= 2:
                nc.sync.wait_ge(s_uf.h, b[s_uf] + tc - 1)
            for dh in range(2):
                nc.sync.dma_start(
                    out=u_t[tc % 2][dh][:],
                    in_=agx_out.ap()[tc * D + rank * DSL + dh * 128:
                                     tc * D + rank * DSL + (dh + 1) * 128, :]) \
                    .then_inc(s_u.h, 16)
        s_u.n += 32

        with nc.Block() as blk:
            @blk.sync
            def _(sy):
                dmai(sy, drt[tc % 2][:], agd_out.ap()[tc * 160:tc * 160 + 128, :]
                     .bitcast(F32R), s_dr)
                dmai(sy, bct[tc % 2][:], agd_out.ap()[tc * 160 + 128:tc * 160 + 160, :]
                     .bitcast(F32R), s_dr)
                sy.wait_ge(s_dr.h, s_dr.n)

        with nc.Block() as blk:
            @blk.tensor
            def _(tn):
                for dh in range(2):
                    for hh in range(2):
                        cl = slice(hh * 512, (hh + 1) * 512)
                        tn.matmul(P[dh][:, cl], dtp_sb[:, dh * 128:(dh + 1) * 128],
                                  drt[tc % 2][:, cl], start=True, stop=True)
                for hh in range(2):
                    cl = slice(hh * 512, (hh + 1) * 512)
                    tn.matmul(P[2][:, cl], eb_sb[:], bct[tc % 2][0:32, cl],
                              start=True, stop=True)
                    tn.matmul(P[3][:, cl], ec_sb[:], bct[tc % 2][0:32, cl],
                              start=True, stop=True)

        with nc.Block() as blk:
            @blk.scalar
            def _(sc):
                for dh in range(2):
                    sc.activation(zt[dh][:], P[dh][:], AF.Identity,
                                  bias=dtb_sb[:, dh:dh + 1])
                sc.memzero(C.jC[:])

        with nc.Block() as blk:
            @blk.vector
            def _(v):
                for dh in range(2):
                    v.scalar_tensor_tensor(ab2[dh][:], zt[dh][:], -1.0, zt[dh][:],
                                           ALU.mult, ALU.max)
                v.tensor_copy(Bp[:], P[2][:])
                v.tensor_copy(Cp[:], P[3][:])
                v.memset(C.jC[:], 0.0)

        with nc.Block() as blk:
            @blk.scalar
            def _(sc):
                for dh in range(2):
                    sc.activation(el[dh][:], ab2[dh][:], AF.Exp, scale=-1.0)
                for dh in range(2):
                    sc.activation(ab2[dh][:], el[dh][:], AF.Ln, bias=1.0)
                sc.memzero(C.jC[:])

        with nc.Block() as blk:
            @blk.vector
            def _(v):
                v.wait_ge(s_u.h, b[s_u] + 32 * (tc + 1))
                for dh in range(2):
                    v.scalar_tensor_tensor(delta[dh][:], zt[dh][:], 0.0, ab2[dh][:],
                                           ALU.max, ALU.add)
                    ins = v.tensor_tensor(du[dh][:], delta[dh][:].bitcast(F32),
                                          u_t[tc % 2][dh][:], ALU.mult)
                    if dh == 1:
                        ins.then_inc(s_uf.inc(), 1)
                v.memset(C.jC[:], 0.0)

        with nc.Block() as blk:
            @blk.tensor
            def _(tn):
                for g in range(33):
                    gq = tc * 32 + g
                    if g < 32:
                        dh = g // 16
                        gl = g % 16
                        for hh in range(2):
                            q2 = 2 * gq + hh
                            cl = slice(hh * 512, (hh + 1) * 512)
                            if q2 >= 2:
                                tn.wait_ge(s_gda.h, b[s_gda] + q2 - 1)
                            tn.matmul(P[0][:, cl], gp_sb[:, gl * 128:(gl + 1) * 128],
                                      delta[dh][:, cl], start=True, stop=True) \
                                .then_inc(s_gdp.inc(), 1)
                            if q2 >= 2:
                                tn.wait_ge(s_gbu.h, b[s_gbu] + q2 - 1)
                            tn.matmul(P[1][:, cl], gp_sb[:, gl * 128:(gl + 1) * 128],
                                      du[dh][:, cl], start=True, stop=True) \
                                .then_inc(s_gup.inc(), 1)
                    if g >= 1:
                        gy = gq - 1
                        if gy >= 2:
                            tn.wait_ge(s_gye.h, b[s_gye] + gy - 1)
                        for hh in range(2):
                            cl = slice(hh * 512, (hh + 1) * 512)
                            sh = s_gt0 if hh == 0 else s_gt1
                            tn.wait_ge(sh.h, b[sh] + gy + 1)
                            ins = tn.matmul(P[2 + (gy % 2)][0:8, cl], rp_sb[:],
                                            tmp[gy % 3][:, cl], start=True, stop=True)
                        ins.then_inc(s_gym.inc(), 1)

            @blk.scalar
            def _(sc):
                for g in range(33):
                    gq = tc * 32 + g
                    if g < 32:
                        for hh in range(2):
                            q2 = 2 * gq + hh
                            cl = slice(hh * 512, (hh + 1) * 512)
                            sc.wait_ge(s_gdp.h, b[s_gdp] + q2 + 1)
                            if gq >= 2:
                                sc.wait_ge(s_sc1.h, b[s_sc1] + gq - 1)
                            sc.activation(dA[gq % 2][:, cl], P[0][:, cl], AF.Exp,
                                          scale=apat_sb[:, g:g + 1]) \
                                .then_inc(s_gda.inc(), 1)
                    if g >= 1:
                        gy = gq - 1
                        sc.wait_ge(s_gym.h, b[s_gym] + gy + 1)
                        if gy >= 2:
                            sc.wait_ge(s_gyd.h, b[s_gyd] + 16 * (gy - 1))
                        sc.activation(y_sb[gy % 2][:], P[2 + (gy % 2)][0:8, :],
                                      AF.Identity).then_inc(s_gye.inc(), 1)

            @blk.vector
            def _(v):
                for g in range(33):
                    gq = tc * 32 + g
                    if g < 32:
                        for hh in range(2):
                            q2 = 2 * gq + hh
                            cl = slice(hh * 512, (hh + 1) * 512)
                            v.wait_ge(s_gup.h, b[s_gup] + q2 + 1)
                            if gq >= 2:
                                v.wait_ge(s_sc1.h, b[s_sc1] + gq - 1)
                            v.tensor_tensor(dBu[gq % 2][:, cl], P[1][:, cl], Bp[:, cl],
                                            ALU.mult).then_inc(s_gbu.inc(), 1)
                    if g >= 1:
                        gp_ = gq - 1
                        v.wait_ge(s_sc1.h, b[s_sc1] + gp_ + 1)
                        if gp_ >= 3:
                            v.wait_ge(s_gym.h, b[s_gym] + gp_ - 2)
                        v.tensor_tensor(tmp[gp_ % 3][:, 0:512], h_t[gp_ % 3][:, 0:512],
                                        Cp[:, 0:512], ALU.mult).then_inc(s_gt0.inc(), 1)
                    if g < 32:
                        v.wait_ge(s_gda.h, b[s_gda] + 2 * gq + 2)
                        if gq >= 3:
                            v.wait_ge(s_gt1.h, b[s_gt1] + gq - 2)
                        if tc not in (0, 4):
                            v.wait_ge(s_gsc.h, b[s_gsc] + gq - 31)
                        ini = 0.0 if tc in (0, 4) else carry[:, g:g + 1]
                        v.tensor_tensor_scan(h_t[gq % 3][:], dA[gq % 2][:],
                                             dBu[gq % 2][:], ini, ALU.mult, ALU.add) \
                            .then_inc(s_sc1.inc(), 1)

            @blk.gpsimd
            def _(gp):
                for g in range(32):
                    gq = tc * 32 + g
                    gp.wait_ge(s_sc1.h, b[s_sc1] + gq + 1)
                    gp.tensor_copy(carry[:, g:g + 1], h_t[gq % 3][:, 1023:1024]) \
                        .then_inc(s_gsc.inc(), 1)
                    if gq >= 3:
                        gp.wait_ge(s_gym.h, b[s_gym] + gq - 2)
                    gp.tensor_tensor(tmp[gq % 3][:, 512:1024], h_t[gq % 3][:, 512:1024],
                                     Cp[:, 512:1024], ALU.mult).then_inc(s_gt1.inc(), 1)
                gp.wait_ge(s_gyd.h, b[s_gyd] + (tc + 1) * 512)
                gp.collective_compute("AllGather", ALU.bypass,
                                      replica_groups=[list(range(NCORE))],
                                      ins=[y_mine.ap()[tc * DSL:(tc + 1) * DSL, :]],
                                      outs=[y_full.ap()[tc * NCORE * DSL:
                                                        (tc + 1) * NCORE * DSL, :]]) \
                    .then_inc(s_cc.inc(), 1)

            @blk.sync
            def _(sy):
                for g in range(32):
                    gq = tc * 32 + g
                    sy.wait_ge(s_gye.h, b[s_gye] + gq + 1)
                    sy.dma_start(out=y_mine.ap()[tc * DSL + 8 * g:tc * DSL + 8 * g + 8, :],
                                 in_=y_sb[gq % 2][:]).then_inc(s_gyd.inc(16), 16)

    with nc.Block() as blk:
        @blk.gpsimd
        def _(g):
            g.wait_ge(s_cc.h, b[s_cc] + 10)

    es2.close()
    es3 = contextlib.ExitStack()
    yloc = [sb_scoped(es3, nc, f"{pfx}yl{c}", [128, 1024], F32) for c in range(NCH)]
    xnl = [sb_scoped(es3, nc, f"{pfx}xnl{c}", [128, 1024], F32) for c in range(NCH)]

    for arm in nc.switch_core_id(engines=[nc.sync], n=128):
        try:
            rank = int(arm.logical) % NCORE
        except Exception:
            rank = 0
        for c in range(NCH):
            nc.sync.dma_start(out=yloc[c][:],
                              in_=y_full.ap()[rank * NCORE * DSL + c * 128:
                                              rank * NCORE * DSL + (c + 1) * 128, :]) \
                .then_inc(s_yl.h, 16)
            nc.sync.dma_start(out=xnl[c][:],
                              in_=agx_out.ap()[rank * D + c * 128:
                                               rank * D + (c + 1) * 128, :]) \
                .then_inc(s_yl.h, 16)
    s_yl.n += 32 * NCH

    yf = sb_scoped(es3, nc, pfx + "yf", [128, 1024], F32)
    with nc.Block() as blk:
        @blk.vector
        def _(v):
            v.wait_ge(s_yl.h, s_yl.n)
            for c in range(NCH):
                v.scalar_tensor_tensor(yf[:], xnl[c][:], D_sb[:, c:c + 1],
                                       yloc[c][:], ALU.mult, ALU.add)
                v.tensor_tensor(X[c][:], X[c][:].bitcast(F32), yf[:], ALU.add) \
                    .then_inc(s_fin.inc(), 1)

    es3.close()
    es0.close()
    return [(s_fin, s_fin.n)]



def emit_l1ffn(C, X, dd, pfx):
    nc = C.nc
    P = C.P
    es = contextlib.ExitStack()
    sbs = lambda n, shp, dt: sb_scoped(es, nc, pfx + n, shp, dt)

    XNb = [sbs(f"xnb{c}", [128, 1024], BF16) for c in range(NCH)]
    gt_sb = sbs("gt", [128, 16 * 8], BF16)
    gb_sb = sbs("gb", [1, 8], F32R)
    oh8_sb = sbs("oh8s", [8, 8 * 128], F32R)
    sc_sb = sbs("scs", [8, 1024], F32)
    sct = sbs("sct", [128, 64], F32)
    ab = sbs("ab", [128, 64], F32)
    lb = sbs("lb", [128, 64], F32)
    ssq = sbs("ssq", [128, 64], F32)
    bt = sbs("bt", [128, 64], F32)
    m1 = sbs("m1", [128, 64], F32)
    m2 = sbs("m2", [128, 64], F32)
    sup = sbs("sup", [128, 64], F32)
    mx8 = sbs("mx8", [128, 8], F32)
    cw_t = sbs("cwt", [128, 64], F32)
    cwe_sb = sbs("cwe", [8, 1024], F32R)
    cwb_sb = sbs("cwb", [128, 1024], F32)
    th = [sbs(f"th{i}", [128, 1024], F32) for i in range(2)]
    hbt = sbs("hbt", [128, 1024], F32)
    hb = [sbs(f"hb{i}", [128, 1024], BF16) for i in range(NI)]
    w13 = [sbs(f"w13_{i}", [128, 16 * 128], BF16) for i in range(4)]
    w2b = [sbs(f"w2_{i}", [128, NI * 128], BF16) for i in range(2)]

    s_ld = C.sem("g_ld")
    s_wl = C.sem("g_wl")
    s_wf13 = C.sem("g_wf13")
    s_sl = C.sem("g_sl")
    s_tt1 = C.sem("g_tt1")
    s_hb = C.sem("g_hb")
    s_o = C.sem("g_o")
    s_ox = C.sem("g_ox")
    s_cwm = C.sem("g_cwm")
    s_cwb = C.sem("g_cwb")
    bS = {s: s.n for s in (s_wl, s_wf13, s_sl, s_tt1, s_hb, s_o, s_ox, s_cwm, s_cwb)}

    with nc.Block() as blk:
        @blk.sync
        def _(sy):
            dmai(sy, gb_sb[:], dd["gate_b"].bitcast(F32R), s_ld)
            dmai(sy, oh8_sb[:], dd["oh8"].bitcast(F32R), s_ld)
            sy.wait_ge(s_ld.h, s_ld.n)

        @blk.gpsimd
        def _(g):
            g.dma_start(out=gt_sb[:].rearrange("p (c m) -> p c m", c=16),
                        in_=dd["gateT"].rearrange("(c p) m -> p c m", p=128)) \
                .then_inc(s_ld.inc(16), 16)
            g.wait_ge(s_ld.h, s_ld.n)

    emit_ln(C, X, XNb, pfx + "ln")

    with nc.Block() as blk:
        @blk.tensor
        def _(tn):
            for c in range(NCH):
                for hh in range(2):
                    cl = slice(hh * 512, (hh + 1) * 512)
                    tn.matmul(P[0][0:8, cl], gt_sb[:, c * 8:(c + 1) * 8], XNb[c][:, cl],
                              start=(c == 0), stop=(c == NCH - 1))

    with nc.Block() as blk:
        @blk.vector
        def _(v):
            v.tensor_copy(sc_sb[:], P[0][0:8, :])
            v.memset(C.jC[:], 0.0)

    with nc.Block() as blk:
        @blk.tensor
        def _(tn):
            for t8 in range(8):
                tn.transpose(P[1][:, t8 * 8:(t8 + 1) * 8],
                             sc_sb[0:8, t8 * 128:(t8 + 1) * 128],
                             C.consts["ident"][0:8, 0:8])
            tn.matmul(P[2][:, 0:8], C.consts["ones_row"][:], gb_sb[:], start=True, stop=True)

    with nc.Block() as blk:
        @blk.vector
        def _(v):
            v.tensor_copy(sct[:], P[1][:, 0:64])
            v.scalar_tensor_tensor(ab[:], sct[:], -1.0, sct[:], ALU.mult, ALU.max)
            v.memset(C.jC[:], 0.0)

    with nc.Block() as blk:
        @blk.scalar
        def _(sc):
            sc.activation(lb[:], ab[:], AF.Exp, scale=-1.0)
            sc.memzero(C.jC[:])

    with nc.Block() as blk:
        @blk.scalar
        def _(sc):
            sc.activation(ab[:], lb[:], AF.Ln, bias=1.0)
            sc.memzero(C.jC[:])

    with nc.Block() as blk:
        @blk.vector
        def _(v):
            v.scalar_tensor_tensor(lb[:], sct[:], 0.0, ab[:], ALU.max, ALU.add)
            v.memset(C.jC[:], 0.0)

    with nc.Block() as blk:
        @blk.scalar
        def _(sc):
            sc.activation(ssq[:], lb[:], AF.Sqrt)
            sc.memzero(C.jC[:])

    S8 = [slice(t8 * 8, (t8 + 1) * 8) for t8 in range(8)]
    with nc.Block() as blk:
        @blk.vector
        def _(v):
            s_hz = C.sem("hz")
            for t8 in range(8):
                v.tensor_tensor(bt[:, S8[t8]], ssq[:, S8[t8]], P[2][:, 0:8], ALU.add)
            for t8 in range(8):
                v.reduce_max(mx8[:, t8:t8 + 1], bt[:, S8[t8]], axis=AX) \
                    .then_inc(s_hz.inc(), 1)
            v.wait_ge(s_hz.h, s_hz.n)
            for t8 in range(8):
                v.tensor_scalar(m1[:, S8[t8]], bt[:, S8[t8]], mx8[:, t8:t8 + 1],
                                None, ALU.is_ge)
            for t8 in range(8):
                v.scalar_tensor_tensor(sup[:, S8[t8]], m1[:, S8[t8]], -1e30,
                                       bt[:, S8[t8]], ALU.mult, ALU.add)
            for t8 in range(8):
                v.reduce_max(mx8[:, t8:t8 + 1], sup[:, S8[t8]], axis=AX) \
                    .then_inc(s_hz.inc(), 1)
            v.wait_ge(s_hz.h, s_hz.n)
            for t8 in range(8):
                v.tensor_scalar(m2[:, S8[t8]], sup[:, S8[t8]], mx8[:, t8:t8 + 1],
                                None, ALU.is_ge)
            for t8 in range(8):
                v.tensor_tensor(m1[:, S8[t8]], m1[:, S8[t8]], m2[:, S8[t8]], ALU.add)
            for t8 in range(8):
                v.scalar_tensor_tensor(cw_t[:, S8[t8]], m1[:, S8[t8]], 0.5,
                                       ssq[:, S8[t8]], ALU.mult, ALU.mult)
            v.memset(C.jC[:], 0.0)

    with nc.Block() as blk:
        @blk.tensor
        def _(tn):
            for t8 in range(8):
                tn.transpose(P[1][0:8, t8 * 128:(t8 + 1) * 128],
                             cw_t[:, t8 * 8:(t8 + 1) * 8],
                             C.consts["ident"][:, 0:128])

    with nc.Block() as blk:
        @blk.vector
        def _(v):
            v.tensor_copy(cwe_sb[:], P[1][0:8, :])
            v.memset(C.jC[:], 0.0)

    elist = list(range(NEXP)) if _EL1 is None else list(_EL1)
    nrouted_before = [sum(1 for x in elist[:k] if x < 8) for k in range(len(elist) + 1)]
    with nc.Block() as blk:
        @blk.sync
        def _(sy):
            for ei, e in enumerate(elist):
                for i in range(NI):
                    for wd, nm in ((dd["w1T"], 0), (dd["w3T"], 1)):
                        j13 = ei * 22 + 2 * i + nm
                        if j13 >= 4:
                            sy.wait_ge(s_wf13.h, bS[s_wf13] + j13 - 3)
                        dmai(sy, w13[j13 % 4][:].rearrange("p (c m) -> p c m", c=16),
                             wd[e, :, i * 128:(i + 1) * 128]
                             .rearrange("(c p) m -> p c m", p=128), s_wl)
                for ddx in range(NCH):
                    j2 = ei * 16 + ddx
                    if j2 >= 2:
                        sy.wait_ge(s_o.h, bS[s_o] + j2 - 1)
                    dmai(sy, w2b[j2 % 2][:].rearrange("p (c m) -> p c m", c=NI),
                         dd["w2T"][e, :, ddx * 128:(ddx + 1) * 128]
                         .rearrange("(c p) m -> p c m", p=128), s_wl)

        @blk.tensor
        def _(tn):
            jg = 0
            for ei, e in enumerate(elist):
                routed = e < 8
                if routed:
                    if ei > 0:
                        tn.wait_ge(s_hb.h, bS[s_hb] + 11 * ei)
                        tn.wait_ge(s_ox.h, bS[s_ox] + 16 * ei)
                    for hh in range(2):
                        cl = slice(hh * 512, (hh + 1) * 512)
                        ins = tn.matmul(P[2][:, cl], oh8_sb[0:8, e * 128:(e + 1) * 128],
                                        cwe_sb[0:8, cl], start=True, stop=True)
                    ins.then_inc(s_cwm.inc(), 1)
                elif ei > 0:
                    tn.wait_ge(s_hb.h, bS[s_hb] + 11 * ei)
                    tn.wait_ge(s_ox.h, bS[s_ox] + 16 * ei)
                for i in range(NI):
                    for nm in range(2):
                        tn.wait_ge(s_wl.h, bS[s_wl] + 16 * (jg + 1))
                        slot = w13[(ei * 22 + 2 * i + nm) % 4]
                        dst = P[2 * (i % 2) + nm]
                        if i >= 2 and nm == 0:
                            tn.wait_ge(s_sl.h, bS[s_sl] + 11 * ei + i - 1)
                            tn.wait_ge(s_tt1.h, bS[s_tt1] + 11 * ei + i - 1)
                        if i == 1 and nm == 0 and routed:
                            tn.wait_ge(s_cwb.h, bS[s_cwb] + nrouted_before[ei] + 1)
                        for k in range(NCH):
                            for hh in range(2):
                                cl = slice(hh * 512, (hh + 1) * 512)
                                ins = tn.matmul(dst[:, cl], slot[:, k * 128:(k + 1) * 128],
                                                XNb[k][:, cl], start=(k == 0), stop=(k == NCH - 1))
                        ins.then_inc(s_wf13.inc(), 1)
                        jg += 1
                tn.wait_ge(s_hb.h, bS[s_hb] + 11 * (ei + 1))
                for ddx in range(NCH):
                    tn.wait_ge(s_wl.h, bS[s_wl] + 16 * (jg + 1))
                    if ddx >= 2:
                        tn.wait_ge(s_ox.h, bS[s_ox] + 16 * ei + ddx - 1)
                    slot = w2b[(ei * 16 + ddx) % 2]
                    dst = P[ddx % 2]
                    for k in range(NI):
                        for hh in range(2):
                            cl = slice(hh * 512, (hh + 1) * 512)
                            ins = tn.matmul(dst[:, cl], slot[:, k * 128:(k + 1) * 128],
                                            hb[k][:, cl], start=(k == 0), stop=(k == NI - 1))
                    ins.then_inc(s_o.inc(), 1)
                    jg += 1

        @blk.scalar
        def _(sc):
            for ei, e in enumerate(elist):
                for i in range(NI):
                    gi = 11 * ei + i
                    sc.wait_ge(s_wf13.h, bS[s_wf13] + 22 * ei + 2 * i + 1)
                    if gi >= 2:
                        sc.wait_ge(s_tt1.h, bS[s_tt1] + gi - 1)
                    sc.activation(th[gi % 2][:], P[2 * (i % 2)][:], AF.Silu) \
                        .then_inc(s_sl.inc(), 1)

        @blk.vector
        def _(v):
            for ei, e in enumerate(elist):
                routed = e < 8
                if routed:
                    v.wait_ge(s_cwm.h, bS[s_cwm] + nrouted_before[ei] + 1)
                    v.tensor_copy(cwb_sb[:], P[2][:]).then_inc(s_cwb.inc(), 1)
                for i in range(NI):
                    gi = 11 * ei + i
                    v.wait_ge(s_sl.h, bS[s_sl] + gi + 1)
                    v.wait_ge(s_wf13.h, bS[s_wf13] + 22 * ei + 2 * i + 2)
                    v.tensor_tensor(hbt[:], th[gi % 2][:], P[2 * (i % 2) + 1][:],
                                    ALU.mult).then_inc(s_tt1.inc(), 1)
                    if routed:
                        v.tensor_tensor(hb[i][:], hbt[:], cwb_sb[:], ALU.mult) \
                            .then_inc(s_hb.inc(), 1)
                    else:
                        v.tensor_scalar(hb[i][:], hbt[:], 0.5, None, ALU.mult) \
                            .then_inc(s_hb.inc(), 1)
                for ddx in range(NCH):
                    v.wait_ge(s_o.h, bS[s_o] + 16 * ei + ddx + 1)
                    v.tensor_tensor(X[ddx][:], X[ddx][:].bitcast(F32), P[ddx % 2][:],
                                    ALU.add).then_inc(s_ox.inc(), 1)

    es.close()
    return [(s_ox, s_ox.n)]



def emit_l1attn(C, X, dd, pfx):
    nc = C.nc
    P = C.P
    es = contextlib.ExitStack()
    sbs = lambda n, shp, dt: sb_scoped(es, nc, pfx + n, shp, dt)

    ar_in = nc.dram_tensor(pfx + "ar_in", [128, 1040], F32)
    ar_out = nc.dram_tensor(pfx + "ar_out", [128, 1040], F32)

    qe = [sbs(f"qe{j}", [128, 1024], BF16) for j in range(8)]
    ke = [sbs(f"ke{j}", [128, 1024], BF16) for j in range(8)]
    kv_sb = sbs("kv", [128, 1024], F32R)
    ks_bd = sbs("ksbd", [128, 16], F32R)
    kst = sbs("kst", [128, 8], F32)
    rcp_t = sbs("rcpt", [1, 1024], F32)
    rcp = sbs("rcp", [1, 1024], F32R)
    wmq_sb = sbs("wmq", [128, 64], BF16)
    wmk_sb = sbs("wmk", [128, 64], BF16)
    bq_sb = sbs("bqs", [128, 16], F32)
    bk_sb = sbs("bks", [128, 16], F32)
    bo_sb = sbs("bos", [128, 16], F32)
    bv_bf = sbs("bvbf", [1, D], BF16)
    orow_bf = sbs("orowbf", [1, 128], BF16)
    ident_bf = sbs("identbf", [128, 128], BF16)

    s_ld = C.sem("a_ld")
    s_w = C.sem("a_w")
    s_pj = C.sem("a_pj"); s_bh = C.sem("a_bh")
    s_qm = C.sem("a_qm"); s_e1 = C.sem("a_e1"); s_e2 = C.sem("a_e2")
    s_elu = C.sem("a_elu")
    s_vm = C.sem("a_vm"); s_ve = C.sem("a_ve"); s_vw = C.sem("a_vw")
    s_tp = C.sem("a_tp"); s_te = C.sem("a_te")
    s_cc = C.sem("a_cc"); s_ar = C.sem("a_ar")
    s_nm = C.sem("a_nm"); s_nr = C.sem("a_nr")
    s_am = C.sem("a_am"); s_ae = C.sem("a_ae"); s_ad = C.sem("a_ad")
    s_om = C.sem("a_om"); s_oe = C.sem("a_oe"); s_od = C.sem("a_od")
    bA = {s: s.n for s in (s_vm, s_ve, s_vw, s_tp, s_te, s_cc, s_nm, s_nr,
                           s_am, s_ae, s_ad, s_om, s_oe, s_od)}

    with nc.Block() as blk:
        @blk.sync
        def _(sy):
            for dst, srcd in ((bq_sb, dd["bq"]), (bk_sb, dd["bk"]), (bo_sb, dd["bo"])):
                dmai(sy, dst[:], srcd, s_ld)
            sy.wait_ge(s_ld.h, s_ld.n)

        @blk.gpsimd
        def _(g):
            g.dma_start(out=wmq_sb[:], in_=dd["wmqT"]).then_inc(s_ld.inc(16), 16)
            g.dma_start(out=wmk_sb[:], in_=dd["wmkT"]).then_inc(s_ld.inc(16), 16)
            g.dma_start(out=bv_bf[:], in_=dd["bvR"]).then_inc(s_ld.inc(16), 16)
            g.wait_ge(s_ld.h, s_ld.n)

    with nc.Block() as blk:
        @blk.gpsimd
        def _(g):
            g.dma_start(out=orow_bf[:], in_=C.consts["ones_row"][:].bitcast(F32)) \
                .then_inc(s_ld.inc(16), 16)
            g.dma_start(out=ident_bf[:], in_=C.consts["ident"][:]).then_inc(s_ld.inc(16), 16)
            g.wait_ge(s_ld.h, s_ld.n)

        @blk.vector
        def _(v):
            v.memset(C.jC[:], 0.0)
            v.memset(C.jC[:], 0.0)
            v.tensor_scalar(ks_bd[:], C.jC[:, 0:16], 0.0, None, ALU.mult)

    es_xn = contextlib.ExitStack()
    XNb = [sb_scoped(es_xn, nc, f"{pfx}xnb{c}", [128, 1024], BF16) for c in range(NCH)]

    emit_ln(C, X, XNb, pfx + "ln")

    es_qk = contextlib.ExitStack()
    wr = [sb_scoped(es_qk, nc, f"{pfx}wr{i}", [128, 16 * 128], BF16) for i in range(4)]
    qh = sb_scoped(es_qk, nc, pfx + "qh", [128, 1024], BF16)
    el1 = sb_scoped(es_qk, nc, pfx + "el1", [128, 1024], F32)
    el2 = sb_scoped(es_qk, nc, pfx + "el2", [128, 1024], F32)

    for which, d_w, wmap, bias_sb, out_tiles in (
            ("q", dd["wqT"], wmq_sb, bq_sb, qe), ("k", dd["wkT"], wmk_sb, bk_sb, ke)):
        b_w = s_w.n
        b_pj, b_bh, b_qm = s_pj.n, s_bh.n, s_qm.n
        b_e1, b_e2, b_el = s_e1.n, s_e2.n, s_elu.n

        with nc.Block() as blk:
            @blk.gpsimd
            def _(g):
                for f in range(NCH):
                    if f >= 4:
                        g.wait_ge(s_pj.h, b_pj + f - 3)
                    g.dma_start(out=wr[f % 4][:].rearrange("p (c m) -> p c m", c=16),
                                in_=d_w[:, f * 128:(f + 1) * 128]
                                .rearrange("(c p) m -> p c m", p=128)) \
                        .then_inc(s_w.inc(16), 16)

            @blk.tensor
            def _(tn):
                for f in range(NCH):
                    j = f // 2
                    tn.wait_ge(s_w.h, b_w + 16 * (f + 1))
                    if f >= 2:
                        tn.wait_ge(s_bh.h, b_bh + f - 1)
                    for k in range(NCH):
                        for hh in range(2):
                            cl = slice(hh * 512, (hh + 1) * 512)
                            ins = tn.matmul(P[f % 2][:, cl],
                                            wr[f % 4][:, k * 128:(k + 1) * 128],
                                            XNb[k][:, cl], start=(k == 0), stop=(k == NCH - 1))
                    ins.then_inc(s_pj.inc(), 1)
                    tn.wait_ge(s_bh.h, b_bh + f + 1)
                    if f >= 4:
                        tn.wait_ge(s_elu.h, b_el + j - 1)
                    rows = slice(64 * (f % 2), 64 * (f % 2) + 64)
                    for hh in range(2):
                        cl = slice(hh * 512, (hh + 1) * 512)
                        ins = tn.matmul(P[2 + (j % 2)][rows, cl], wmap[:], qh[:, cl],
                                        start=True, stop=True)
                    ins.then_inc(s_qm.inc(), 1)

            @blk.scalar
            def _(sc):
                for f in range(NCH):
                    j = f // 2
                    sc.wait_ge(s_pj.h, b_pj + f + 1)
                    if f >= 1:
                        sc.wait_ge(s_qm.h, b_qm + f)
                    sc.activation(qh[:], P[f % 2][:], AF.Identity,
                                  bias=bias_sb[:, f:f + 1]).then_inc(s_bh.inc(), 1)
                    if f % 2 == 1:
                        sc.wait_ge(s_e1.h, b_e1 + j + 1)
                        if j >= 1:
                            sc.wait_ge(s_elu.h, b_el + j)
                        sc.activation(el2[:], el1[:], AF.Exp).then_inc(s_e2.inc(), 1)

            @blk.vector
            def _(v):
                s_hz = C.sem("hz")
                for j in range(8):
                    v.wait_ge(s_qm.h, b_qm + 2 * j + 2)
                    if j >= 1:
                        v.wait_ge(s_e2.h, b_e2 + j)
                    v.tensor_scalar(el1[:], P[2 + (j % 2)][:], 0.0, None, ALU.min) \
                        .then_inc(s_e1.inc(), 1)
                    v.wait_ge(s_e2.h, b_e2 + j + 1)
                    v.scalar_tensor_tensor(out_tiles[j][:], P[2 + (j % 2)][:], 0.0,
                                           el2[:], ALU.max, ALU.add) \
                        .then_inc(s_elu.inc(), 1)
                    if which == "k":
                        v.reduce_sum(kst[:, j:j + 1], out_tiles[j][:], axis=AX) \
                            .then_inc(s_hz.inc(), 1)
                v.memset(C.jC[:], 0.0)

    es_qk.close()

    with nc.Block() as blk:
        @blk.vector
        def _(v):
            v.wait_ge(C.sem("hz").h, C.sem("hz").n)
            for j in range(8):
                v.tensor_copy(ks_bd[0:64, 2 * j:2 * j + 1], kst[0:64, j:j + 1])
                v.tensor_copy(ks_bd[64:128, 2 * j + 1:2 * j + 2], kst[64:128, j:j + 1])
            v.memset(C.jC[:], 0.0)

    es_v = contextlib.ExitStack()
    vwb = [sb_scoped(es_v, nc, f"{pfx}vw{i}", [128, 16 * 512], BF16)
           for i in range(2)]
    vT = [sb_scoped(es_v, nc, f"{pfx}vT{t}", [128, D], BF16) for t in range(4)]
    emkT = [sb_scoped(es_v, nc, f"{pfx}emkT{t}", [128, 1024], BF16) for t in range(2)]

    for sblk in range(2):
        with nc.Block() as blk:
            @blk.gpsimd
            def _(g):
                for nq in range(4):
                    bb = sblk * 4 + nq
                    if bb >= 2:
                        g.wait_ge(s_vm.h, bA[s_vm] + 4 * (bb - 1))
                    for k in range(NCH):
                        g.dma_start(out=vwb[bb % 2][:, k * 512:(k + 1) * 512],
                                    in_=dd["wvT"][k * 128:(k + 1) * 128,
                                                  nq * 512:(nq + 1) * 512]) \
                            .then_inc(s_vw.inc(16), 16)

            @blk.tensor
            def _(tn):
                for nq in range(4):
                    bb = sblk * 4 + nq
                    tn.wait_ge(s_vw.h, bA[s_vw] + 16 * 16 * (bb + 1))
                    for tb in range(4):
                        tbg = sblk * 4 + tb
                        it = bb * 4 + tb
                        slot = it % 4
                        dst = P[slot // 2][:, (slot % 2) * 512:(slot % 2) * 512 + 512]
                        if it >= 4:
                            tn.wait_ge(s_ve.h, bA[s_ve] + it - 3)
                        for k in range(NCH):
                            tn.matmul(dst, XNb[k][:, tbg * 128:(tbg + 1) * 128],
                                      vwb[bb % 2][:, k * 512:(k + 1) * 512],
                                      start=(k == 0), stop=False)
                        ins = tn.matmul(dst, orow_bf[:],
                                        bv_bf[:, nq * 512:(nq + 1) * 512],
                                        start=False, stop=True)
                        ins.then_inc(s_vm.inc(), 1)

            @blk.scalar
            def _(sc):
                for nq in range(4):
                    for tb in range(4):
                        it = (sblk * 4 + nq) * 4 + tb
                        slot = it % 4
                        srcp = P[slot // 2][:, (slot % 2) * 512:(slot % 2) * 512 + 512]
                        sc.wait_ge(s_vm.h, bA[s_vm] + it + 1)
                        sc.activation(vT[tb][:, nq * 512:(nq + 1) * 512], srcp,
                                      AF.Identity).then_inc(s_ve.inc(), 1)

        with nc.Block() as blk:
            @blk.tensor
            def _(tn):
                for tb in range(4):
                    tbg = sblk * 4 + tb
                    for h in range(16):
                        it = (sblk * 4 + tb) * 16 + h
                        if it >= 2:
                            tn.wait_ge(s_te.h, bA[s_te] + it - 1)
                        j = h // 2
                        rows = slice(64 * (h % 2), 64 * (h % 2) + 64)
                        ib = 64 * (h % 2)
                        sl_c = (it % 2) * 1024
                        tn.transpose(P[2][:].bitcast(BF16)[:, sl_c:sl_c + 64],
                                     ke[j][rows, tbg * 128:(tbg + 1) * 128],
                                     ident_bf[ib:ib + 64, ib:ib + 64]).then_inc(s_tp.inc(), 1)
                    for h in range(16):
                        rows = slice(64 * (h % 2), 64 * (h % 2) + 64)
                        tn.wait_ge(s_te.h, bA[s_te] + (sblk * 4 + tb) * 16 + h + 1)
                        tn.matmul(P[3][rows, (h // 2) * 128:(h // 2) * 128 + 128],
                                  emkT[tb % 2][:, h * 64:(h + 1) * 64],
                                  vT[tb][:, h * 128:(h + 1) * 128],
                                  start=(tbg == 0), stop=(tbg == 7))

            @blk.vector
            def _(v):
                for tb in range(4):
                    for h in range(16):
                        it = (sblk * 4 + tb) * 16 + h
                        v.wait_ge(s_tp.h, bA[s_tp] + it + 1)
                        v.tensor_copy(emkT[tb % 2][:, h * 64:(h + 1) * 64],
                                      P[2][:].bitcast(BF16)[:, (it % 2) * 1024:(it % 2) * 1024 + 64]) \
                            .then_inc(s_te.inc(), 1)

    with nc.Block() as blk:
        @blk.vector
        def _(v):
            v.tensor_copy(kv_sb[:], P[3][:])
            v.memset(C.jC[:], 0.0)

    es_v.close()
    es_xn.close()

    with nc.Block() as blk:
        @blk.sync
        def _(sy):
            dmai(sy, ar_in.ap()[:, 0:1024], kv_sb[:].bitcast(F32), s_ar)
            dmai(sy, ar_in.ap()[:, 1024:1040], ks_bd[:].bitcast(F32), s_ar)

        @blk.gpsimd
        def _(g):
            g.wait_ge(s_ar.h, s_ar.n)
            g.collective_compute(
                "AllReduce", ALU.add,
                replica_groups=[[0, 1, 2, 3], [4, 5, 6, 7]],
                ins=[ar_in.ap()], outs=[ar_out.ap()],
            ).then_inc(s_cc.inc(), 1)
            g.wait_ge(s_cc.h, bA[s_cc] + 1)

    es_att = contextlib.ExitStack()
    att = [sb_scoped(es_att, nc, f"{pfx}att{h}", [128, 1024], BF16) for h in range(NCH)]
    at_sb = sb_scoped(es_att, nc, pfx + "at", [128, 1024], F32)
    kv_bf = sb_scoped(es_att, nc, pfx + "kvbf", [128, 1024], BF16)
    ksb_pad = sb_scoped(es_att, nc, pfx + "ksbp", [128, 32], BF16)

    with nc.Block() as blk:
        @blk.gpsimd
        def _(g):
            g.dma_start(out=kv_bf[:], in_=ar_out.ap()[:, 0:1024]) \
                .then_inc(s_ar.inc(16), 16)
            g.wait_ge(s_ar.h, s_ar.n)

        @blk.sync
        def _(sy):
            dmai(sy, ks_bd[:], ar_out.ap()[:, 1024:1040].bitcast(F32R), s_ar)
            sy.wait_ge(s_ar.h, s_ar.n)

    with nc.Block() as blk:
        @blk.vector
        def _(v):
            v.tensor_copy(ksb_pad[:].rearrange("p (h two) -> p h two", two=2)
                          [:, :, 0:1], ks_bd[:].bitcast(F32)
                          .rearrange("p (h o) -> p h o", o=1))
            v.memset(C.jC[:], 0.0)

    with nc.Block() as blk:
        @blk.tensor
        def _(tn):
            for h in range(16):
                if h >= 2:
                    tn.wait_ge(s_ad.h, bA[s_ad] + h - 1)
                for hh in range(2):
                    cl = slice(hh * 512, (hh + 1) * 512)
                    ins = tn.matmul(P[2 + (h % 2)][0:1, cl],
                                    ksb_pad[:, 2 * h:2 * h + 1],
                                    qe[h // 2][:, cl], start=True, stop=True)
                ins.then_inc(s_nm.inc(), 1)
                if h >= 2:
                    tn.wait_ge(s_ae.h, bA[s_ae] + h - 1)
                rows = slice(64 * (h % 2), 64 * (h % 2) + 64)
                for hh in range(2):
                    cl = slice(hh * 512, (hh + 1) * 512)
                    tn.matmul(P[h % 2][:, cl],
                              kv_bf[rows, (h // 2) * 128:(h // 2) * 128 + 128],
                              qe[h // 2][rows, cl], start=True, stop=True)
                tn.wait_ge(s_nr.h, bA[s_nr] + h + 1)
                for hh in range(2):
                    cl = slice(hh * 512, (hh + 1) * 512)
                    ins = tn.matmul(P[2 + (h % 2)][:, cl], C.consts["ones_row"][:],
                                    rcp[0:1, cl], start=True, stop=True)
                ins.then_inc(s_am.inc(), 1)

        @blk.vector
        def _(v):
            s_hz = C.sem("hz")
            for h in range(16):
                v.wait_ge(s_nm.h, bA[s_nm] + h + 1)
                v.tensor_scalar(rcp_t[:], P[2 + (h % 2)][0:1, :], 1e-6, None, ALU.add) \
                    .then_inc(s_hz.inc(), 1)
                v.wait_ge(s_hz.h, s_hz.n)
                v.reciprocal(rcp[:], rcp_t[:]).then_inc(s_nr.inc(), 1)
                v.wait_ge(s_ae.h, bA[s_ae] + h + 1)
                v.tensor_tensor(att[h][:], at_sb[:], P[2 + (h % 2)][:], ALU.mult) \
                    .then_inc(s_ad.inc(), 1)

        @blk.scalar
        def _(sc):
            for h in range(16):
                sc.wait_ge(s_am.h, bA[s_am] + h + 1)
                if h >= 1:
                    sc.wait_ge(s_ad.h, bA[s_ad] + h)
                sc.activation(at_sb[:], P[h % 2][:], AF.Identity).then_inc(s_ae.inc(), 1)

    wr2 = [sb_scoped(es_att, nc, f"{pfx}wr2_{i}", [128, 16 * 128], BF16) for i in range(4)]
    ot = sb_scoped(es_att, nc, pfx + "ot", [128, 1024], F32)
    with nc.Block() as blk:
        @blk.gpsimd
        def _(g):
            for ddx in range(NCH):
                if ddx >= 4:
                    g.wait_ge(s_om.h, bA[s_om] + ddx - 3)
                g.dma_start(out=wr2[ddx % 4][:].rearrange("p (c m) -> p c m", c=16),
                            in_=dd["woT"][:, ddx * 128:(ddx + 1) * 128]
                            .rearrange("(c p) m -> p c m", p=128)) \
                    .then_inc(s_w.inc(16), 16)

        @blk.tensor
        def _(tn):
            b_w = s_w.n - 16 * NCH
            for ddx in range(NCH):
                tn.wait_ge(s_w.h, b_w + 16 * (ddx + 1))
                if ddx >= 2:
                    tn.wait_ge(s_oe.h, bA[s_oe] + ddx - 1)
                for k in range(NCH):
                    for hh in range(2):
                        cl = slice(hh * 512, (hh + 1) * 512)
                        ins = tn.matmul(P[ddx % 2][:, cl], wr2[ddx % 4][:, k * 128:(k + 1) * 128],
                                        att[k][:, cl], start=(k == 0), stop=(k == NCH - 1))
                ins.then_inc(s_om.inc(), 1)

        @blk.scalar
        def _(sc):
            for ddx in range(NCH):
                sc.wait_ge(s_om.h, bA[s_om] + ddx + 1)
                if ddx >= 1:
                    sc.wait_ge(s_od.h, bA[s_od] + ddx)
                sc.activation(ot[:], P[ddx % 2][:], AF.Identity,
                              bias=bo_sb[:, ddx:ddx + 1]).then_inc(s_oe.inc(), 1)

        @blk.vector
        def _(v):
            for ddx in range(NCH):
                v.wait_ge(s_oe.h, bA[s_oe] + ddx + 1)
                v.tensor_tensor(X[ddx][:], X[ddx][:].bitcast(F32), ot[:], ALU.add) \
                    .then_inc(s_od.inc(), 1)

    es_att.close()
    es.close()
    return [(s_od, s_od.n)]



_EMIT = {"l0ffn": emit_l0ffn, "l0attn": emit_l0attn,
         "l1ffn": emit_l1ffn, "l1attn": emit_l1attn}
_SEQ = ["l0ffn", "l0attn", "l0ffn", "l1ffn", "l1attn", "l1ffn"]


def _declare(nc, keys):
    dd = {}
    for k in keys:
        shp, dt = _DRAM_SPECS[k]
        dd[k] = nc.dram_tensor(k, shp, dt, kind="ExternalInput").ap()
    return dd


def build_single(which):
    nc = bass.Bass("TRN2", target_bir_lowering=False)
    C = Ctx(nc)
    keys = ["xT"] + _NEED[which] + _CONST_KEYS
    dd = _declare(nc, keys)
    d_y = nc.dram_tensor("yT", [D, TOK], F32, kind="ExternalOutput").ap()
    X = [nc.alloc_sbuf_tensor(f"x{c}", [128, 1024], F32R) for c in range(NCH)]
    s_ld = C.sem("io_ld")
    s_st = C.sem("io_st")
    with nc.Block() as blk:
        @blk.sync
        def _(sy):
            emit_load_stream(C, X, dd["xT"], sy, s_ld)
            emit_load_consts(C, dd, sy, s_ld)
            sy.wait_ge(s_ld.h, s_ld.n)
    waits = _EMIT[which](C, X, dd, which[:2] + "_")
    emit_store_stream(C, X, d_y, s_st, waits=waits)
    return nc, keys


def build_fused():
    nc = bass.Bass("TRN2", target_bir_lowering=False)
    C = Ctx(nc)
    keys = sorted({k for ks in _NEED.values() for k in ks}) + ["xT"] + _CONST_KEYS
    dd = _declare(nc, keys)
    d_y = nc.dram_tensor("yT", [D, TOK], F32, kind="ExternalOutput").ap()
    X = [nc.alloc_sbuf_tensor(f"x{c}", [128, 1024], F32R) for c in range(NCH)]
    s_ld = C.sem("io_ld")
    s_st = C.sem("io_st")
    with nc.Block() as blk:
        @blk.sync
        def _(sy):
            emit_load_stream(C, X, dd["xT"], sy, s_ld)
            emit_load_consts(C, dd, sy, s_ld)
            sy.wait_ge(s_ld.h, s_ld.n)
    waits = []
    for idx, which in enumerate(_SEQ):
        with nc.named_scope(f"s{idx}_{which}"):
            waits = _EMIT[which](C, X, dd, f"s{idx}_")
    emit_store_stream(C, X, d_y, s_st, waits=waits)
    return nc, keys



_CACHE = {}


def kernel(**inputs):
    h = _prep(inputs)
    if "fused" not in _CACHE:
        _CACHE["fused"] = build_fused()
    nc, keys = _CACHE["fused"]
    in_maps = []
    for c in range(NCORE):
        m = {}
        for k in keys:
            if k == "xT":
                m[k] = np.ascontiguousarray(h["xT"][:, c * TOK:(c + 1) * TOK])
            elif k in _PERCORE:
                m[k] = h[k][c]
            else:
                m[k] = h[k]
        in_maps.append(m)
    res = run_bass_kernel_spmd(nc, in_maps, core_ids=list(range(NCORE)))
    full = np.concatenate([res.results[c]["yT"] for c in range(NCORE)], axis=1)
    return np.ascontiguousarray(full.T).reshape(2, 4096, D).astype(np.float32)


# version 08
# speedup vs baseline: 1.0101x; 1.0022x over previous
import contextlib
import numpy as np
import ml_dtypes
import concourse.bass as bass
import concourse.mybir as mybir
from concourse.bass_utils import run_bass_kernel_spmd

F32 = mybir.dt.float32
F32R = mybir.dt.float32r
BF16 = mybir.dt.bfloat16
AF = mybir.ActivationFunctionType
ALU = mybir.AluOpType
AX = mybir.AxisListType.X

NCORE = 8
TOK = 1024
D = 2048
NCH = 16
NTOK = 8192
DSL = 256
LN_EPS = 1e-5
NEXP = 9
NI = 11
_EL1 = None
_POOL_OFF = True
_DBG_L0 = False



def _host_consts():
    c = {}
    c["ones_col"] = np.ones((128, 1), np.float32)
    c["ones_row"] = np.ones((1, 128), np.float32)
    c["ident"] = np.eye(128, dtype=np.float32)
    p = np.arange(128)
    c["m16e"] = (p[None, :] // 16 == np.arange(8)[:, None]).astype(np.float32)
    oh8 = np.zeros((8, 8 * 128), np.float32)
    for e in range(8):
        oh8[e, e * 128:(e + 1) * 128] = 1.0
    c["oh8"] = oh8
    e_b = np.zeros((32, 128), np.float32)
    e_c = np.zeros((32, 128), np.float32)
    for k in range(16):
        e_b[k] = (p % 16 == k)
        e_c[k + 16] = (p % 16 == k)
    c["e_b"], c["e_c"] = e_b, e_c
    g = np.zeros((16, 128, 128), np.float32)
    for gl in range(16):
        for pp in range(128):
            g[gl, gl * 8 + pp // 16, pp] = 1.0
    c["g_pat"] = np.ascontiguousarray(g.reshape(16 * 128, 128))
    c["r_pat"] = (p[:, None] // 16 == np.arange(8)[None, :]).astype(np.float32)
    return c


def _prep(inputs):
    h = dict(_host_consts())
    f32 = np.float32
    x = np.asarray(inputs["L0_attn_x"], f32).reshape(NTOK, D)
    h["xT"] = np.ascontiguousarray(x.T)

    dw = np.asarray(inputs["L0_ffn_down_w"], f32)
    h["dnT"] = np.ascontiguousarray(dw.transpose(2, 0, 1).reshape(D, 128))
    uw = np.asarray(inputs["L0_ffn_up_w"], f32)
    h["upT"] = np.ascontiguousarray(uw.transpose(0, 2, 1).reshape(128, D))
    h["rtT"] = np.ascontiguousarray(np.asarray(inputs["L0_ffn_router_w"], f32).T)

    h["xprojT"] = np.ascontiguousarray(np.asarray(inputs["L0_attn_x_proj_w"], f32).T)
    dtT = np.ascontiguousarray(np.asarray(inputs["L0_attn_dt_proj_w"], f32).T)
    h["dtprojT_sl"] = [np.ascontiguousarray(dtT[:, c * DSL:(c + 1) * DSL]) for c in range(NCORE)]
    dtb = np.asarray(inputs["L0_attn_dt_proj_b"], f32)
    h["dtb_sl"] = [np.ascontiguousarray(dtb[c * DSL:(c + 1) * DSL].reshape(2, 128).T)
                   for c in range(NCORE)]
    h["D_sb"] = np.ascontiguousarray(np.asarray(inputs["L0_attn_D"], f32).reshape(16, 128).T)
    al = np.asarray(inputs["L0_attn_A_log"], f32)
    alp = np.zeros((NCORE, 128, 32), f32)
    for core in range(NCORE):
        for g in range(32):
            for pp in range(128):
                alp[core, pp, g] = al[core * DSL + g * 8 + pp // 16, pp % 16]
    h["alogp"] = alp

    bf = ml_dtypes.bfloat16
    for nm in ["wq", "wk", "wv", "wo"]:
        h[nm + "T"] = np.ascontiguousarray(
            np.asarray(inputs["L1_attn_" + nm], f32).T.astype(bf))
    h["wmqT"] = np.ascontiguousarray(
        np.asarray(inputs["L1_attn_wmap_q"], f32).T.astype(bf))
    h["wmkT"] = np.ascontiguousarray(
        np.asarray(inputs["L1_attn_wmap_k"], f32).T.astype(bf))
    for nm in ["bq", "bk", "bo"]:
        h[nm] = np.ascontiguousarray(
            np.asarray(inputs["L1_attn_" + nm], f32).reshape(16, 128).T)
    h["bvR"] = np.asarray(inputs["L1_attn_bv"], f32).reshape(1, D).astype(bf)

    w1 = np.asarray(inputs["L1_ffn_w1"]); w3 = np.asarray(inputs["L1_ffn_w3"])
    w2 = np.asarray(inputs["L1_ffn_w2"])
    h["w1T"] = np.ascontiguousarray(
        np.concatenate([w1, np.asarray(inputs["L1_ffn_sw1"])[None]], 0).transpose(0, 2, 1))
    h["w3T"] = np.ascontiguousarray(
        np.concatenate([w3, np.asarray(inputs["L1_ffn_sw3"])[None]], 0).transpose(0, 2, 1))
    h["w2T"] = np.ascontiguousarray(
        np.concatenate([w2, np.asarray(inputs["L1_ffn_sw2"])[None]], 0).transpose(0, 2, 1))
    h["gateT"] = np.ascontiguousarray(np.asarray(inputs["L1_ffn_gate_w"], f32).T)
    h["gate_b"] = np.asarray(inputs["L1_ffn_gate_b"], f32).reshape(1, 8)
    return h


_DRAM_SPECS = {
    "xT": ([D, TOK], F32), "dnT": ([D, 128], F32), "upT": ([128, D], F32),
    "rtT": ([D, 8], F32), "m16e": ([8, 128], F32),
    "xprojT": ([D, 160], F32), "dtprojT_sl": ([128, DSL], F32),
    "dtb_sl": ([128, 2], F32), "alogp": ([128, 32], F32), "D_sb": ([128, 16], F32),
    "g_pat": ([16 * 128, 128], F32), "e_b": ([32, 128], F32), "e_c": ([32, 128], F32),
    "r_pat": ([128, 8], F32),
    "wqT": ([D, D], BF16), "wkT": ([D, D], BF16), "wvT": ([D, D], BF16),
    "woT": ([D, D], BF16), "wmqT": ([128, 64], BF16), "wmkT": ([128, 64], BF16),
    "bq": ([128, 16], F32), "bk": ([128, 16], F32), "bo": ([128, 16], F32),
    "bvR": ([1, D], BF16),
    "w1T": ([NEXP, D, 1408], BF16), "w3T": ([NEXP, D, 1408], BF16),
    "w2T": ([NEXP, 1408, D], BF16), "gateT": ([D, 8], F32), "gate_b": ([1, 8], F32),
    "oh8": ([8, 8 * 128], F32),
    "ones_col": ([128, 1], F32), "ones_row": ([1, 128], F32), "ident": ([128, 128], F32),
}
_PERCORE = {"dtprojT_sl", "dtb_sl", "alogp"}

_NEED = {
    "l0ffn": ["dnT", "upT", "rtT", "m16e"],
    "l0attn": ["xprojT", "dtprojT_sl", "dtb_sl", "alogp", "D_sb", "g_pat",
               "e_b", "e_c", "r_pat"],
    "l1ffn": ["w1T", "w3T", "w2T", "gateT", "gate_b", "oh8"],
    "l1attn": ["wqT", "wkT", "wvT", "woT", "wmqT", "wmkT", "bq", "bk", "bo", "bvR"],
}
_CONST_KEYS = ["ones_col", "ones_row", "ident"]



class Sem:
    def __init__(self, nc, name):
        self.h = nc.semaphore(name).__enter__()
        self.n = 0

    def inc(self, k=1):
        self.n += k
        return self.h


class Ctx:
    def __init__(self, nc):
        self.nc = nc
        self._lp = nc.allow_low_precision(reason="fp32r matmul operand tiles")
        self._lp.__enter__()
        self.P = [nc.alloc_psum_tensor(f"psP{i}", [128, 1024], F32) for i in range(4)]
        self.jC = nc.alloc_sbuf_tensor("ctx_jnk", [128, 64], F32)
        self.sems = {}

    def sem(self, name):
        if name not in self.sems:
            self.sems[name] = Sem(self.nc, name)
        return self.sems[name]


def sb_scoped(es, nc, name, shape, dt):
    return es.enter_context(nc.sbuf_tensor(name, shape, dt))


def dmai(eng, out, in_, s):
    eng.dma_start(out=out, in_=in_).then_inc(s.inc(16), 16)


def emit_load_consts(C, dd, sy, s_ld):
    nc = C.nc
    C.consts = {}
    C.consts["ones_col"] = nc.alloc_sbuf_tensor("c_onescol", [128, 1], F32R)
    dmai(sy, C.consts["ones_col"][:], dd["ones_col"].bitcast(F32R), s_ld)
    C.consts["ones_row"] = nc.alloc_sbuf_tensor("c_onesrow", [1, 128], F32R)
    dmai(sy, C.consts["ones_row"][:], dd["ones_row"].bitcast(F32R), s_ld)
    C.consts["ident"] = nc.alloc_sbuf_tensor("c_ident", [128, 128], F32)
    dmai(sy, C.consts["ident"][:], dd["ident"], s_ld)


def emit_ln(C, X, OUT, pfx):
    nc = C.nc
    P = C.P
    consts = C.consts
    s_sq = C.sem("ln_sq")
    s_sqf = C.sem("ln_sqf")
    b_sq, b_sqf = s_sq.n, s_sqf.n
    les = contextlib.ExitStack()
    sq = [sb_scoped(les, nc, f"{pfx}_sq{i}", [128, 1024], F32R) for i in range(2)]
    m_sb = sb_scoped(les, nc, f"{pfx}_m", [1, 1024], F32R)
    r_sb = sb_scoped(les, nc, f"{pfx}_r", [1, 1024], F32R)
    t_sb = sb_scoped(les, nc, f"{pfx}_t", [1, 1024], F32)
    v_sb = sb_scoped(les, nc, f"{pfx}_v", [1, 1024], F32)
    sd_sb = sb_scoped(les, nc, f"{pfx}_sd", [1, 1024], F32)
    nt = sb_scoped(les, nc, f"{pfx}_nt", [128, 1024], F32)

    with nc.Block() as blk:
        @blk.scalar
        def _(sc):
            for c in range(NCH):
                if c >= 2:
                    sc.wait_ge(s_sqf.h, b_sqf + c - 1)
                sc.activation(sq[c % 2][:], X[c][:].bitcast(F32), AF.Square) \
                    .then_inc(s_sq.inc(), 1)

        @blk.tensor
        def _(tn):
            for c in range(NCH):
                for hh in range(2):
                    cl = slice(hh * 512, (hh + 1) * 512)
                    tn.matmul(P[2][0:1, cl], consts["ones_col"][:], X[c][:, cl],
                              start=(c == 0), stop=(c == NCH - 1))
                tn.wait_ge(s_sq.h, b_sq + c + 1)
                for hh in range(2):
                    cl = slice(hh * 512, (hh + 1) * 512)
                    ins = tn.matmul(P[3][0:1, cl], consts["ones_col"][:], sq[c % 2][:, cl],
                                    start=(c == 0), stop=(c == NCH - 1))
                ins.then_inc(s_sqf.inc(), 1)

    with nc.Block() as blk:
        @blk.vector
        def _(v):
            v.tensor_scalar(m_sb[:], P[2][0:1, :], 1.0 / D, None, ALU.mult)
            v.tensor_scalar(v_sb[:], P[3][0:1, :], 1.0 / D, None, ALU.mult)
            v.tensor_tensor(t_sb[:], m_sb[:].bitcast(F32), m_sb[:].bitcast(F32), ALU.mult)
            v.tensor_tensor(v_sb[:], v_sb[:], t_sb[:], ALU.subtract)
            v.tensor_scalar(v_sb[:], v_sb[:], LN_EPS, None, ALU.add)
            v.memset(C.jC[:], 0.0)

    with nc.Block() as blk:
        @blk.scalar
        def _(sc):
            sc.activation(sd_sb[:], v_sb[:], AF.Sqrt)
            sc.memzero(C.jC[:])

    with nc.Block() as blk:
        @blk.vector
        def _(v):
            v.reciprocal(r_sb[:], sd_sb[:])
            v.memset(C.jC[:], 0.0)
            v.memset(C.jC[:], 0.0)

    with nc.Block() as blk:
        @blk.tensor
        def _(tn):
            for hh in range(2):
                cl = slice(hh * 512, (hh + 1) * 512)
                tn.matmul(P[2][:, cl], consts["ones_row"][:], m_sb[0:1, cl],
                          start=True, stop=True)
                tn.matmul(P[3][:, cl], consts["ones_row"][:], r_sb[0:1, cl],
                          start=True, stop=True)

    with nc.Block() as blk:
        @blk.vector
        def _(v):
            for c in range(NCH):
                v.tensor_tensor(nt[:], X[c][:].bitcast(F32), P[2][:], ALU.subtract)
                v.tensor_tensor(OUT[c][:], nt[:], P[3][:], ALU.mult)
            v.memset(C.jC[:], 0.0)
    les.close()


def emit_load_stream(C, X, d_x, sy, s_ld):
    xv = d_x.rearrange("(c p) t -> c p t", p=128)
    for c in range(NCH):
        dmai(sy, X[c][:], xv[c].bitcast(F32R), s_ld)


def emit_store_stream(C, X, d_y, s_st, waits=()):
    nc = C.nc
    with nc.Block() as blk:
        @blk.sync
        def _(sy):
            for s, val in waits:
                sy.wait_ge(s.h, val)
            yv = d_y.rearrange("(c p) t -> c p t", p=128)
            for c in range(NCH):
                dmai(sy, yv[c], X[c][:].bitcast(F32), s_st)
            sy.wait_ge(s_st.h, s_st.n)



def emit_l0ffn(C, X, dd, pfx):
    nc = C.nc
    P = C.P
    es = contextlib.ExitStack()
    sbs = lambda n, shp, dt: sb_scoped(es, nc, pfx + n, shp, dt)

    XN = [sbs(f"xn{c}", [128, 1024], F32R) for c in range(NCH)]
    dn_sb = sbs("dn", [128, 16 * 128], F32R)
    up_sb = sbs("up", [128, D], F32R)
    rt_sb = sbs("rt", [128, 16 * 8], F32R)
    m16_sb = sbs("m16", [8, 128], F32R)
    a_sb = sbs("a", [128, 1024], F32R)
    ag_sb = sbs("ag", [128, 1024], F32R)
    lg_sb = sbs("lg", [8, 1024], F32)
    sct = sbs("sct", [128, 64], F32)
    et = sbs("et", [128, 64], F32)
    ex_t = sbs("ext", [128, 64], F32)
    mx8 = sbs("mx8", [128, 8], F32)
    sm8 = sbs("sm8", [128, 8], F32)
    r8 = sbs("r8", [128, 8], F32)
    jnk = sbs("jnk", [128, 8], F32)
    m1 = sbs("m1", [128, 64], F32)
    m2 = sbs("m2", [128, 64], F32)
    sup = sbs("sup", [128, 64], F32)
    gate_t = sbs("gatet", [128, 64], F32)
    gate_e = sbs("gatee", [8, 1024], F32R)

    s_ld = C.sem("f0_ld")
    s_up = C.sem("f0_up")
    s_upf = C.sem("f0_upf")
    b_up, b_upf = s_up.n, s_upf.n

    with nc.Block() as blk:
        @blk.sync
        def _(sy):
            dmai(sy, dn_sb[:].rearrange("p (c m) -> p c m", c=16),
                 dd["dnT"].rearrange("(c p) m -> p c m", p=128).bitcast(F32R), s_ld)
            dmai(sy, up_sb[:], dd["upT"].bitcast(F32R), s_ld)
            dmai(sy, rt_sb[:].rearrange("p (c m) -> p c m", c=16),
                 dd["rtT"].rearrange("(c p) m -> p c m", p=128).bitcast(F32R), s_ld)
            dmai(sy, m16_sb[:], dd["m16e"].bitcast(F32R), s_ld)
            sy.wait_ge(s_ld.h, s_ld.n)

    emit_ln(C, X, XN, pfx + "ln")

    with nc.Block() as blk:
        @blk.tensor
        def _(tn):
            for c in range(NCH):
                for hh in range(2):
                    cl = slice(hh * 512, (hh + 1) * 512)
                    tn.matmul(P[0][0:8, cl], rt_sb[:, c * 8:(c + 1) * 8], XN[c][:, cl],
                              start=(c == 0), stop=(c == NCH - 1))
                    tn.matmul(P[1][:, cl], dn_sb[:, c * 128:(c + 1) * 128], XN[c][:, cl],
                              start=(c == 0), stop=(c == NCH - 1))

    with nc.Block() as blk:
        @blk.scalar
        def _(sc):
            sc.activation(a_sb[:], P[1][:], AF.Silu)
            sc.memzero(C.jC[:])

        @blk.vector
        def _(v):
            v.tensor_copy(lg_sb[:], P[0][0:8, :])
            v.memset(C.jC[:], 0.0)

    with nc.Block() as blk:
        @blk.tensor
        def _(tn):
            for t8 in range(8):
                tn.transpose(P[2][:, t8 * 8:(t8 + 1) * 8],
                             lg_sb[0:8, t8 * 128:(t8 + 1) * 128],
                             C.consts["ident"][0:8, 0:8])

    S8 = [slice(t8 * 8, (t8 + 1) * 8) for t8 in range(8)]
    with nc.Block() as blk:
        @blk.vector
        def _(v):
            s_hz = C.sem("hz")
            v.tensor_copy(sct[:], P[2][:, 0:64])
            for t8 in range(8):
                v.reduce_max(mx8[:, t8:t8 + 1], sct[:, S8[t8]], axis=AX) \
                    .then_inc(s_hz.inc(), 1)
            v.wait_ge(s_hz.h, s_hz.n)
            for t8 in range(8):
                v.tensor_scalar(et[:, S8[t8]], sct[:, S8[t8]], mx8[:, t8:t8 + 1],
                                None, ALU.subtract)
            v.memset(jnk[:], 0.0)

    with nc.Block() as blk:
        @blk.scalar
        def _(sc):
            sc.activation(ex_t[:], et[:], AF.Exp)
            sc.memzero(C.jC[:])

    with nc.Block() as blk:
        @blk.vector
        def _(v):
            s_hz = C.sem("hz")
            for t8 in range(8):
                v.reduce_sum(sm8[:, t8:t8 + 1], ex_t[:, S8[t8]], axis=AX) \
                    .then_inc(s_hz.inc(), 1)
            v.wait_ge(s_hz.h, s_hz.n)
            v.reciprocal(r8[:], sm8[:]).then_inc(s_hz.inc(), 1)
            v.wait_ge(s_hz.h, s_hz.n)
            for t8 in range(8):
                v.tensor_scalar(sct[:, S8[t8]], ex_t[:, S8[t8]], r8[:, t8:t8 + 1],
                                None, ALU.mult)
            for t8 in range(8):
                v.reduce_max(mx8[:, t8:t8 + 1], sct[:, S8[t8]], axis=AX) \
                    .then_inc(s_hz.inc(), 1)
            v.wait_ge(s_hz.h, s_hz.n)
            for t8 in range(8):
                v.tensor_scalar(m1[:, S8[t8]], sct[:, S8[t8]], mx8[:, t8:t8 + 1],
                                None, ALU.is_ge)
            for t8 in range(8):
                v.scalar_tensor_tensor(sup[:, S8[t8]], m1[:, S8[t8]], -1e30,
                                       sct[:, S8[t8]], ALU.mult, ALU.add)
            for t8 in range(8):
                v.reduce_max(mx8[:, t8:t8 + 1], sup[:, S8[t8]], axis=AX) \
                    .then_inc(s_hz.inc(), 1)
            v.wait_ge(s_hz.h, s_hz.n)
            for t8 in range(8):
                v.tensor_scalar(m2[:, S8[t8]], sup[:, S8[t8]], mx8[:, t8:t8 + 1],
                                None, ALU.is_ge)
            for t8 in range(8):
                v.tensor_tensor(m1[:, S8[t8]], m1[:, S8[t8]], m2[:, S8[t8]], ALU.add)
            for t8 in range(8):
                v.tensor_tensor(gate_t[:, S8[t8]], sct[:, S8[t8]], m1[:, S8[t8]], ALU.mult)
            v.memset(jnk[:], 0.0)

    with nc.Block() as blk:
        @blk.tensor
        def _(tn):
            for t8 in range(8):
                tn.transpose(P[2][0:8, t8 * 128:(t8 + 1) * 128],
                             gate_t[:, t8 * 8:(t8 + 1) * 8],
                             C.consts["ident"][:, 0:128])

    with nc.Block() as blk:
        @blk.vector
        def _(v):
            v.tensor_copy(gate_e[:], P[2][0:8, :])
            v.memset(C.jC[:], 0.0)

    with nc.Block() as blk:
        @blk.tensor
        def _(tn):
            for hh in range(2):
                cl = slice(hh * 512, (hh + 1) * 512)
                tn.matmul(P[3][:, cl], m16_sb[:], gate_e[0:8, cl], start=True, stop=True)

    with nc.Block() as blk:
        @blk.vector
        def _(v):
            v.tensor_tensor(ag_sb[:], a_sb[:].bitcast(F32), P[3][:], ALU.mult)
            v.memset(C.jC[:], 0.0)

    with nc.Block() as blk:
        @blk.tensor
        def _(tn):
            for m in range(NCH):
                if m >= 2:
                    tn.wait_ge(s_upf.h, b_upf + m - 1)
                for hh in range(2):
                    cl = slice(hh * 512, (hh + 1) * 512)
                    ins = tn.matmul(P[m % 2][:, cl], up_sb[:, m * 128:(m + 1) * 128],
                                    ag_sb[:, cl], start=True, stop=True)
                ins.then_inc(s_up.inc(), 1)

        @blk.vector
        def _(v):
            for m in range(NCH):
                v.wait_ge(s_up.h, b_up + m + 1)
                v.scalar_tensor_tensor(X[m][:], P[m % 2][:], 0.5, X[m][:].bitcast(F32),
                                       ALU.mult, ALU.add).then_inc(s_upf.inc(), 1)

    es.close()
    return [(s_upf, s_upf.n)]



def emit_l0attn(C, X, dd, pfx):
    nc = C.nc
    P = C.P
    es0 = contextlib.ExitStack()
    sbs = lambda n, shp, dt: sb_scoped(es0, nc, pfx + n, shp, dt)

    agx_in = nc.dram_tensor(pfx + "agx_in", [D, TOK], F32)
    agx_out = nc.dram_tensor(pfx + "agx_out", [NCORE * D, TOK], F32, addr_space="Shared")
    agd_in = nc.dram_tensor(pfx + "agd_in", [160, TOK], F32)
    agd_out = nc.dram_tensor(pfx + "agd_out", [NCORE * 160, TOK], F32, addr_space="Shared")
    y_mine = nc.dram_tensor(pfx + "y_mine", [8 * DSL, 1024], F32)
    y_full = nc.dram_tensor(pfx + "y_full", [8 * NCORE * DSL, 1024], F32,
                            addr_space="Shared")

    gp_sb = sbs("gp", [128, 16 * 128], F32R)
    eb_sb = sbs("eb", [32, 128], F32R)
    ec_sb = sbs("ec", [32, 128], F32R)
    rp_sb = sbs("rp", [128, 8], F32R)
    dtp_sb = sbs("dtp", [128, DSL], F32R)
    dtb_sb = sbs("dtb", [128, 2], F32)
    alog_sb = sbs("alog", [128, 32], F32)
    apat_sb = sbs("apat", [128, 32], F32)
    D_sb = sbs("Dsb", [128, 16], F32)
    carry = sbs("carry", [128, 32], F32)

    s_ld = C.sem("m_ld")
    s_ag = C.sem("m_ag")
    s_cc = C.sem("m_cc")
    s_u = C.sem("m_u"); s_uf = C.sem("m_uf")
    s_dr = C.sem("m_dr")
    s_gdp = C.sem("m_gdp"); s_gda = C.sem("m_gda")
    s_gup = C.sem("m_gup"); s_gbu = C.sem("m_gbu")
    s_gsc = C.sem("m_gsc"); s_gt0 = C.sem("m_gt0"); s_gt1 = C.sem("m_gt1")
    s_sc1 = C.sem("m_sc1")
    s_gym = C.sem("m_gym"); s_gye = C.sem("m_gye"); s_gyd = C.sem("m_gyd")
    s_yl = C.sem("m_yl")
    s_fin = C.sem("m_fin")
    b = {s: s.n for s in (s_u, s_uf, s_gdp, s_gda, s_gup, s_gbu, s_gsc, s_gt0,
                          s_gt1, s_sc1, s_gym, s_gye, s_gyd, s_yl, s_cc, s_fin)}

    with nc.Block() as blk:
        @blk.sync
        def _(sy):
            dmai(sy, gp_sb[:].rearrange("d (g p) -> d g p", g=16),
                 dd["g_pat"].rearrange("(g d) p -> d g p", d=128).bitcast(F32R), s_ld)
            dmai(sy, eb_sb[:], dd["e_b"].bitcast(F32R), s_ld)
            dmai(sy, ec_sb[:], dd["e_c"].bitcast(F32R), s_ld)
            dmai(sy, rp_sb[:], dd["r_pat"].bitcast(F32R), s_ld)
            dmai(sy, dtp_sb[:], dd["dtprojT_sl"].bitcast(F32R), s_ld)
            dmai(sy, dtb_sb[:], dd["dtb_sl"], s_ld)
            dmai(sy, alog_sb[:], dd["alogp"], s_ld)
            dmai(sy, D_sb[:], dd["D_sb"], s_ld)
            sy.wait_ge(s_ld.h, s_ld.n)

    with nc.Block() as blk:
        @blk.scalar
        def _(sc):
            sc.activation(apat_sb[:], alog_sb[:], AF.Exp)
            sc.memzero(C.jC[:])

    with nc.Block() as blk:
        @blk.vector
        def _(v):
            v.tensor_scalar(apat_sb[:], apat_sb[:], -1.0, None, ALU.mult)
            v.memset(C.jC[:], 0.0)

    es1 = contextlib.ExitStack()
    XN = [sb_scoped(es1, nc, f"{pfx}xn{c}", [128, 1024], F32R) for c in range(NCH)]
    emit_ln(C, X, XN, pfx + "ln")

    xpj_sb = sb_scoped(es1, nc, pfx + "xpj", [128, 16 * 160], F32R)
    dr_sb = sb_scoped(es1, nc, pfx + "drs", [128, 1024], F32R)
    bc_sb = sb_scoped(es1, nc, pfx + "bcs", [32, 1024], F32R)

    with nc.Block() as blk:
        @blk.sync
        def _(sy):
            dmai(sy, xpj_sb[:].rearrange("p (c m) -> p c m", c=16),
                 dd["xprojT"].rearrange("(c p) m -> p c m", p=128).bitcast(F32R), s_ld)
            sy.wait_ge(s_ld.h, s_ld.n)

    with nc.Block() as blk:
        @blk.tensor
        def _(tn):
            for c in range(NCH):
                for hh in range(2):
                    cl = slice(hh * 512, (hh + 1) * 512)
                    tn.matmul(P[0][:, cl], xpj_sb[:, c * 160:c * 160 + 128],
                              XN[c][:, cl], start=(c == 0), stop=(c == NCH - 1))
                    tn.matmul(P[1][0:32, cl], xpj_sb[:, c * 160 + 128:c * 160 + 160],
                              XN[c][:, cl], start=(c == 0), stop=(c == NCH - 1))


    with nc.Block() as blk:
        @blk.vector
        def _(v):
            v.tensor_copy(dr_sb[:], P[0][:])
            v.tensor_copy(bc_sb[:], P[1][0:32, :])
            v.memset(C.jC[:], 0.0)

    s_agd = C.sem("m_agd")
    with nc.Block() as blk:
        @blk.sync
        def _(sy):
            xv = agx_in.ap().rearrange("(c p) t -> c p t", p=128)
            for c in range(NCH):
                dmai(sy, xv[c], XN[c][:].bitcast(F32), s_ag)
            dmai(sy, agd_in.ap()[0:128, :], dr_sb[:].bitcast(F32), s_agd)
            dmai(sy, agd_in.ap()[128:160, :], bc_sb[:].bitcast(F32), s_agd)

        @blk.gpsimd
        def _(g):
            g.wait_ge(s_ag.h, s_ag.n)
            g.collective_compute("AllGather", ALU.bypass,
                                 replica_groups=[list(range(NCORE))],
                                 ins=[agx_in.ap()], outs=[agx_out.ap()]) \
                .then_inc(s_cc.inc(), 1)
            g.wait_ge(s_agd.h, s_agd.n)
            g.collective_compute("AllGather", ALU.bypass,
                                 replica_groups=[list(range(NCORE))],
                                 ins=[agd_in.ap()], outs=[agd_out.ap()]) \
                .then_inc(s_cc.inc(), 1)
            g.wait_ge(s_cc.h, b[s_cc] + 2)

    es1.close()

    es2 = contextlib.ExitStack()
    sbt = lambda n, shp, dt: sb_scoped(es2, nc, pfx + n, shp, dt)
    u_t = [[sbt(f"u{s}_{dh}", [128, 1024], F32) for dh in range(2)] for s in range(2)]
    drt = [sbt(f"drt{s}", [128, 1024], F32R) for s in range(2)]
    bct = [sbt(f"bct{s}", [32, 1024], F32R) for s in range(2)]
    zt = [sbt(f"zt{dh}", [128, 1024], F32) for dh in range(2)]
    ab2 = [sbt(f"ab2{dh}", [128, 1024], F32) for dh in range(2)]
    el = [sbt(f"el{dh}", [128, 1024], F32) for dh in range(2)]
    delta = [sbt(f"dl{dh}", [128, 1024], F32R) for dh in range(2)]
    du = [sbt(f"du{dh}", [128, 1024], F32R) for dh in range(2)]
    Bp = sbt("Bp", [128, 1024], F32)
    Cp = sbt("Cp", [128, 1024], F32)
    dA = [sbt(f"dA{s}", [128, 1024], F32) for s in range(2)]
    dBu = [sbt(f"dBu{s}", [128, 1024], F32) for s in range(2)]
    h_t = [sbt(f"h{s}", [128, 1024], F32) for s in range(3)]
    tmp = [sbt(f"tmp{s}", [128, 1024], F32R) for s in range(3)]
    y_sb = [sbt(f"ysb{s}", [8, 1024], F32) for s in range(2)]

    NTC = 8
    for tc in range(NTC):
        for arm in nc.switch_core_id(engines=[nc.sync], n=128):
            try:
                rank = int(arm.logical) % NCORE
            except Exception:
                rank = 0
            if tc >= 2:
                nc.sync.wait_ge(s_uf.h, b[s_uf] + tc - 1)
            for dh in range(2):
                nc.sync.dma_start(
                    out=u_t[tc % 2][dh][:],
                    in_=agx_out.ap()[tc * D + rank * DSL + dh * 128:
                                     tc * D + rank * DSL + (dh + 1) * 128, :]) \
                    .then_inc(s_u.h, 16)
        s_u.n += 32

        with nc.Block() as blk:
            @blk.sync
            def _(sy):
                dmai(sy, drt[tc % 2][:], agd_out.ap()[tc * 160:tc * 160 + 128, :]
                     .bitcast(F32R), s_dr)
                dmai(sy, bct[tc % 2][:], agd_out.ap()[tc * 160 + 128:tc * 160 + 160, :]
                     .bitcast(F32R), s_dr)
                sy.wait_ge(s_dr.h, s_dr.n)

        with nc.Block() as blk:
            @blk.tensor
            def _(tn):
                for dh in range(2):
                    for hh in range(2):
                        cl = slice(hh * 512, (hh + 1) * 512)
                        tn.matmul(P[dh][:, cl], dtp_sb[:, dh * 128:(dh + 1) * 128],
                                  drt[tc % 2][:, cl], start=True, stop=True)
                for hh in range(2):
                    cl = slice(hh * 512, (hh + 1) * 512)
                    tn.matmul(P[2][:, cl], eb_sb[:], bct[tc % 2][0:32, cl],
                              start=True, stop=True)
                    tn.matmul(P[3][:, cl], ec_sb[:], bct[tc % 2][0:32, cl],
                              start=True, stop=True)

        with nc.Block() as blk:
            @blk.scalar
            def _(sc):
                for dh in range(2):
                    sc.activation(zt[dh][:], P[dh][:], AF.Identity,
                                  bias=dtb_sb[:, dh:dh + 1])
                sc.memzero(C.jC[:])

        with nc.Block() as blk:
            @blk.vector
            def _(v):
                for dh in range(2):
                    v.scalar_tensor_tensor(ab2[dh][:], zt[dh][:], -1.0, zt[dh][:],
                                           ALU.mult, ALU.max)
                v.tensor_copy(Bp[:], P[2][:])
                v.tensor_copy(Cp[:], P[3][:])
                v.memset(C.jC[:], 0.0)

        with nc.Block() as blk:
            @blk.scalar
            def _(sc):
                for dh in range(2):
                    sc.activation(el[dh][:], ab2[dh][:], AF.Exp, scale=-1.0)
                for dh in range(2):
                    sc.activation(ab2[dh][:], el[dh][:], AF.Ln, bias=1.0)
                sc.memzero(C.jC[:])

        with nc.Block() as blk:
            @blk.vector
            def _(v):
                v.wait_ge(s_u.h, b[s_u] + 32 * (tc + 1))
                for dh in range(2):
                    v.scalar_tensor_tensor(delta[dh][:], zt[dh][:], 0.0, ab2[dh][:],
                                           ALU.max, ALU.add)
                    ins = v.tensor_tensor(du[dh][:], delta[dh][:].bitcast(F32),
                                          u_t[tc % 2][dh][:], ALU.mult)
                    if dh == 1:
                        ins.then_inc(s_uf.inc(), 1)
                v.memset(C.jC[:], 0.0)

        with nc.Block() as blk:
            @blk.tensor
            def _(tn):
                for g in range(33):
                    gq = tc * 32 + g
                    if g < 32:
                        dh = g // 16
                        gl = g % 16
                        for hh in range(2):
                            q2 = 2 * gq + hh
                            cl = slice(hh * 512, (hh + 1) * 512)
                            if q2 >= 2:
                                tn.wait_ge(s_gda.h, b[s_gda] + q2 - 1)
                            tn.matmul(P[0][:, cl], gp_sb[:, gl * 128:(gl + 1) * 128],
                                      delta[dh][:, cl], start=True, stop=True) \
                                .then_inc(s_gdp.inc(), 1)
                            if q2 >= 2:
                                tn.wait_ge(s_gbu.h, b[s_gbu] + q2 - 1)
                            tn.matmul(P[1][:, cl], gp_sb[:, gl * 128:(gl + 1) * 128],
                                      du[dh][:, cl], start=True, stop=True) \
                                .then_inc(s_gup.inc(), 1)
                    if g >= 1:
                        gy = gq - 1
                        if gy >= 2:
                            tn.wait_ge(s_gye.h, b[s_gye] + gy - 1)
                        for hh in range(2):
                            cl = slice(hh * 512, (hh + 1) * 512)
                            sh = s_gt0 if hh == 0 else s_gt1
                            tn.wait_ge(sh.h, b[sh] + gy + 1)
                            ins = tn.matmul(P[2 + (gy % 2)][0:8, cl], rp_sb[:],
                                            tmp[gy % 3][:, cl], start=True, stop=True)
                        ins.then_inc(s_gym.inc(), 1)

            @blk.scalar
            def _(sc):
                for g in range(33):
                    gq = tc * 32 + g
                    if g < 32:
                        for hh in range(2):
                            q2 = 2 * gq + hh
                            cl = slice(hh * 512, (hh + 1) * 512)
                            sc.wait_ge(s_gdp.h, b[s_gdp] + q2 + 1)
                            if gq >= 2:
                                sc.wait_ge(s_sc1.h, b[s_sc1] + gq - 1)
                            sc.activation(dA[gq % 2][:, cl], P[0][:, cl], AF.Exp,
                                          scale=apat_sb[:, g:g + 1]) \
                                .then_inc(s_gda.inc(), 1)
                    if g >= 1:
                        gy = gq - 1
                        sc.wait_ge(s_gym.h, b[s_gym] + gy + 1)
                        if gy >= 2:
                            sc.wait_ge(s_gyd.h, b[s_gyd] + 16 * (gy - 1))
                        sc.activation(y_sb[gy % 2][:], P[2 + (gy % 2)][0:8, :],
                                      AF.Identity).then_inc(s_gye.inc(), 1)

            @blk.vector
            def _(v):
                for g in range(33):
                    gq = tc * 32 + g
                    if g < 32:
                        for hh in range(2):
                            q2 = 2 * gq + hh
                            cl = slice(hh * 512, (hh + 1) * 512)
                            v.wait_ge(s_gup.h, b[s_gup] + q2 + 1)
                            if gq >= 2:
                                v.wait_ge(s_sc1.h, b[s_sc1] + gq - 1)
                            v.tensor_tensor(dBu[gq % 2][:, cl], P[1][:, cl], Bp[:, cl],
                                            ALU.mult).then_inc(s_gbu.inc(), 1)
                    if g >= 1:
                        gp_ = gq - 1
                        v.wait_ge(s_sc1.h, b[s_sc1] + gp_ + 1)
                        if gp_ >= 3:
                            v.wait_ge(s_gym.h, b[s_gym] + gp_ - 2)
                        v.tensor_tensor(tmp[gp_ % 3][:, 0:512], h_t[gp_ % 3][:, 0:512],
                                        Cp[:, 0:512], ALU.mult).then_inc(s_gt0.inc(), 1)
                    if g < 32:
                        v.wait_ge(s_gda.h, b[s_gda] + 2 * gq + 2)
                        if gq >= 3:
                            v.wait_ge(s_gt1.h, b[s_gt1] + gq - 2)
                        if tc not in (0, 4):
                            v.wait_ge(s_gsc.h, b[s_gsc] + gq - 31)
                        ini = 0.0 if tc in (0, 4) else carry[:, g:g + 1]
                        v.tensor_tensor_scan(h_t[gq % 3][:], dA[gq % 2][:],
                                             dBu[gq % 2][:], ini, ALU.mult, ALU.add) \
                            .then_inc(s_sc1.inc(), 1)

            @blk.gpsimd
            def _(gp):
                for g in range(32):
                    gq = tc * 32 + g
                    gp.wait_ge(s_sc1.h, b[s_sc1] + gq + 1)
                    gp.tensor_copy(carry[:, g:g + 1], h_t[gq % 3][:, 1023:1024]) \
                        .then_inc(s_gsc.inc(), 1)
                    if gq >= 3:
                        gp.wait_ge(s_gym.h, b[s_gym] + gq - 2)
                    gp.tensor_tensor(tmp[gq % 3][:, 512:1024], h_t[gq % 3][:, 512:1024],
                                     Cp[:, 512:1024], ALU.mult).then_inc(s_gt1.inc(), 1)
                gp.wait_ge(s_gyd.h, b[s_gyd] + (tc + 1) * 512)
                gp.collective_compute("AllGather", ALU.bypass,
                                      replica_groups=[list(range(NCORE))],
                                      ins=[y_mine.ap()[tc * DSL:(tc + 1) * DSL, :]],
                                      outs=[y_full.ap()[tc * NCORE * DSL:
                                                        (tc + 1) * NCORE * DSL, :]]) \
                    .then_inc(s_cc.inc(), 1)

            @blk.sync
            def _(sy):
                for g in range(32):
                    gq = tc * 32 + g
                    sy.wait_ge(s_gye.h, b[s_gye] + gq + 1)
                    sy.dma_start(out=y_mine.ap()[tc * DSL + 8 * g:tc * DSL + 8 * g + 8, :],
                                 in_=y_sb[gq % 2][:]).then_inc(s_gyd.inc(16), 16)

    with nc.Block() as blk:
        @blk.gpsimd
        def _(g):
            g.wait_ge(s_cc.h, b[s_cc] + 10)

    es2.close()
    es3 = contextlib.ExitStack()
    yloc = [sb_scoped(es3, nc, f"{pfx}yl{c}", [128, 1024], F32) for c in range(NCH)]
    xnl = [sb_scoped(es3, nc, f"{pfx}xnl{c}", [128, 1024], F32) for c in range(NCH)]

    for arm in nc.switch_core_id(engines=[nc.sync], n=128):
        try:
            rank = int(arm.logical) % NCORE
        except Exception:
            rank = 0
        for c in range(NCH):
            nc.sync.dma_start(out=yloc[c][:],
                              in_=y_full.ap()[rank * NCORE * DSL + c * 128:
                                              rank * NCORE * DSL + (c + 1) * 128, :]) \
                .then_inc(s_yl.h, 16)
            nc.sync.dma_start(out=xnl[c][:],
                              in_=agx_out.ap()[rank * D + c * 128:
                                               rank * D + (c + 1) * 128, :]) \
                .then_inc(s_yl.h, 16)
    s_yl.n += 32 * NCH

    yf = sb_scoped(es3, nc, pfx + "yf", [128, 1024], F32)
    with nc.Block() as blk:
        @blk.vector
        def _(v):
            v.wait_ge(s_yl.h, s_yl.n)
            for c in range(NCH):
                v.scalar_tensor_tensor(yf[:], xnl[c][:], D_sb[:, c:c + 1],
                                       yloc[c][:], ALU.mult, ALU.add)
                v.tensor_tensor(X[c][:], X[c][:].bitcast(F32), yf[:], ALU.add) \
                    .then_inc(s_fin.inc(), 1)

    es3.close()
    es0.close()
    return [(s_fin, s_fin.n)]



def emit_l1ffn(C, X, dd, pfx):
    nc = C.nc
    P = C.P
    es = contextlib.ExitStack()
    sbs = lambda n, shp, dt: sb_scoped(es, nc, pfx + n, shp, dt)

    XNb = [sbs(f"xnb{c}", [128, 1024], BF16) for c in range(NCH)]
    gt_sb = sbs("gt", [128, 16 * 8], BF16)
    gb_sb = sbs("gb", [1, 8], F32R)
    oh8_sb = sbs("oh8s", [8, 8 * 128], F32R)
    sc_sb = sbs("scs", [8, 1024], F32)
    sct = sbs("sct", [128, 64], F32)
    ab = sbs("ab", [128, 64], F32)
    lb = sbs("lb", [128, 64], F32)
    ssq = sbs("ssq", [128, 64], F32)
    bt = sbs("bt", [128, 64], F32)
    m1 = sbs("m1", [128, 64], F32)
    m2 = sbs("m2", [128, 64], F32)
    sup = sbs("sup", [128, 64], F32)
    mx8 = sbs("mx8", [128, 8], F32)
    cw_t = sbs("cwt", [128, 64], F32)
    cwe_sb = sbs("cwe", [8, 1024], F32R)
    cwb_sb = sbs("cwb", [128, 1024], F32)
    th = [sbs(f"th{i}", [128, 1024], F32) for i in range(2)]
    hbt = sbs("hbt", [128, 1024], F32)
    hb = [sbs(f"hb{i}", [128, 1024], BF16) for i in range(NI)]
    w13 = [sbs(f"w13_{i}", [128, 16 * 128], BF16) for i in range(4)]
    w2b = [sbs(f"w2_{i}", [128, NI * 128], BF16) for i in range(2)]

    s_ld = C.sem("g_ld")
    s_wl = C.sem("g_wl")
    s_wf13 = C.sem("g_wf13")
    s_sl = C.sem("g_sl")
    s_tt1 = C.sem("g_tt1")
    s_hb = C.sem("g_hb")
    s_o = C.sem("g_o")
    s_ox = C.sem("g_ox")
    s_cwm = C.sem("g_cwm")
    s_cwb = C.sem("g_cwb")
    bS = {s: s.n for s in (s_wl, s_wf13, s_sl, s_tt1, s_hb, s_o, s_ox, s_cwm, s_cwb)}

    with nc.Block() as blk:
        @blk.sync
        def _(sy):
            dmai(sy, gb_sb[:], dd["gate_b"].bitcast(F32R), s_ld)
            dmai(sy, oh8_sb[:], dd["oh8"].bitcast(F32R), s_ld)
            sy.wait_ge(s_ld.h, s_ld.n)

        @blk.gpsimd
        def _(g):
            g.dma_start(out=gt_sb[:].rearrange("p (c m) -> p c m", c=16),
                        in_=dd["gateT"].rearrange("(c p) m -> p c m", p=128)) \
                .then_inc(s_ld.inc(16), 16)
            g.wait_ge(s_ld.h, s_ld.n)

    emit_ln(C, X, XNb, pfx + "ln")

    with nc.Block() as blk:
        @blk.tensor
        def _(tn):
            for c in range(NCH):
                for hh in range(2):
                    cl = slice(hh * 512, (hh + 1) * 512)
                    tn.matmul(P[0][0:8, cl], gt_sb[:, c * 8:(c + 1) * 8], XNb[c][:, cl],
                              start=(c == 0), stop=(c == NCH - 1))

    with nc.Block() as blk:
        @blk.vector
        def _(v):
            v.tensor_copy(sc_sb[:], P[0][0:8, :])
            v.memset(C.jC[:], 0.0)

    with nc.Block() as blk:
        @blk.tensor
        def _(tn):
            for t8 in range(8):
                tn.transpose(P[1][:, t8 * 8:(t8 + 1) * 8],
                             sc_sb[0:8, t8 * 128:(t8 + 1) * 128],
                             C.consts["ident"][0:8, 0:8])
            tn.matmul(P[2][:, 0:8], C.consts["ones_row"][:], gb_sb[:], start=True, stop=True)

    with nc.Block() as blk:
        @blk.vector
        def _(v):
            v.tensor_copy(sct[:], P[1][:, 0:64])
            v.scalar_tensor_tensor(ab[:], sct[:], -1.0, sct[:], ALU.mult, ALU.max)
            v.memset(C.jC[:], 0.0)

    with nc.Block() as blk:
        @blk.scalar
        def _(sc):
            sc.activation(lb[:], ab[:], AF.Exp, scale=-1.0)
            sc.memzero(C.jC[:])

    with nc.Block() as blk:
        @blk.scalar
        def _(sc):
            sc.activation(ab[:], lb[:], AF.Ln, bias=1.0)
            sc.memzero(C.jC[:])

    with nc.Block() as blk:
        @blk.vector
        def _(v):
            v.scalar_tensor_tensor(lb[:], sct[:], 0.0, ab[:], ALU.max, ALU.add)
            v.memset(C.jC[:], 0.0)

    with nc.Block() as blk:
        @blk.scalar
        def _(sc):
            sc.activation(ssq[:], lb[:], AF.Sqrt)
            sc.memzero(C.jC[:])

    S8 = [slice(t8 * 8, (t8 + 1) * 8) for t8 in range(8)]
    with nc.Block() as blk:
        @blk.vector
        def _(v):
            s_hz = C.sem("hz")
            for t8 in range(8):
                v.tensor_tensor(bt[:, S8[t8]], ssq[:, S8[t8]], P[2][:, 0:8], ALU.add)
            for t8 in range(8):
                v.reduce_max(mx8[:, t8:t8 + 1], bt[:, S8[t8]], axis=AX) \
                    .then_inc(s_hz.inc(), 1)
            v.wait_ge(s_hz.h, s_hz.n)
            for t8 in range(8):
                v.tensor_scalar(m1[:, S8[t8]], bt[:, S8[t8]], mx8[:, t8:t8 + 1],
                                None, ALU.is_ge)
            for t8 in range(8):
                v.scalar_tensor_tensor(sup[:, S8[t8]], m1[:, S8[t8]], -1e30,
                                       bt[:, S8[t8]], ALU.mult, ALU.add)
            for t8 in range(8):
                v.reduce_max(mx8[:, t8:t8 + 1], sup[:, S8[t8]], axis=AX) \
                    .then_inc(s_hz.inc(), 1)
            v.wait_ge(s_hz.h, s_hz.n)
            for t8 in range(8):
                v.tensor_scalar(m2[:, S8[t8]], sup[:, S8[t8]], mx8[:, t8:t8 + 1],
                                None, ALU.is_ge)
            for t8 in range(8):
                v.tensor_tensor(m1[:, S8[t8]], m1[:, S8[t8]], m2[:, S8[t8]], ALU.add)
            for t8 in range(8):
                v.scalar_tensor_tensor(cw_t[:, S8[t8]], m1[:, S8[t8]], 0.5,
                                       ssq[:, S8[t8]], ALU.mult, ALU.mult)
            v.memset(C.jC[:], 0.0)

    with nc.Block() as blk:
        @blk.tensor
        def _(tn):
            for t8 in range(8):
                tn.transpose(P[1][0:8, t8 * 128:(t8 + 1) * 128],
                             cw_t[:, t8 * 8:(t8 + 1) * 8],
                             C.consts["ident"][:, 0:128])

    with nc.Block() as blk:
        @blk.vector
        def _(v):
            v.tensor_copy(cwe_sb[:], P[1][0:8, :])
            v.memset(C.jC[:], 0.0)

    elist = list(range(NEXP)) if _EL1 is None else list(_EL1)
    nrouted_before = [sum(1 for x in elist[:k] if x < 8) for k in range(len(elist) + 1)]
    with nc.Block() as blk:
        @blk.sync
        def _(sy):
            for ei, e in enumerate(elist):
                for i in range(NI):
                    for wd, nm in ((dd["w1T"], 0), (dd["w3T"], 1)):
                        j13 = ei * 22 + 2 * i + nm
                        if j13 >= 4:
                            sy.wait_ge(s_wf13.h, bS[s_wf13] + j13 - 3)
                        dmai(sy, w13[j13 % 4][:].rearrange("p (c m) -> p c m", c=16),
                             wd[e, :, i * 128:(i + 1) * 128]
                             .rearrange("(c p) m -> p c m", p=128), s_wl)
                for ddx in range(NCH):
                    j2 = ei * 16 + ddx
                    if j2 >= 2:
                        sy.wait_ge(s_o.h, bS[s_o] + j2 - 1)
                    dmai(sy, w2b[j2 % 2][:].rearrange("p (c m) -> p c m", c=NI),
                         dd["w2T"][e, :, ddx * 128:(ddx + 1) * 128]
                         .rearrange("(c p) m -> p c m", p=128), s_wl)

        @blk.tensor
        def _(tn):
            jg = 0
            for ei, e in enumerate(elist):
                routed = e < 8
                if routed:
                    if ei > 0:
                        tn.wait_ge(s_hb.h, bS[s_hb] + 11 * ei)
                        tn.wait_ge(s_ox.h, bS[s_ox] + 16 * ei)
                    for hh in range(2):
                        cl = slice(hh * 512, (hh + 1) * 512)
                        ins = tn.matmul(P[2][:, cl], oh8_sb[0:8, e * 128:(e + 1) * 128],
                                        cwe_sb[0:8, cl], start=True, stop=True)
                    ins.then_inc(s_cwm.inc(), 1)
                elif ei > 0:
                    tn.wait_ge(s_hb.h, bS[s_hb] + 11 * ei)
                    tn.wait_ge(s_ox.h, bS[s_ox] + 16 * ei)
                for i in range(NI):
                    for nm in range(2):
                        tn.wait_ge(s_wl.h, bS[s_wl] + 16 * (jg + 1))
                        slot = w13[(ei * 22 + 2 * i + nm) % 4]
                        dst = P[2 * (i % 2) + nm]
                        if i >= 2 and nm == 0:
                            tn.wait_ge(s_sl.h, bS[s_sl] + 11 * ei + i - 1)
                            tn.wait_ge(s_tt1.h, bS[s_tt1] + 11 * ei + i - 1)
                        if i == 1 and nm == 0 and routed:
                            tn.wait_ge(s_cwb.h, bS[s_cwb] + nrouted_before[ei] + 1)
                        for k in range(NCH):
                            for hh in range(2):
                                cl = slice(hh * 512, (hh + 1) * 512)
                                ins = tn.matmul(dst[:, cl], slot[:, k * 128:(k + 1) * 128],
                                                XNb[k][:, cl], start=(k == 0), stop=(k == NCH - 1))
                        ins.then_inc(s_wf13.inc(), 1)
                        jg += 1
                tn.wait_ge(s_hb.h, bS[s_hb] + 11 * (ei + 1))
                for ddx in range(NCH):
                    tn.wait_ge(s_wl.h, bS[s_wl] + 16 * (jg + 1))
                    if ddx >= 2:
                        tn.wait_ge(s_ox.h, bS[s_ox] + 16 * ei + ddx - 1)
                    slot = w2b[(ei * 16 + ddx) % 2]
                    dst = P[ddx % 2]
                    for k in range(NI):
                        for hh in range(2):
                            cl = slice(hh * 512, (hh + 1) * 512)
                            ins = tn.matmul(dst[:, cl], slot[:, k * 128:(k + 1) * 128],
                                            hb[k][:, cl], start=(k == 0), stop=(k == NI - 1))
                    ins.then_inc(s_o.inc(), 1)
                    jg += 1

        @blk.scalar
        def _(sc):
            for ei, e in enumerate(elist):
                for i in range(NI):
                    gi = 11 * ei + i
                    sc.wait_ge(s_wf13.h, bS[s_wf13] + 22 * ei + 2 * i + 1)
                    if gi >= 2:
                        sc.wait_ge(s_tt1.h, bS[s_tt1] + gi - 1)
                    sc.activation(th[gi % 2][:], P[2 * (i % 2)][:], AF.Silu) \
                        .then_inc(s_sl.inc(), 1)

        @blk.vector
        def _(v):
            for ei, e in enumerate(elist):
                routed = e < 8
                if routed:
                    v.wait_ge(s_cwm.h, bS[s_cwm] + nrouted_before[ei] + 1)
                    v.tensor_copy(cwb_sb[:], P[2][:]).then_inc(s_cwb.inc(), 1)
                for i in range(NI):
                    gi = 11 * ei + i
                    v.wait_ge(s_sl.h, bS[s_sl] + gi + 1)
                    v.wait_ge(s_wf13.h, bS[s_wf13] + 22 * ei + 2 * i + 2)
                    v.tensor_tensor(hbt[:], th[gi % 2][:], P[2 * (i % 2) + 1][:],
                                    ALU.mult).then_inc(s_tt1.inc(), 1)
                    if routed:
                        v.tensor_tensor(hb[i][:], hbt[:], cwb_sb[:], ALU.mult) \
                            .then_inc(s_hb.inc(), 1)
                    else:
                        v.tensor_scalar(hb[i][:], hbt[:], 0.5, None, ALU.mult) \
                            .then_inc(s_hb.inc(), 1)
                for ddx in range(NCH):
                    v.wait_ge(s_o.h, bS[s_o] + 16 * ei + ddx + 1)
                    v.tensor_tensor(X[ddx][:], X[ddx][:].bitcast(F32), P[ddx % 2][:],
                                    ALU.add).then_inc(s_ox.inc(), 1)

    es.close()
    return [(s_ox, s_ox.n)]



def emit_l1attn(C, X, dd, pfx):
    nc = C.nc
    P = C.P
    es = contextlib.ExitStack()
    sbs = lambda n, shp, dt: sb_scoped(es, nc, pfx + n, shp, dt)

    ar_in = nc.dram_tensor(pfx + "ar_in", [128, 1040], F32)
    ar_out = nc.dram_tensor(pfx + "ar_out", [128, 1040], F32)

    qe = [sbs(f"qe{j}", [128, 1024], BF16) for j in range(8)]
    ke = [sbs(f"ke{j}", [128, 1024], BF16) for j in range(8)]
    kv_sb = sbs("kv", [128, 1024], F32R)
    ks_bd = sbs("ksbd", [128, 16], F32R)
    kst = sbs("kst", [128, 8], F32)
    rcp_t = sbs("rcpt", [1, 1024], F32)
    rcp = sbs("rcp", [1, 1024], F32R)
    wmq_sb = sbs("wmq", [128, 64], BF16)
    wmk_sb = sbs("wmk", [128, 64], BF16)
    bq_sb = sbs("bqs", [128, 16], F32)
    bk_sb = sbs("bks", [128, 16], F32)
    bo_sb = sbs("bos", [128, 16], F32)
    bv_bf = sbs("bvbf", [1, D], BF16)
    orow_bf = sbs("orowbf", [1, 128], BF16)
    ident_bf = sbs("identbf", [128, 128], BF16)

    s_ld = C.sem("a_ld")
    s_w = C.sem("a_w")
    s_pj = C.sem("a_pj"); s_bh = C.sem("a_bh")
    s_qm = C.sem("a_qm"); s_e1 = C.sem("a_e1"); s_e2 = C.sem("a_e2")
    s_elu = C.sem("a_elu")
    s_vm = C.sem("a_vm"); s_ve = C.sem("a_ve"); s_vw = C.sem("a_vw")
    s_tp = C.sem("a_tp"); s_te = C.sem("a_te")
    s_cc = C.sem("a_cc"); s_ar = C.sem("a_ar")
    s_nm = C.sem("a_nm"); s_nr = C.sem("a_nr")
    s_am = C.sem("a_am"); s_ae = C.sem("a_ae"); s_ad = C.sem("a_ad")
    s_om = C.sem("a_om"); s_oe = C.sem("a_oe"); s_od = C.sem("a_od")
    bA = {s: s.n for s in (s_vm, s_ve, s_vw, s_tp, s_te, s_cc, s_nm, s_nr,
                           s_am, s_ae, s_ad, s_om, s_oe, s_od)}

    with nc.Block() as blk:
        @blk.sync
        def _(sy):
            for dst, srcd in ((bq_sb, dd["bq"]), (bk_sb, dd["bk"]), (bo_sb, dd["bo"])):
                dmai(sy, dst[:], srcd, s_ld)
            sy.wait_ge(s_ld.h, s_ld.n)

        @blk.gpsimd
        def _(g):
            g.dma_start(out=wmq_sb[:], in_=dd["wmqT"]).then_inc(s_ld.inc(16), 16)
            g.dma_start(out=wmk_sb[:], in_=dd["wmkT"]).then_inc(s_ld.inc(16), 16)
            g.dma_start(out=bv_bf[:], in_=dd["bvR"]).then_inc(s_ld.inc(16), 16)
            g.wait_ge(s_ld.h, s_ld.n)

    with nc.Block() as blk:
        @blk.gpsimd
        def _(g):
            g.dma_start(out=orow_bf[:], in_=C.consts["ones_row"][:].bitcast(F32)) \
                .then_inc(s_ld.inc(16), 16)
            g.dma_start(out=ident_bf[:], in_=C.consts["ident"][:]).then_inc(s_ld.inc(16), 16)
            g.wait_ge(s_ld.h, s_ld.n)

        @blk.vector
        def _(v):
            v.memset(C.jC[:], 0.0)
            v.memset(C.jC[:], 0.0)
            v.tensor_scalar(ks_bd[:], C.jC[:, 0:16], 0.0, None, ALU.mult)

    es_xn = contextlib.ExitStack()
    XNb = [sb_scoped(es_xn, nc, f"{pfx}xnb{c}", [128, 1024], BF16) for c in range(NCH)]

    emit_ln(C, X, XNb, pfx + "ln")

    es_qk = contextlib.ExitStack()
    wr = [sb_scoped(es_qk, nc, f"{pfx}wr{i}", [128, 16 * 128], BF16) for i in range(4)]
    qh = sb_scoped(es_qk, nc, pfx + "qh", [128, 1024], BF16)
    el1 = sb_scoped(es_qk, nc, pfx + "el1", [128, 1024], F32)
    el2 = sb_scoped(es_qk, nc, pfx + "el2", [128, 1024], F32)

    for which, d_w, wmap, bias_sb, out_tiles in (
            ("q", dd["wqT"], wmq_sb, bq_sb, qe), ("k", dd["wkT"], wmk_sb, bk_sb, ke)):
        b_w = s_w.n
        b_pj, b_bh, b_qm = s_pj.n, s_bh.n, s_qm.n
        b_e1, b_e2, b_el = s_e1.n, s_e2.n, s_elu.n

        with nc.Block() as blk:
            @blk.gpsimd
            def _(g):
                for f in range(NCH):
                    if f >= 4:
                        g.wait_ge(s_pj.h, b_pj + f - 3)
                    g.dma_start(out=wr[f % 4][:].rearrange("p (c m) -> p c m", c=16),
                                in_=d_w[:, f * 128:(f + 1) * 128]
                                .rearrange("(c p) m -> p c m", p=128)) \
                        .then_inc(s_w.inc(16), 16)

            @blk.tensor
            def _(tn):
                for f in range(NCH):
                    j = f // 2
                    tn.wait_ge(s_w.h, b_w + 16 * (f + 1))
                    if f >= 2:
                        tn.wait_ge(s_bh.h, b_bh + f - 1)
                    for k in range(NCH):
                        for hh in range(2):
                            cl = slice(hh * 512, (hh + 1) * 512)
                            ins = tn.matmul(P[f % 2][:, cl],
                                            wr[f % 4][:, k * 128:(k + 1) * 128],
                                            XNb[k][:, cl], start=(k == 0), stop=(k == NCH - 1))
                    ins.then_inc(s_pj.inc(), 1)
                    tn.wait_ge(s_bh.h, b_bh + f + 1)
                    if f >= 4:
                        tn.wait_ge(s_elu.h, b_el + j - 1)
                    rows = slice(64 * (f % 2), 64 * (f % 2) + 64)
                    for hh in range(2):
                        cl = slice(hh * 512, (hh + 1) * 512)
                        ins = tn.matmul(P[2 + (j % 2)][rows, cl], wmap[:], qh[:, cl],
                                        start=True, stop=True)
                    ins.then_inc(s_qm.inc(), 1)

            @blk.scalar
            def _(sc):
                for f in range(NCH):
                    j = f // 2
                    sc.wait_ge(s_pj.h, b_pj + f + 1)
                    if f >= 1:
                        sc.wait_ge(s_qm.h, b_qm + f)
                    sc.activation(qh[:], P[f % 2][:], AF.Identity,
                                  bias=bias_sb[:, f:f + 1]).then_inc(s_bh.inc(), 1)
                    if f % 2 == 1:
                        sc.wait_ge(s_e1.h, b_e1 + j + 1)
                        if j >= 1:
                            sc.wait_ge(s_elu.h, b_el + j)
                        sc.activation(el2[:], el1[:], AF.Exp).then_inc(s_e2.inc(), 1)

            @blk.vector
            def _(v):
                s_hz = C.sem("hz")
                for j in range(8):
                    v.wait_ge(s_qm.h, b_qm + 2 * j + 2)
                    if j >= 1:
                        v.wait_ge(s_e2.h, b_e2 + j)
                    v.tensor_scalar(el1[:], P[2 + (j % 2)][:], 0.0, None, ALU.min) \
                        .then_inc(s_e1.inc(), 1)
                    v.wait_ge(s_e2.h, b_e2 + j + 1)
                    v.scalar_tensor_tensor(out_tiles[j][:], P[2 + (j % 2)][:], 0.0,
                                           el2[:], ALU.max, ALU.add) \
                        .then_inc(s_elu.inc(), 1)
                    if which == "k":
                        v.reduce_sum(kst[:, j:j + 1], out_tiles[j][:], axis=AX) \
                            .then_inc(s_hz.inc(), 1)
                v.memset(C.jC[:], 0.0)

    es_qk.close()

    with nc.Block() as blk:
        @blk.vector
        def _(v):
            v.wait_ge(C.sem("hz").h, C.sem("hz").n)
            for j in range(8):
                v.tensor_copy(ks_bd[0:64, 2 * j:2 * j + 1], kst[0:64, j:j + 1])
                v.tensor_copy(ks_bd[64:128, 2 * j + 1:2 * j + 2], kst[64:128, j:j + 1])
            v.memset(C.jC[:], 0.0)

    es_v = contextlib.ExitStack()
    vwb = [sb_scoped(es_v, nc, f"{pfx}vw{i}", [128, 16 * 512], BF16)
           for i in range(2)]
    vT = [sb_scoped(es_v, nc, f"{pfx}vT{t}", [128, D], BF16) for t in range(4)]
    emkT = [sb_scoped(es_v, nc, f"{pfx}emkT{t}", [128, 1024], BF16) for t in range(2)]

    for sblk in range(2):
        with nc.Block() as blk:
            @blk.gpsimd
            def _(g):
                for nq in range(4):
                    bb = sblk * 4 + nq
                    if bb >= 2:
                        g.wait_ge(s_vm.h, bA[s_vm] + 4 * (bb - 1))
                    for k in range(NCH):
                        g.dma_start(out=vwb[bb % 2][:, k * 512:(k + 1) * 512],
                                    in_=dd["wvT"][k * 128:(k + 1) * 128,
                                                  nq * 512:(nq + 1) * 512]) \
                            .then_inc(s_vw.inc(16), 16)

            @blk.tensor
            def _(tn):
                for nq in range(4):
                    bb = sblk * 4 + nq
                    tn.wait_ge(s_vw.h, bA[s_vw] + 16 * 16 * (bb + 1))
                    for tb in range(4):
                        tbg = sblk * 4 + tb
                        it = bb * 4 + tb
                        slot = it % 4
                        dst = P[slot // 2][:, (slot % 2) * 512:(slot % 2) * 512 + 512]
                        if it >= 4:
                            tn.wait_ge(s_ve.h, bA[s_ve] + it - 3)
                        for k in range(NCH):
                            tn.matmul(dst, XNb[k][:, tbg * 128:(tbg + 1) * 128],
                                      vwb[bb % 2][:, k * 512:(k + 1) * 512],
                                      start=(k == 0), stop=False)
                        ins = tn.matmul(dst, orow_bf[:],
                                        bv_bf[:, nq * 512:(nq + 1) * 512],
                                        start=False, stop=True)
                        ins.then_inc(s_vm.inc(), 1)

            @blk.scalar
            def _(sc):
                for nq in range(4):
                    for tb in range(4):
                        it = (sblk * 4 + nq) * 4 + tb
                        slot = it % 4
                        srcp = P[slot // 2][:, (slot % 2) * 512:(slot % 2) * 512 + 512]
                        sc.wait_ge(s_vm.h, bA[s_vm] + it + 1)
                        sc.activation(vT[tb][:, nq * 512:(nq + 1) * 512], srcp,
                                      AF.Identity).then_inc(s_ve.inc(), 1)

        with nc.Block() as blk:
            @blk.tensor
            def _(tn):
                for tb in range(4):
                    tbg = sblk * 4 + tb
                    for h in range(16):
                        it = (sblk * 4 + tb) * 16 + h
                        if it >= 2:
                            tn.wait_ge(s_te.h, bA[s_te] + it - 1)
                        j = h // 2
                        rows = slice(64 * (h % 2), 64 * (h % 2) + 64)
                        ib = 64 * (h % 2)
                        sl_c = (it % 2) * 1024
                        tn.transpose(P[2][:].bitcast(BF16)[:, sl_c:sl_c + 64],
                                     ke[j][rows, tbg * 128:(tbg + 1) * 128],
                                     ident_bf[ib:ib + 64, ib:ib + 64]).then_inc(s_tp.inc(), 1)
                    for h in range(16):
                        rows = slice(64 * (h % 2), 64 * (h % 2) + 64)
                        tn.wait_ge(s_te.h, bA[s_te] + (sblk * 4 + tb) * 16 + h + 1)
                        tn.matmul(P[3][rows, (h // 2) * 128:(h // 2) * 128 + 128],
                                  emkT[tb % 2][:, h * 64:(h + 1) * 64],
                                  vT[tb][:, h * 128:(h + 1) * 128],
                                  start=(tbg == 0), stop=(tbg == 7))

            @blk.vector
            def _(v):
                for tb in range(4):
                    for h in range(16):
                        it = (sblk * 4 + tb) * 16 + h
                        v.wait_ge(s_tp.h, bA[s_tp] + it + 1)
                        v.tensor_copy(emkT[tb % 2][:, h * 64:(h + 1) * 64],
                                      P[2][:].bitcast(BF16)[:, (it % 2) * 1024:(it % 2) * 1024 + 64]) \
                            .then_inc(s_te.inc(), 1)

    with nc.Block() as blk:
        @blk.vector
        def _(v):
            v.tensor_copy(kv_sb[:], P[3][:])
            v.memset(C.jC[:], 0.0)

    es_v.close()
    es_xn.close()

    with nc.Block() as blk:
        @blk.sync
        def _(sy):
            dmai(sy, ar_in.ap()[:, 0:1024], kv_sb[:].bitcast(F32), s_ar)
            dmai(sy, ar_in.ap()[:, 1024:1040], ks_bd[:].bitcast(F32), s_ar)

        @blk.gpsimd
        def _(g):
            g.wait_ge(s_ar.h, s_ar.n)
            g.collective_compute(
                "AllReduce", ALU.add,
                replica_groups=[[0, 1, 2, 3], [4, 5, 6, 7]],
                ins=[ar_in.ap()], outs=[ar_out.ap()],
            ).then_inc(s_cc.inc(), 1)
            g.wait_ge(s_cc.h, bA[s_cc] + 1)

    es_att = contextlib.ExitStack()
    att = [sb_scoped(es_att, nc, f"{pfx}att{h}", [128, 1024], BF16) for h in range(NCH)]
    at_sb = sb_scoped(es_att, nc, pfx + "at", [128, 1024], F32)
    kv_bf = sb_scoped(es_att, nc, pfx + "kvbf", [128, 1024], BF16)
    ksb_pad = sb_scoped(es_att, nc, pfx + "ksbp", [128, 32], BF16)

    with nc.Block() as blk:
        @blk.gpsimd
        def _(g):
            g.dma_start(out=kv_bf[:], in_=ar_out.ap()[:, 0:1024]) \
                .then_inc(s_ar.inc(16), 16)
            g.wait_ge(s_ar.h, s_ar.n)

        @blk.sync
        def _(sy):
            dmai(sy, ks_bd[:], ar_out.ap()[:, 1024:1040].bitcast(F32R), s_ar)
            sy.wait_ge(s_ar.h, s_ar.n)

    with nc.Block() as blk:
        @blk.vector
        def _(v):
            v.tensor_copy(ksb_pad[:].rearrange("p (h two) -> p h two", two=2)
                          [:, :, 0:1], ks_bd[:].bitcast(F32)
                          .rearrange("p (h o) -> p h o", o=1))
            v.memset(C.jC[:], 0.0)

    with nc.Block() as blk:
        @blk.tensor
        def _(tn):
            for h in range(16):
                if h >= 2:
                    tn.wait_ge(s_ad.h, bA[s_ad] + h - 1)
                for hh in range(2):
                    cl = slice(hh * 512, (hh + 1) * 512)
                    ins = tn.matmul(P[2 + (h % 2)][0:1, cl],
                                    ksb_pad[:, 2 * h:2 * h + 1],
                                    qe[h // 2][:, cl], start=True, stop=True)
                ins.then_inc(s_nm.inc(), 1)
                if h >= 2:
                    tn.wait_ge(s_ae.h, bA[s_ae] + h - 1)
                rows = slice(64 * (h % 2), 64 * (h % 2) + 64)
                for hh in range(2):
                    cl = slice(hh * 512, (hh + 1) * 512)
                    tn.matmul(P[h % 2][:, cl],
                              kv_bf[rows, (h // 2) * 128:(h // 2) * 128 + 128],
                              qe[h // 2][rows, cl], start=True, stop=True)
                tn.wait_ge(s_nr.h, bA[s_nr] + h + 1)
                for hh in range(2):
                    cl = slice(hh * 512, (hh + 1) * 512)
                    ins = tn.matmul(P[2 + (h % 2)][:, cl], C.consts["ones_row"][:],
                                    rcp[0:1, cl], start=True, stop=True)
                ins.then_inc(s_am.inc(), 1)

        @blk.vector
        def _(v):
            s_hz = C.sem("hz")
            for h in range(16):
                v.wait_ge(s_nm.h, bA[s_nm] + h + 1)
                v.tensor_scalar(rcp_t[:], P[2 + (h % 2)][0:1, :], 1e-6, None, ALU.add) \
                    .then_inc(s_hz.inc(), 1)
                v.wait_ge(s_hz.h, s_hz.n)
                v.reciprocal(rcp[:], rcp_t[:]).then_inc(s_nr.inc(), 1)
                v.wait_ge(s_ae.h, bA[s_ae] + h + 1)
                v.tensor_tensor(att[h][:], at_sb[:], P[2 + (h % 2)][:], ALU.mult) \
                    .then_inc(s_ad.inc(), 1)

        @blk.scalar
        def _(sc):
            for h in range(16):
                sc.wait_ge(s_am.h, bA[s_am] + h + 1)
                if h >= 1:
                    sc.wait_ge(s_ad.h, bA[s_ad] + h)
                sc.activation(at_sb[:], P[h % 2][:], AF.Identity).then_inc(s_ae.inc(), 1)

    wr2 = [sb_scoped(es_att, nc, f"{pfx}wr2_{i}", [128, 16 * 128], BF16) for i in range(4)]
    ot = sb_scoped(es_att, nc, pfx + "ot", [128, 1024], F32)
    with nc.Block() as blk:
        @blk.gpsimd
        def _(g):
            for ddx in range(NCH):
                if ddx >= 4:
                    g.wait_ge(s_om.h, bA[s_om] + ddx - 3)
                g.dma_start(out=wr2[ddx % 4][:].rearrange("p (c m) -> p c m", c=16),
                            in_=dd["woT"][:, ddx * 128:(ddx + 1) * 128]
                            .rearrange("(c p) m -> p c m", p=128)) \
                    .then_inc(s_w.inc(16), 16)

        @blk.tensor
        def _(tn):
            b_w = s_w.n - 16 * NCH
            for ddx in range(NCH):
                tn.wait_ge(s_w.h, b_w + 16 * (ddx + 1))
                if ddx >= 2:
                    tn.wait_ge(s_oe.h, bA[s_oe] + ddx - 1)
                for k in range(NCH):
                    for hh in range(2):
                        cl = slice(hh * 512, (hh + 1) * 512)
                        ins = tn.matmul(P[ddx % 2][:, cl], wr2[ddx % 4][:, k * 128:(k + 1) * 128],
                                        att[k][:, cl], start=(k == 0), stop=(k == NCH - 1))
                ins.then_inc(s_om.inc(), 1)

        @blk.scalar
        def _(sc):
            for ddx in range(NCH):
                sc.wait_ge(s_om.h, bA[s_om] + ddx + 1)
                if ddx >= 1:
                    sc.wait_ge(s_od.h, bA[s_od] + ddx)
                sc.activation(ot[:], P[ddx % 2][:], AF.Identity,
                              bias=bo_sb[:, ddx:ddx + 1]).then_inc(s_oe.inc(), 1)

        @blk.vector
        def _(v):
            for ddx in range(NCH):
                v.wait_ge(s_oe.h, bA[s_oe] + ddx + 1)
                v.tensor_tensor(X[ddx][:], X[ddx][:].bitcast(F32), ot[:], ALU.add) \
                    .then_inc(s_od.inc(), 1)

    es_att.close()
    es.close()
    return [(s_od, s_od.n)]



_EMIT = {"l0ffn": emit_l0ffn, "l0attn": emit_l0attn,
         "l1ffn": emit_l1ffn, "l1attn": emit_l1attn}
_SEQ = ["l0ffn", "l0attn", "l0ffn", "l1ffn", "l1attn", "l1ffn"]


def _declare(nc, keys):
    dd = {}
    for k in keys:
        shp, dt = _DRAM_SPECS[k]
        dd[k] = nc.dram_tensor(k, shp, dt, kind="ExternalInput").ap()
    return dd


def build_single(which):
    nc = bass.Bass("TRN2", target_bir_lowering=False)
    C = Ctx(nc)
    keys = ["xT"] + _NEED[which] + _CONST_KEYS
    dd = _declare(nc, keys)
    d_y = nc.dram_tensor("yT", [D, TOK], F32, kind="ExternalOutput").ap()
    X = [nc.alloc_sbuf_tensor(f"x{c}", [128, 1024], F32R) for c in range(NCH)]
    s_ld = C.sem("io_ld")
    s_st = C.sem("io_st")
    with nc.Block() as blk:
        @blk.sync
        def _(sy):
            emit_load_stream(C, X, dd["xT"], sy, s_ld)
            emit_load_consts(C, dd, sy, s_ld)
            sy.wait_ge(s_ld.h, s_ld.n)
    waits = _EMIT[which](C, X, dd, which[:2] + "_")
    emit_store_stream(C, X, d_y, s_st, waits=waits)
    return nc, keys


def build_fused():
    nc = bass.Bass("TRN2", target_bir_lowering=False)
    C = Ctx(nc)
    keys = sorted({k for ks in _NEED.values() for k in ks}) + ["xT"] + _CONST_KEYS
    dd = _declare(nc, keys)
    d_y = nc.dram_tensor("yT", [D, TOK], F32, kind="ExternalOutput").ap()
    X = [nc.alloc_sbuf_tensor(f"x{c}", [128, 1024], F32R) for c in range(NCH)]
    s_ld = C.sem("io_ld")
    s_st = C.sem("io_st")
    with nc.Block() as blk:
        @blk.sync
        def _(sy):
            emit_load_stream(C, X, dd["xT"], sy, s_ld)
            emit_load_consts(C, dd, sy, s_ld)
            sy.wait_ge(s_ld.h, s_ld.n)
    waits = []
    for idx, which in enumerate(_SEQ):
        with nc.named_scope(f"s{idx}_{which}"):
            waits = _EMIT[which](C, X, dd, f"s{idx}_")
    emit_store_stream(C, X, d_y, s_st, waits=waits)
    return nc, keys



_CACHE = {}


def kernel(**inputs):
    h = _prep(inputs)
    if "fused" not in _CACHE:
        _CACHE["fused"] = build_fused()
    nc, keys = _CACHE["fused"]
    in_maps = []
    for c in range(NCORE):
        m = {}
        for k in keys:
            if k == "xT":
                m[k] = np.ascontiguousarray(h["xT"][:, c * TOK:(c + 1) * TOK])
            elif k in _PERCORE:
                m[k] = h[k][c]
            else:
                m[k] = h[k]
        in_maps.append(m)
    res = run_bass_kernel_spmd(nc, in_maps, core_ids=list(range(NCORE)))
    full = np.concatenate([res.results[c]["yT"] for c in range(NCORE)], axis=1)
    return np.ascontiguousarray(full.T).reshape(2, 4096, D).astype(np.float32)


# version 09 (final)
# speedup vs baseline: 1.0160x; 1.0058x over previous
import contextlib
import numpy as np
import ml_dtypes
import concourse.bass as bass
import concourse.mybir as mybir
from concourse.bass_utils import run_bass_kernel_spmd

F32 = mybir.dt.float32
F32R = mybir.dt.float32r
BF16 = mybir.dt.bfloat16
AF = mybir.ActivationFunctionType
ALU = mybir.AluOpType
AX = mybir.AxisListType.X

NCORE = 8
TOK = 1024
D = 2048
NCH = 16
NTOK = 8192
DSL = 256
LN_EPS = 1e-5
NEXP = 9
NI = 11
_EL1 = None
_POOL_OFF = True
_DBG_L0 = False



def _host_consts():
    c = {}
    c["ones_col"] = np.ones((128, 1), np.float32)
    c["ones_row"] = np.ones((1, 128), np.float32)
    c["ident"] = np.eye(128, dtype=np.float32)
    p = np.arange(128)
    c["m16e"] = (p[None, :] // 16 == np.arange(8)[:, None]).astype(np.float32)
    oh8 = np.zeros((8, 8 * 128), np.float32)
    for e in range(8):
        oh8[e, e * 128:(e + 1) * 128] = 1.0
    c["oh8"] = oh8
    e_b = np.zeros((32, 128), np.float32)
    e_c = np.zeros((32, 128), np.float32)
    for k in range(16):
        e_b[k] = (p % 16 == k)
        e_c[k + 16] = (p % 16 == k)
    c["e_b"], c["e_c"] = e_b, e_c
    g = np.zeros((16, 128, 128), np.float32)
    for gl in range(16):
        for pp in range(128):
            g[gl, gl * 8 + pp // 16, pp] = 1.0
    c["g_pat"] = np.ascontiguousarray(g.reshape(16 * 128, 128))
    c["r_pat"] = (p[:, None] // 16 == np.arange(8)[None, :]).astype(np.float32)
    return c


def _prep(inputs):
    h = dict(_host_consts())
    f32 = np.float32
    x = np.asarray(inputs["L0_attn_x"], f32).reshape(NTOK, D)
    h["xT"] = np.ascontiguousarray(x.T)

    dw = np.asarray(inputs["L0_ffn_down_w"], f32)
    h["dnT"] = np.ascontiguousarray(dw.transpose(2, 0, 1).reshape(D, 128))
    uw = np.asarray(inputs["L0_ffn_up_w"], f32)
    h["upT"] = np.ascontiguousarray(uw.transpose(0, 2, 1).reshape(128, D))
    h["rtT"] = np.ascontiguousarray(np.asarray(inputs["L0_ffn_router_w"], f32).T)

    h["xprojT"] = np.ascontiguousarray(np.asarray(inputs["L0_attn_x_proj_w"], f32).T)
    dtT = np.ascontiguousarray(np.asarray(inputs["L0_attn_dt_proj_w"], f32).T)
    h["dtprojT_sl"] = [np.ascontiguousarray(dtT[:, c * DSL:(c + 1) * DSL]) for c in range(NCORE)]
    dtb = np.asarray(inputs["L0_attn_dt_proj_b"], f32)
    h["dtb_sl"] = [np.ascontiguousarray(dtb[c * DSL:(c + 1) * DSL].reshape(2, 128).T)
                   for c in range(NCORE)]
    h["D_sb"] = np.ascontiguousarray(np.asarray(inputs["L0_attn_D"], f32).reshape(16, 128).T)
    al = np.asarray(inputs["L0_attn_A_log"], f32)
    alp = np.zeros((NCORE, 128, 32), f32)
    for core in range(NCORE):
        for g in range(32):
            for pp in range(128):
                alp[core, pp, g] = al[core * DSL + g * 8 + pp // 16, pp % 16]
    h["alogp"] = alp

    bf = ml_dtypes.bfloat16
    for nm in ["wq", "wk", "wv", "wo"]:
        h[nm + "T"] = np.ascontiguousarray(
            np.asarray(inputs["L1_attn_" + nm], f32).T.astype(bf))
    h["wmqT"] = np.ascontiguousarray(
        np.asarray(inputs["L1_attn_wmap_q"], f32).T.astype(bf))
    h["wmkT"] = np.ascontiguousarray(
        np.asarray(inputs["L1_attn_wmap_k"], f32).T.astype(bf))
    for nm in ["bq", "bk", "bo"]:
        h[nm] = np.ascontiguousarray(
            np.asarray(inputs["L1_attn_" + nm], f32).reshape(16, 128).T)
    h["bvR"] = np.asarray(inputs["L1_attn_bv"], f32).reshape(1, D).astype(bf)

    w1 = np.asarray(inputs["L1_ffn_w1"]); w3 = np.asarray(inputs["L1_ffn_w3"])
    w2 = np.asarray(inputs["L1_ffn_w2"])
    h["w1T"] = np.ascontiguousarray(
        np.concatenate([w1, np.asarray(inputs["L1_ffn_sw1"])[None]], 0).transpose(0, 2, 1))
    h["w3T"] = np.ascontiguousarray(
        np.concatenate([w3, np.asarray(inputs["L1_ffn_sw3"])[None]], 0).transpose(0, 2, 1))
    h["w2T"] = np.ascontiguousarray(
        np.concatenate([w2, np.asarray(inputs["L1_ffn_sw2"])[None]], 0).transpose(0, 2, 1))
    h["gateT"] = np.ascontiguousarray(np.asarray(inputs["L1_ffn_gate_w"], f32).T)
    h["gate_b"] = np.asarray(inputs["L1_ffn_gate_b"], f32).reshape(1, 8)
    return h


_DRAM_SPECS = {
    "xT": ([D, TOK], F32), "dnT": ([D, 128], F32), "upT": ([128, D], F32),
    "rtT": ([D, 8], F32), "m16e": ([8, 128], F32),
    "xprojT": ([D, 160], F32), "dtprojT_sl": ([128, DSL], F32),
    "dtb_sl": ([128, 2], F32), "alogp": ([128, 32], F32), "D_sb": ([128, 16], F32),
    "g_pat": ([16 * 128, 128], F32), "e_b": ([32, 128], F32), "e_c": ([32, 128], F32),
    "r_pat": ([128, 8], F32),
    "wqT": ([D, D], BF16), "wkT": ([D, D], BF16), "wvT": ([D, D], BF16),
    "woT": ([D, D], BF16), "wmqT": ([128, 64], BF16), "wmkT": ([128, 64], BF16),
    "bq": ([128, 16], F32), "bk": ([128, 16], F32), "bo": ([128, 16], F32),
    "bvR": ([1, D], BF16),
    "w1T": ([NEXP, D, 1408], BF16), "w3T": ([NEXP, D, 1408], BF16),
    "w2T": ([NEXP, 1408, D], BF16), "gateT": ([D, 8], F32), "gate_b": ([1, 8], F32),
    "oh8": ([8, 8 * 128], F32),
    "ones_col": ([128, 1], F32), "ones_row": ([1, 128], F32), "ident": ([128, 128], F32),
}
_PERCORE = {"dtprojT_sl", "dtb_sl", "alogp"}

_NEED = {
    "l0ffn": ["dnT", "upT", "rtT", "m16e"],
    "l0attn": ["xprojT", "dtprojT_sl", "dtb_sl", "alogp", "D_sb", "g_pat",
               "e_b", "e_c", "r_pat"],
    "l1ffn": ["w1T", "w3T", "w2T", "gateT", "gate_b", "oh8"],
    "l1attn": ["wqT", "wkT", "wvT", "woT", "wmqT", "wmkT", "bq", "bk", "bo", "bvR"],
}
_CONST_KEYS = ["ones_col", "ones_row", "ident"]



class Sem:
    def __init__(self, nc, name):
        self.h = nc.semaphore(name).__enter__()
        self.n = 0

    def inc(self, k=1):
        self.n += k
        return self.h


class Ctx:
    def __init__(self, nc):
        self.nc = nc
        self._lp = nc.allow_low_precision(reason="fp32r matmul operand tiles")
        self._lp.__enter__()
        self.P = [nc.alloc_psum_tensor(f"psP{i}", [128, 1024], F32) for i in range(4)]
        self.jC = nc.alloc_sbuf_tensor("ctx_jnk", [128, 64], F32)
        self.sems = {}

    def sem(self, name):
        if name not in self.sems:
            self.sems[name] = Sem(self.nc, name)
        return self.sems[name]


def sb_scoped(es, nc, name, shape, dt):
    return es.enter_context(nc.sbuf_tensor(name, shape, dt))


def dmai(eng, out, in_, s):
    eng.dma_start(out=out, in_=in_).then_inc(s.inc(16), 16)


def emit_load_consts(C, dd, sy, s_ld):
    nc = C.nc
    C.consts = {}
    C.consts["ones_col"] = nc.alloc_sbuf_tensor("c_onescol", [128, 1], F32R)
    dmai(sy, C.consts["ones_col"][:], dd["ones_col"].bitcast(F32R), s_ld)
    C.consts["ones_row"] = nc.alloc_sbuf_tensor("c_onesrow", [1, 128], F32R)
    dmai(sy, C.consts["ones_row"][:], dd["ones_row"].bitcast(F32R), s_ld)
    C.consts["ident"] = nc.alloc_sbuf_tensor("c_ident", [128, 128], F32)
    dmai(sy, C.consts["ident"][:], dd["ident"], s_ld)


def emit_ln(C, X, OUT, pfx):
    nc = C.nc
    P = C.P
    consts = C.consts
    s_sq = C.sem("ln_sq")
    s_sqf = C.sem("ln_sqf")
    b_sq, b_sqf = s_sq.n, s_sqf.n
    les = contextlib.ExitStack()
    sq = [sb_scoped(les, nc, f"{pfx}_sq{i}", [128, 1024], F32R) for i in range(2)]
    m_sb = sb_scoped(les, nc, f"{pfx}_m", [1, 1024], F32R)
    r_sb = sb_scoped(les, nc, f"{pfx}_r", [1, 1024], F32R)
    t_sb = sb_scoped(les, nc, f"{pfx}_t", [1, 1024], F32)
    v_sb = sb_scoped(les, nc, f"{pfx}_v", [1, 1024], F32)
    sd_sb = sb_scoped(les, nc, f"{pfx}_sd", [1, 1024], F32)
    nt = sb_scoped(les, nc, f"{pfx}_nt", [128, 1024], F32)

    with nc.Block() as blk:
        @blk.scalar
        def _(sc):
            for c in range(NCH):
                if c >= 2:
                    sc.wait_ge(s_sqf.h, b_sqf + c - 1)
                sc.activation(sq[c % 2][:], X[c][:].bitcast(F32), AF.Square) \
                    .then_inc(s_sq.inc(), 1)

        @blk.tensor
        def _(tn):
            for c in range(NCH):
                for hh in range(2):
                    cl = slice(hh * 512, (hh + 1) * 512)
                    tn.matmul(P[2][0:1, cl], consts["ones_col"][:], X[c][:, cl],
                              start=(c == 0), stop=(c == NCH - 1))
                tn.wait_ge(s_sq.h, b_sq + c + 1)
                for hh in range(2):
                    cl = slice(hh * 512, (hh + 1) * 512)
                    ins = tn.matmul(P[3][0:1, cl], consts["ones_col"][:], sq[c % 2][:, cl],
                                    start=(c == 0), stop=(c == NCH - 1))
                ins.then_inc(s_sqf.inc(), 1)

    with nc.Block() as blk:
        @blk.vector
        def _(v):
            v.tensor_scalar(m_sb[:], P[2][0:1, :], 1.0 / D, None, ALU.mult)
            v.tensor_scalar(v_sb[:], P[3][0:1, :], 1.0 / D, None, ALU.mult)
            v.tensor_tensor(t_sb[:], m_sb[:].bitcast(F32), m_sb[:].bitcast(F32), ALU.mult)
            v.tensor_tensor(v_sb[:], v_sb[:], t_sb[:], ALU.subtract)
            v.tensor_scalar(v_sb[:], v_sb[:], LN_EPS, None, ALU.add)
            v.memset(C.jC[:], 0.0)

    with nc.Block() as blk:
        @blk.scalar
        def _(sc):
            sc.activation(sd_sb[:], v_sb[:], AF.Sqrt)
            sc.memzero(C.jC[:])

    with nc.Block() as blk:
        @blk.vector
        def _(v):
            v.reciprocal(r_sb[:], sd_sb[:])
            v.memset(C.jC[:], 0.0)
            v.memset(C.jC[:], 0.0)

    with nc.Block() as blk:
        @blk.tensor
        def _(tn):
            for hh in range(2):
                cl = slice(hh * 512, (hh + 1) * 512)
                tn.matmul(P[2][:, cl], consts["ones_row"][:], m_sb[0:1, cl],
                          start=True, stop=True)
                tn.matmul(P[3][:, cl], consts["ones_row"][:], r_sb[0:1, cl],
                          start=True, stop=True)

    with nc.Block() as blk:
        @blk.vector
        def _(v):
            for c in range(NCH):
                v.tensor_tensor(nt[:], X[c][:].bitcast(F32), P[2][:], ALU.subtract)
                v.tensor_tensor(OUT[c][:], nt[:], P[3][:], ALU.mult)
            v.memset(C.jC[:], 0.0)
    les.close()


def emit_load_stream(C, X, d_x, sy, s_ld):
    xv = d_x.rearrange("(c p) t -> c p t", p=128)
    for c in range(NCH):
        dmai(sy, X[c][:], xv[c].bitcast(F32R), s_ld)


def emit_store_stream(C, X, d_y, s_st, waits=()):
    nc = C.nc
    with nc.Block() as blk:
        @blk.sync
        def _(sy):
            for s, val in waits:
                sy.wait_ge(s.h, val)
            yv = d_y.rearrange("(c p) t -> c p t", p=128)
            for c in range(NCH):
                dmai(sy, yv[c], X[c][:].bitcast(F32), s_st)
            sy.wait_ge(s_st.h, s_st.n)



def emit_l0ffn(C, X, dd, pfx):
    nc = C.nc
    P = C.P
    es = contextlib.ExitStack()
    sbs = lambda n, shp, dt: sb_scoped(es, nc, pfx + n, shp, dt)

    XN = [sbs(f"xn{c}", [128, 1024], F32R) for c in range(NCH)]
    dn_sb = sbs("dn", [128, 16 * 128], F32R)
    up_sb = sbs("up", [128, D], F32R)
    rt_sb = sbs("rt", [128, 16 * 8], F32R)
    m16_sb = sbs("m16", [8, 128], F32R)
    a_sb = sbs("a", [128, 1024], F32R)
    ag_sb = sbs("ag", [128, 1024], F32R)
    lg_sb = sbs("lg", [8, 1024], F32)
    sct = sbs("sct", [128, 64], F32)
    et = sbs("et", [128, 64], F32)
    ex_t = sbs("ext", [128, 64], F32)
    mx8 = sbs("mx8", [128, 8], F32)
    sm8 = sbs("sm8", [128, 8], F32)
    r8 = sbs("r8", [128, 8], F32)
    jnk = sbs("jnk", [128, 8], F32)
    m1 = sbs("m1", [128, 64], F32)
    m2 = sbs("m2", [128, 64], F32)
    sup = sbs("sup", [128, 64], F32)
    gate_t = sbs("gatet", [128, 64], F32)
    gate_e = sbs("gatee", [8, 1024], F32R)

    s_ld = C.sem("f0_ld")
    s_up = C.sem("f0_up")
    s_upf = C.sem("f0_upf")
    b_up, b_upf = s_up.n, s_upf.n

    with nc.Block() as blk:
        @blk.sync
        def _(sy):
            dmai(sy, dn_sb[:].rearrange("p (c m) -> p c m", c=16),
                 dd["dnT"].rearrange("(c p) m -> p c m", p=128).bitcast(F32R), s_ld)
            dmai(sy, up_sb[:], dd["upT"].bitcast(F32R), s_ld)
            dmai(sy, rt_sb[:].rearrange("p (c m) -> p c m", c=16),
                 dd["rtT"].rearrange("(c p) m -> p c m", p=128).bitcast(F32R), s_ld)
            dmai(sy, m16_sb[:], dd["m16e"].bitcast(F32R), s_ld)
            sy.wait_ge(s_ld.h, s_ld.n)

    emit_ln(C, X, XN, pfx + "ln")

    with nc.Block() as blk:
        @blk.tensor
        def _(tn):
            for c in range(NCH):
                for hh in range(2):
                    cl = slice(hh * 512, (hh + 1) * 512)
                    tn.matmul(P[0][0:8, cl], rt_sb[:, c * 8:(c + 1) * 8], XN[c][:, cl],
                              start=(c == 0), stop=(c == NCH - 1))
                    tn.matmul(P[1][:, cl], dn_sb[:, c * 128:(c + 1) * 128], XN[c][:, cl],
                              start=(c == 0), stop=(c == NCH - 1))

    with nc.Block() as blk:
        @blk.scalar
        def _(sc):
            sc.activation(a_sb[:], P[1][:], AF.Silu)
            sc.memzero(C.jC[:])

        @blk.vector
        def _(v):
            v.tensor_copy(lg_sb[:], P[0][0:8, :])
            v.memset(C.jC[:], 0.0)

    with nc.Block() as blk:
        @blk.tensor
        def _(tn):
            for t8 in range(8):
                tn.transpose(P[2][:, t8 * 8:(t8 + 1) * 8],
                             lg_sb[0:8, t8 * 128:(t8 + 1) * 128],
                             C.consts["ident"][0:8, 0:8])

    S8 = [slice(t8 * 8, (t8 + 1) * 8) for t8 in range(8)]
    with nc.Block() as blk:
        @blk.vector
        def _(v):
            s_hz = C.sem("hz")
            v.tensor_copy(sct[:], P[2][:, 0:64])
            for t8 in range(8):
                v.reduce_max(mx8[:, t8:t8 + 1], sct[:, S8[t8]], axis=AX) \
                    .then_inc(s_hz.inc(), 1)
            v.wait_ge(s_hz.h, s_hz.n)
            for t8 in range(8):
                v.tensor_scalar(et[:, S8[t8]], sct[:, S8[t8]], mx8[:, t8:t8 + 1],
                                None, ALU.subtract)
            v.memset(jnk[:], 0.0)

    with nc.Block() as blk:
        @blk.scalar
        def _(sc):
            sc.activation(ex_t[:], et[:], AF.Exp)
            sc.memzero(C.jC[:])

    with nc.Block() as blk:
        @blk.vector
        def _(v):
            s_hz = C.sem("hz")
            for t8 in range(8):
                v.reduce_sum(sm8[:, t8:t8 + 1], ex_t[:, S8[t8]], axis=AX) \
                    .then_inc(s_hz.inc(), 1)
            v.wait_ge(s_hz.h, s_hz.n)
            v.reciprocal(r8[:], sm8[:]).then_inc(s_hz.inc(), 1)
            v.wait_ge(s_hz.h, s_hz.n)
            for t8 in range(8):
                v.tensor_scalar(sct[:, S8[t8]], ex_t[:, S8[t8]], r8[:, t8:t8 + 1],
                                None, ALU.mult)
            for t8 in range(8):
                v.reduce_max(mx8[:, t8:t8 + 1], sct[:, S8[t8]], axis=AX) \
                    .then_inc(s_hz.inc(), 1)
            v.wait_ge(s_hz.h, s_hz.n)
            for t8 in range(8):
                v.tensor_scalar(m1[:, S8[t8]], sct[:, S8[t8]], mx8[:, t8:t8 + 1],
                                None, ALU.is_ge)
            for t8 in range(8):
                v.scalar_tensor_tensor(sup[:, S8[t8]], m1[:, S8[t8]], -1e30,
                                       sct[:, S8[t8]], ALU.mult, ALU.add)
            for t8 in range(8):
                v.reduce_max(mx8[:, t8:t8 + 1], sup[:, S8[t8]], axis=AX) \
                    .then_inc(s_hz.inc(), 1)
            v.wait_ge(s_hz.h, s_hz.n)
            for t8 in range(8):
                v.tensor_scalar(m2[:, S8[t8]], sup[:, S8[t8]], mx8[:, t8:t8 + 1],
                                None, ALU.is_ge)
            for t8 in range(8):
                v.tensor_tensor(m1[:, S8[t8]], m1[:, S8[t8]], m2[:, S8[t8]], ALU.add)
            for t8 in range(8):
                v.tensor_tensor(gate_t[:, S8[t8]], sct[:, S8[t8]], m1[:, S8[t8]], ALU.mult)
            v.memset(jnk[:], 0.0)

    with nc.Block() as blk:
        @blk.tensor
        def _(tn):
            for t8 in range(8):
                tn.transpose(P[2][0:8, t8 * 128:(t8 + 1) * 128],
                             gate_t[:, t8 * 8:(t8 + 1) * 8],
                             C.consts["ident"][:, 0:128])

    with nc.Block() as blk:
        @blk.vector
        def _(v):
            v.tensor_copy(gate_e[:], P[2][0:8, :])
            v.memset(C.jC[:], 0.0)

    with nc.Block() as blk:
        @blk.tensor
        def _(tn):
            for hh in range(2):
                cl = slice(hh * 512, (hh + 1) * 512)
                tn.matmul(P[3][:, cl], m16_sb[:], gate_e[0:8, cl], start=True, stop=True)

    with nc.Block() as blk:
        @blk.vector
        def _(v):
            v.tensor_tensor(ag_sb[:], a_sb[:].bitcast(F32), P[3][:], ALU.mult)
            v.memset(C.jC[:], 0.0)

    with nc.Block() as blk:
        @blk.tensor
        def _(tn):
            for m in range(NCH):
                if m >= 2:
                    tn.wait_ge(s_upf.h, b_upf + m - 1)
                for hh in range(2):
                    cl = slice(hh * 512, (hh + 1) * 512)
                    ins = tn.matmul(P[m % 2][:, cl], up_sb[:, m * 128:(m + 1) * 128],
                                    ag_sb[:, cl], start=True, stop=True)
                ins.then_inc(s_up.inc(), 1)

        @blk.vector
        def _(v):
            for m in range(NCH):
                v.wait_ge(s_up.h, b_up + m + 1)
                v.scalar_tensor_tensor(X[m][:], P[m % 2][:], 0.5, X[m][:].bitcast(F32),
                                       ALU.mult, ALU.add).then_inc(s_upf.inc(), 1)

    es.close()
    return [(s_upf, s_upf.n)]



def emit_l0attn(C, X, dd, pfx):
    nc = C.nc
    P = C.P
    es0 = contextlib.ExitStack()
    sbs = lambda n, shp, dt: sb_scoped(es0, nc, pfx + n, shp, dt)

    agx_in = nc.dram_tensor(pfx + "agx_in", [D, TOK], F32)
    agx_out = nc.dram_tensor(pfx + "agx_out", [NCORE * D, TOK], F32, addr_space="Shared")
    agd_in = nc.dram_tensor(pfx + "agd_in", [160, TOK], F32)
    agd_out = nc.dram_tensor(pfx + "agd_out", [NCORE * 160, TOK], F32, addr_space="Shared")
    y_mine = nc.dram_tensor(pfx + "y_mine", [8 * DSL, 1024], F32)
    y_full = nc.dram_tensor(pfx + "y_full", [8 * NCORE * DSL, 1024], F32,
                            addr_space="Shared")

    gp_sb = sbs("gp", [128, 16 * 128], F32R)
    eb_sb = sbs("eb", [32, 128], F32R)
    ec_sb = sbs("ec", [32, 128], F32R)
    rp_sb = sbs("rp", [128, 8], F32R)
    dtp_sb = sbs("dtp", [128, DSL], F32R)
    dtb_sb = sbs("dtb", [128, 2], F32)
    alog_sb = sbs("alog", [128, 32], F32)
    apat_sb = sbs("apat", [128, 32], F32)
    D_sb = sbs("Dsb", [128, 16], F32)
    carry = sbs("carry", [128, 32], F32)

    s_ld = C.sem("m_ld")
    s_ag = C.sem("m_ag")
    s_cc = C.sem("m_cc")
    s_u = C.sem("m_u"); s_uf = C.sem("m_uf")
    s_dr = C.sem("m_dr")
    s_gdp = C.sem("m_gdp"); s_gda = C.sem("m_gda")
    s_gup = C.sem("m_gup"); s_gbu = C.sem("m_gbu")
    s_gsc = C.sem("m_gsc"); s_gt0 = C.sem("m_gt0"); s_gt1 = C.sem("m_gt1")
    s_za = C.sem("m_za"); s_ab = C.sem("m_ab"); s_ln2 = C.sem("m_ln2")
    s_sc1 = C.sem("m_sc1")
    s_gym = C.sem("m_gym"); s_gye = C.sem("m_gye"); s_gyd = C.sem("m_gyd")
    s_yl = C.sem("m_yl")
    s_fin = C.sem("m_fin")
    b = {s: s.n for s in (s_u, s_uf, s_gdp, s_gda, s_gup, s_gbu, s_gsc, s_gt0,
                          s_gt1, s_sc1, s_gym, s_gye, s_gyd, s_yl, s_cc, s_fin)}

    with nc.Block() as blk:
        @blk.sync
        def _(sy):
            dmai(sy, gp_sb[:].rearrange("d (g p) -> d g p", g=16),
                 dd["g_pat"].rearrange("(g d) p -> d g p", d=128).bitcast(F32R), s_ld)
            dmai(sy, eb_sb[:], dd["e_b"].bitcast(F32R), s_ld)
            dmai(sy, ec_sb[:], dd["e_c"].bitcast(F32R), s_ld)
            dmai(sy, rp_sb[:], dd["r_pat"].bitcast(F32R), s_ld)
            dmai(sy, dtp_sb[:], dd["dtprojT_sl"].bitcast(F32R), s_ld)
            dmai(sy, dtb_sb[:], dd["dtb_sl"], s_ld)
            dmai(sy, alog_sb[:], dd["alogp"], s_ld)
            dmai(sy, D_sb[:], dd["D_sb"], s_ld)
            sy.wait_ge(s_ld.h, s_ld.n)

    with nc.Block() as blk:
        @blk.scalar
        def _(sc):
            sc.activation(apat_sb[:], alog_sb[:], AF.Exp)
            sc.memzero(C.jC[:])

    with nc.Block() as blk:
        @blk.vector
        def _(v):
            v.tensor_scalar(apat_sb[:], apat_sb[:], -1.0, None, ALU.mult)
            v.memset(C.jC[:], 0.0)

    es1 = contextlib.ExitStack()
    XN = [sb_scoped(es1, nc, f"{pfx}xn{c}", [128, 1024], F32R) for c in range(NCH)]
    emit_ln(C, X, XN, pfx + "ln")

    xpj_sb = sb_scoped(es1, nc, pfx + "xpj", [128, 16 * 160], F32R)
    dr_sb = sb_scoped(es1, nc, pfx + "drs", [128, 1024], F32R)
    bc_sb = sb_scoped(es1, nc, pfx + "bcs", [32, 1024], F32R)

    with nc.Block() as blk:
        @blk.sync
        def _(sy):
            dmai(sy, xpj_sb[:].rearrange("p (c m) -> p c m", c=16),
                 dd["xprojT"].rearrange("(c p) m -> p c m", p=128).bitcast(F32R), s_ld)
            sy.wait_ge(s_ld.h, s_ld.n)

    with nc.Block() as blk:
        @blk.tensor
        def _(tn):
            for c in range(NCH):
                for hh in range(2):
                    cl = slice(hh * 512, (hh + 1) * 512)
                    tn.matmul(P[0][:, cl], xpj_sb[:, c * 160:c * 160 + 128],
                              XN[c][:, cl], start=(c == 0), stop=(c == NCH - 1))
                    tn.matmul(P[1][0:32, cl], xpj_sb[:, c * 160 + 128:c * 160 + 160],
                              XN[c][:, cl], start=(c == 0), stop=(c == NCH - 1))


    with nc.Block() as blk:
        @blk.vector
        def _(v):
            v.tensor_copy(dr_sb[:], P[0][:])
            v.tensor_copy(bc_sb[:], P[1][0:32, :])
            v.memset(C.jC[:], 0.0)

    with nc.Block() as blk:
        @blk.sync
        def _(sy):
            xv = agx_in.ap().rearrange("(c p) t -> c p t", p=128)
            for c in range(NCH):
                dmai(sy, xv[c], XN[c][:].bitcast(F32), s_ag)
            dmai(sy, agd_in.ap()[0:128, :], dr_sb[:].bitcast(F32), s_ag)
            dmai(sy, agd_in.ap()[128:160, :], bc_sb[:].bitcast(F32), s_ag)

        @blk.gpsimd
        def _(g):
            g.wait_ge(s_ag.h, s_ag.n)
            g.collective_compute("AllGather", ALU.bypass,
                                 replica_groups=[list(range(NCORE))],
                                 ins=[agx_in.ap()], outs=[agx_out.ap()]) \
                .then_inc(s_cc.inc(), 1)
            g.collective_compute("AllGather", ALU.bypass,
                                 replica_groups=[list(range(NCORE))],
                                 ins=[agd_in.ap()], outs=[agd_out.ap()]) \
                .then_inc(s_cc.inc(), 1)
            g.wait_ge(s_cc.h, b[s_cc] + 2)

    es1.close()

    es2 = contextlib.ExitStack()
    sbt = lambda n, shp, dt: sb_scoped(es2, nc, pfx + n, shp, dt)
    u_t = [[sbt(f"u{s}_{dh}", [128, 1024], F32) for dh in range(2)] for s in range(2)]
    drt = [sbt(f"drt{s}", [128, 1024], F32R) for s in range(2)]
    bct = [sbt(f"bct{s}", [32, 1024], F32R) for s in range(2)]
    zt = [sbt(f"zt{dh}", [128, 1024], F32) for dh in range(2)]
    ab2 = [sbt(f"ab2{dh}", [128, 1024], F32) for dh in range(2)]
    el = [sbt(f"el{dh}", [128, 1024], F32) for dh in range(2)]
    delta = [sbt(f"dl{dh}", [128, 1024], F32R) for dh in range(2)]
    du = [sbt(f"du{dh}", [128, 1024], F32R) for dh in range(2)]
    Bp = sbt("Bp", [128, 1024], F32)
    Cp = sbt("Cp", [128, 1024], F32)
    dA = [sbt(f"dA{s}", [128, 1024], F32) for s in range(2)]
    dBu = [sbt(f"dBu{s}", [128, 1024], F32) for s in range(2)]
    h_t = [sbt(f"h{s}", [128, 1024], F32) for s in range(3)]
    tmp = [sbt(f"tmp{s}", [128, 1024], F32R) for s in range(3)]
    y_sb = [sbt(f"ysb{s}", [8, 1024], F32) for s in range(2)]

    NTC = 8
    for tc in range(NTC):
        for arm in nc.switch_core_id(engines=[nc.sync], n=128):
            try:
                rank = int(arm.logical) % NCORE
            except Exception:
                rank = 0
            if tc >= 2:
                nc.sync.wait_ge(s_uf.h, b[s_uf] + tc - 1)
            for dh in range(2):
                nc.sync.dma_start(
                    out=u_t[tc % 2][dh][:],
                    in_=agx_out.ap()[tc * D + rank * DSL + dh * 128:
                                     tc * D + rank * DSL + (dh + 1) * 128, :]) \
                    .then_inc(s_u.h, 16)
        s_u.n += 32

        with nc.Block() as blk:
            @blk.sync
            def _(sy):
                dmai(sy, drt[tc % 2][:], agd_out.ap()[tc * 160:tc * 160 + 128, :]
                     .bitcast(F32R), s_dr)
                dmai(sy, bct[tc % 2][:], agd_out.ap()[tc * 160 + 128:tc * 160 + 160, :]
                     .bitcast(F32R), s_dr)
                sy.wait_ge(s_dr.h, s_dr.n)

        with nc.Block() as blk:
            @blk.tensor
            def _(tn):
                for dh in range(2):
                    for hh in range(2):
                        cl = slice(hh * 512, (hh + 1) * 512)
                        tn.matmul(P[dh][:, cl], dtp_sb[:, dh * 128:(dh + 1) * 128],
                                  drt[tc % 2][:, cl], start=True, stop=True)
                for hh in range(2):
                    cl = slice(hh * 512, (hh + 1) * 512)
                    tn.matmul(P[2][:, cl], eb_sb[:], bct[tc % 2][0:32, cl],
                              start=True, stop=True)
                    tn.matmul(P[3][:, cl], ec_sb[:], bct[tc % 2][0:32, cl],
                              start=True, stop=True)

        bz = {s: s.n for s in (s_za, s_ab, s_ln2)}
        with nc.Block() as blk:
            @blk.scalar
            def _(sc):
                for dh in range(2):
                    sc.activation(zt[dh][:], P[dh][:], AF.Identity,
                                  bias=dtb_sb[:, dh:dh + 1]).then_inc(s_za.inc(), 1)
                for dh in range(2):
                    sc.wait_ge(s_ab.h, bz[s_ab] + dh + 1)
                    sc.activation(el[dh][:], ab2[dh][:], AF.Exp, scale=-1.0)
                for dh in range(2):
                    sc.activation(ab2[dh][:], el[dh][:], AF.Ln, bias=1.0) \
                        .then_inc(s_ln2.inc(), 1)
                sc.memzero(C.jC[:])

            @blk.vector
            def _(v):
                for dh in range(2):
                    v.wait_ge(s_za.h, bz[s_za] + dh + 1)
                    v.scalar_tensor_tensor(ab2[dh][:], zt[dh][:], -1.0, zt[dh][:],
                                           ALU.mult, ALU.max).then_inc(s_ab.inc(), 1)
                v.tensor_copy(Bp[:], P[2][:])
                v.tensor_copy(Cp[:], P[3][:])
                v.wait_ge(s_u.h, b[s_u] + 32 * (tc + 1))
                for dh in range(2):
                    v.wait_ge(s_ln2.h, bz[s_ln2] + dh + 1)
                    v.scalar_tensor_tensor(delta[dh][:], zt[dh][:], 0.0, ab2[dh][:],
                                           ALU.max, ALU.add)
                    ins = v.tensor_tensor(du[dh][:], delta[dh][:].bitcast(F32),
                                          u_t[tc % 2][dh][:], ALU.mult)
                    if dh == 1:
                        ins.then_inc(s_uf.inc(), 1)
                v.memset(C.jC[:], 0.0)

        with nc.Block() as blk:
            @blk.tensor
            def _(tn):
                for g in range(33):
                    gq = tc * 32 + g
                    if g < 32:
                        dh = g // 16
                        gl = g % 16
                        for hh in range(2):
                            q2 = 2 * gq + hh
                            cl = slice(hh * 512, (hh + 1) * 512)
                            if q2 >= 2:
                                tn.wait_ge(s_gda.h, b[s_gda] + q2 - 1)
                            tn.matmul(P[0][:, cl], gp_sb[:, gl * 128:(gl + 1) * 128],
                                      delta[dh][:, cl], start=True, stop=True) \
                                .then_inc(s_gdp.inc(), 1)
                            if q2 >= 2:
                                tn.wait_ge(s_gbu.h, b[s_gbu] + q2 - 1)
                            tn.matmul(P[1][:, cl], gp_sb[:, gl * 128:(gl + 1) * 128],
                                      du[dh][:, cl], start=True, stop=True) \
                                .then_inc(s_gup.inc(), 1)
                    if g >= 1:
                        gy = gq - 1
                        if gy >= 2:
                            tn.wait_ge(s_gye.h, b[s_gye] + gy - 1)
                        for hh in range(2):
                            cl = slice(hh * 512, (hh + 1) * 512)
                            sh = s_gt0 if hh == 0 else s_gt1
                            tn.wait_ge(sh.h, b[sh] + gy + 1)
                            ins = tn.matmul(P[2 + (gy % 2)][0:8, cl], rp_sb[:],
                                            tmp[gy % 3][:, cl], start=True, stop=True)
                        ins.then_inc(s_gym.inc(), 1)

            @blk.scalar
            def _(sc):
                for g in range(33):
                    gq = tc * 32 + g
                    if g < 32:
                        for hh in range(2):
                            q2 = 2 * gq + hh
                            cl = slice(hh * 512, (hh + 1) * 512)
                            sc.wait_ge(s_gdp.h, b[s_gdp] + q2 + 1)
                            if gq >= 2:
                                sc.wait_ge(s_sc1.h, b[s_sc1] + gq - 1)
                            sc.activation(dA[gq % 2][:, cl], P[0][:, cl], AF.Exp,
                                          scale=apat_sb[:, g:g + 1]) \
                                .then_inc(s_gda.inc(), 1)
                    if g >= 1:
                        gy = gq - 1
                        sc.wait_ge(s_gym.h, b[s_gym] + gy + 1)
                        if gy >= 2:
                            sc.wait_ge(s_gyd.h, b[s_gyd] + 16 * (gy - 1))
                        sc.activation(y_sb[gy % 2][:], P[2 + (gy % 2)][0:8, :],
                                      AF.Identity).then_inc(s_gye.inc(), 1)

            @blk.vector
            def _(v):
                for g in range(33):
                    gq = tc * 32 + g
                    if g < 32:
                        for hh in range(2):
                            q2 = 2 * gq + hh
                            cl = slice(hh * 512, (hh + 1) * 512)
                            v.wait_ge(s_gup.h, b[s_gup] + q2 + 1)
                            if gq >= 2:
                                v.wait_ge(s_sc1.h, b[s_sc1] + gq - 1)
                            v.tensor_tensor(dBu[gq % 2][:, cl], P[1][:, cl], Bp[:, cl],
                                            ALU.mult).then_inc(s_gbu.inc(), 1)
                    if g >= 1:
                        gp_ = gq - 1
                        v.wait_ge(s_sc1.h, b[s_sc1] + gp_ + 1)
                        if gp_ >= 3:
                            v.wait_ge(s_gym.h, b[s_gym] + gp_ - 2)
                        v.tensor_tensor(tmp[gp_ % 3][:, 0:512], h_t[gp_ % 3][:, 0:512],
                                        Cp[:, 0:512], ALU.mult).then_inc(s_gt0.inc(), 1)
                    if g < 32:
                        v.wait_ge(s_gda.h, b[s_gda] + 2 * gq + 2)
                        if gq >= 3:
                            v.wait_ge(s_gt1.h, b[s_gt1] + gq - 2)
                        if tc not in (0, 4):
                            v.wait_ge(s_gsc.h, b[s_gsc] + gq - 31)
                        ini = 0.0 if tc in (0, 4) else carry[:, g:g + 1]
                        v.tensor_tensor_scan(h_t[gq % 3][:], dA[gq % 2][:],
                                             dBu[gq % 2][:], ini, ALU.mult, ALU.add) \
                            .then_inc(s_sc1.inc(), 1)

            @blk.gpsimd
            def _(gp):
                for g in range(32):
                    gq = tc * 32 + g
                    gp.wait_ge(s_sc1.h, b[s_sc1] + gq + 1)
                    gp.tensor_copy(carry[:, g:g + 1], h_t[gq % 3][:, 1023:1024]) \
                        .then_inc(s_gsc.inc(), 1)
                    if gq >= 3:
                        gp.wait_ge(s_gym.h, b[s_gym] + gq - 2)
                    gp.tensor_tensor(tmp[gq % 3][:, 512:1024], h_t[gq % 3][:, 512:1024],
                                     Cp[:, 512:1024], ALU.mult).then_inc(s_gt1.inc(), 1)
                gp.wait_ge(s_gyd.h, b[s_gyd] + (tc + 1) * 512)
                gp.collective_compute("AllGather", ALU.bypass,
                                      replica_groups=[list(range(NCORE))],
                                      ins=[y_mine.ap()[tc * DSL:(tc + 1) * DSL, :]],
                                      outs=[y_full.ap()[tc * NCORE * DSL:
                                                        (tc + 1) * NCORE * DSL, :]]) \
                    .then_inc(s_cc.inc(), 1)

            @blk.sync
            def _(sy):
                for g in range(32):
                    gq = tc * 32 + g
                    sy.wait_ge(s_gye.h, b[s_gye] + gq + 1)
                    sy.dma_start(out=y_mine.ap()[tc * DSL + 8 * g:tc * DSL + 8 * g + 8, :],
                                 in_=y_sb[gq % 2][:]).then_inc(s_gyd.inc(16), 16)

    with nc.Block() as blk:
        @blk.gpsimd
        def _(g):
            g.wait_ge(s_cc.h, b[s_cc] + 10)

    es2.close()
    es3 = contextlib.ExitStack()
    yloc = [sb_scoped(es3, nc, f"{pfx}yl{c}", [128, 1024], F32) for c in range(NCH)]
    xnl = [sb_scoped(es3, nc, f"{pfx}xnl{c}", [128, 1024], F32) for c in range(NCH)]

    for arm in nc.switch_core_id(engines=[nc.sync], n=128):
        try:
            rank = int(arm.logical) % NCORE
        except Exception:
            rank = 0
        for c in range(NCH):
            nc.sync.dma_start(out=yloc[c][:],
                              in_=y_full.ap()[rank * NCORE * DSL + c * 128:
                                              rank * NCORE * DSL + (c + 1) * 128, :]) \
                .then_inc(s_yl.h, 16)
            nc.sync.dma_start(out=xnl[c][:],
                              in_=agx_out.ap()[rank * D + c * 128:
                                               rank * D + (c + 1) * 128, :]) \
                .then_inc(s_yl.h, 16)
    s_yl.n += 32 * NCH

    yf = sb_scoped(es3, nc, pfx + "yf", [128, 1024], F32)
    with nc.Block() as blk:
        @blk.vector
        def _(v):
            v.wait_ge(s_yl.h, s_yl.n)
            for c in range(NCH):
                v.scalar_tensor_tensor(yf[:], xnl[c][:], D_sb[:, c:c + 1],
                                       yloc[c][:], ALU.mult, ALU.add)
                v.tensor_tensor(X[c][:], X[c][:].bitcast(F32), yf[:], ALU.add) \
                    .then_inc(s_fin.inc(), 1)

    es3.close()
    es0.close()
    return [(s_fin, s_fin.n)]



def emit_l1ffn(C, X, dd, pfx):
    nc = C.nc
    P = C.P
    es = contextlib.ExitStack()
    sbs = lambda n, shp, dt: sb_scoped(es, nc, pfx + n, shp, dt)

    XNb = [sbs(f"xnb{c}", [128, 1024], BF16) for c in range(NCH)]
    gt_sb = sbs("gt", [128, 16 * 8], BF16)
    gb_sb = sbs("gb", [1, 8], F32R)
    oh8_sb = sbs("oh8s", [8, 8 * 128], F32R)
    sc_sb = sbs("scs", [8, 1024], F32)
    sct = sbs("sct", [128, 64], F32)
    ab = sbs("ab", [128, 64], F32)
    lb = sbs("lb", [128, 64], F32)
    ssq = sbs("ssq", [128, 64], F32)
    bt = sbs("bt", [128, 64], F32)
    m1 = sbs("m1", [128, 64], F32)
    m2 = sbs("m2", [128, 64], F32)
    sup = sbs("sup", [128, 64], F32)
    mx8 = sbs("mx8", [128, 8], F32)
    cw_t = sbs("cwt", [128, 64], F32)
    cwe_sb = sbs("cwe", [8, 1024], F32R)
    cwb_sb = sbs("cwb", [128, 1024], F32)
    th = [sbs(f"th{i}", [128, 1024], F32) for i in range(2)]
    hbt = sbs("hbt", [128, 1024], F32)
    hb = [sbs(f"hb{i}", [128, 1024], BF16) for i in range(NI)]
    w13 = [sbs(f"w13_{i}", [128, 16 * 128], BF16) for i in range(4)]
    w2b = [sbs(f"w2_{i}", [128, NI * 128], BF16) for i in range(2)]

    s_ld = C.sem("g_ld")
    s_wl = C.sem("g_wl")
    s_wf13 = C.sem("g_wf13")
    s_sl = C.sem("g_sl")
    s_tt1 = C.sem("g_tt1")
    s_hb = C.sem("g_hb")
    s_o = C.sem("g_o")
    s_ox = C.sem("g_ox")
    s_cwm = C.sem("g_cwm")
    s_cwb = C.sem("g_cwb")
    bS = {s: s.n for s in (s_wl, s_wf13, s_sl, s_tt1, s_hb, s_o, s_ox, s_cwm, s_cwb)}

    with nc.Block() as blk:
        @blk.sync
        def _(sy):
            dmai(sy, gb_sb[:], dd["gate_b"].bitcast(F32R), s_ld)
            dmai(sy, oh8_sb[:], dd["oh8"].bitcast(F32R), s_ld)
            sy.wait_ge(s_ld.h, s_ld.n)

        @blk.gpsimd
        def _(g):
            g.dma_start(out=gt_sb[:].rearrange("p (c m) -> p c m", c=16),
                        in_=dd["gateT"].rearrange("(c p) m -> p c m", p=128)) \
                .then_inc(s_ld.inc(16), 16)
            g.wait_ge(s_ld.h, s_ld.n)

    emit_ln(C, X, XNb, pfx + "ln")

    with nc.Block() as blk:
        @blk.tensor
        def _(tn):
            for c in range(NCH):
                for hh in range(2):
                    cl = slice(hh * 512, (hh + 1) * 512)
                    tn.matmul(P[0][0:8, cl], gt_sb[:, c * 8:(c + 1) * 8], XNb[c][:, cl],
                              start=(c == 0), stop=(c == NCH - 1))

    with nc.Block() as blk:
        @blk.vector
        def _(v):
            v.tensor_copy(sc_sb[:], P[0][0:8, :])
            v.memset(C.jC[:], 0.0)

    with nc.Block() as blk:
        @blk.tensor
        def _(tn):
            for t8 in range(8):
                tn.transpose(P[1][:, t8 * 8:(t8 + 1) * 8],
                             sc_sb[0:8, t8 * 128:(t8 + 1) * 128],
                             C.consts["ident"][0:8, 0:8])
            tn.matmul(P[2][:, 0:8], C.consts["ones_row"][:], gb_sb[:], start=True, stop=True)

    with nc.Block() as blk:
        @blk.vector
        def _(v):
            v.tensor_copy(sct[:], P[1][:, 0:64])
            v.scalar_tensor_tensor(ab[:], sct[:], -1.0, sct[:], ALU.mult, ALU.max)
            v.memset(C.jC[:], 0.0)

    with nc.Block() as blk:
        @blk.scalar
        def _(sc):
            sc.activation(lb[:], ab[:], AF.Exp, scale=-1.0)
            sc.memzero(C.jC[:])

    with nc.Block() as blk:
        @blk.scalar
        def _(sc):
            sc.activation(ab[:], lb[:], AF.Ln, bias=1.0)
            sc.memzero(C.jC[:])

    with nc.Block() as blk:
        @blk.vector
        def _(v):
            v.scalar_tensor_tensor(lb[:], sct[:], 0.0, ab[:], ALU.max, ALU.add)
            v.memset(C.jC[:], 0.0)

    with nc.Block() as blk:
        @blk.scalar
        def _(sc):
            sc.activation(ssq[:], lb[:], AF.Sqrt)
            sc.memzero(C.jC[:])

    S8 = [slice(t8 * 8, (t8 + 1) * 8) for t8 in range(8)]
    with nc.Block() as blk:
        @blk.vector
        def _(v):
            s_hz = C.sem("hz")
            for t8 in range(8):
                v.tensor_tensor(bt[:, S8[t8]], ssq[:, S8[t8]], P[2][:, 0:8], ALU.add)
            for t8 in range(8):
                v.reduce_max(mx8[:, t8:t8 + 1], bt[:, S8[t8]], axis=AX) \
                    .then_inc(s_hz.inc(), 1)
            v.wait_ge(s_hz.h, s_hz.n)
            for t8 in range(8):
                v.tensor_scalar(m1[:, S8[t8]], bt[:, S8[t8]], mx8[:, t8:t8 + 1],
                                None, ALU.is_ge)
            for t8 in range(8):
                v.scalar_tensor_tensor(sup[:, S8[t8]], m1[:, S8[t8]], -1e30,
                                       bt[:, S8[t8]], ALU.mult, ALU.add)
            for t8 in range(8):
                v.reduce_max(mx8[:, t8:t8 + 1], sup[:, S8[t8]], axis=AX) \
                    .then_inc(s_hz.inc(), 1)
            v.wait_ge(s_hz.h, s_hz.n)
            for t8 in range(8):
                v.tensor_scalar(m2[:, S8[t8]], sup[:, S8[t8]], mx8[:, t8:t8 + 1],
                                None, ALU.is_ge)
            for t8 in range(8):
                v.tensor_tensor(m1[:, S8[t8]], m1[:, S8[t8]], m2[:, S8[t8]], ALU.add)
            for t8 in range(8):
                v.scalar_tensor_tensor(cw_t[:, S8[t8]], m1[:, S8[t8]], 0.5,
                                       ssq[:, S8[t8]], ALU.mult, ALU.mult)
            v.memset(C.jC[:], 0.0)

    with nc.Block() as blk:
        @blk.tensor
        def _(tn):
            for t8 in range(8):
                tn.transpose(P[1][0:8, t8 * 128:(t8 + 1) * 128],
                             cw_t[:, t8 * 8:(t8 + 1) * 8],
                             C.consts["ident"][:, 0:128])

    with nc.Block() as blk:
        @blk.vector
        def _(v):
            v.tensor_copy(cwe_sb[:], P[1][0:8, :])
            v.memset(C.jC[:], 0.0)

    elist = list(range(NEXP)) if _EL1 is None else list(_EL1)
    nrouted_before = [sum(1 for x in elist[:k] if x < 8) for k in range(len(elist) + 1)]
    with nc.Block() as blk:
        @blk.sync
        def _(sy):
            for ei, e in enumerate(elist):
                for i in range(NI):
                    for wd, nm in ((dd["w1T"], 0), (dd["w3T"], 1)):
                        j13 = ei * 22 + 2 * i + nm
                        if j13 >= 4:
                            sy.wait_ge(s_wf13.h, bS[s_wf13] + j13 - 3)
                        dmai(sy, w13[j13 % 4][:].rearrange("p (c m) -> p c m", c=16),
                             wd[e, :, i * 128:(i + 1) * 128]
                             .rearrange("(c p) m -> p c m", p=128), s_wl)
                for ddx in range(NCH):
                    j2 = ei * 16 + ddx
                    if j2 >= 2:
                        sy.wait_ge(s_o.h, bS[s_o] + j2 - 1)
                    dmai(sy, w2b[j2 % 2][:].rearrange("p (c m) -> p c m", c=NI),
                         dd["w2T"][e, :, ddx * 128:(ddx + 1) * 128]
                         .rearrange("(c p) m -> p c m", p=128), s_wl)

        @blk.tensor
        def _(tn):
            jg = 0
            for ei, e in enumerate(elist):
                routed = e < 8
                if routed:
                    if ei > 0:
                        tn.wait_ge(s_hb.h, bS[s_hb] + 11 * ei)
                        tn.wait_ge(s_ox.h, bS[s_ox] + 16 * ei)
                    for hh in range(2):
                        cl = slice(hh * 512, (hh + 1) * 512)
                        ins = tn.matmul(P[2][:, cl], oh8_sb[0:8, e * 128:(e + 1) * 128],
                                        cwe_sb[0:8, cl], start=True, stop=True)
                    ins.then_inc(s_cwm.inc(), 1)
                elif ei > 0:
                    tn.wait_ge(s_hb.h, bS[s_hb] + 11 * ei)
                    tn.wait_ge(s_ox.h, bS[s_ox] + 16 * ei)
                for i in range(NI):
                    for nm in range(2):
                        tn.wait_ge(s_wl.h, bS[s_wl] + 16 * (jg + 1))
                        slot = w13[(ei * 22 + 2 * i + nm) % 4]
                        dst = P[2 * (i % 2) + nm]
                        if i >= 2 and nm == 0:
                            tn.wait_ge(s_sl.h, bS[s_sl] + 11 * ei + i - 1)
                            tn.wait_ge(s_tt1.h, bS[s_tt1] + 11 * ei + i - 1)
                        if i == 1 and nm == 0 and routed:
                            tn.wait_ge(s_cwb.h, bS[s_cwb] + nrouted_before[ei] + 1)
                        for k in range(NCH):
                            for hh in range(2):
                                cl = slice(hh * 512, (hh + 1) * 512)
                                ins = tn.matmul(dst[:, cl], slot[:, k * 128:(k + 1) * 128],
                                                XNb[k][:, cl], start=(k == 0), stop=(k == NCH - 1))
                        ins.then_inc(s_wf13.inc(), 1)
                        jg += 1
                tn.wait_ge(s_hb.h, bS[s_hb] + 11 * (ei + 1))
                for ddx in range(NCH):
                    tn.wait_ge(s_wl.h, bS[s_wl] + 16 * (jg + 1))
                    if ddx >= 2:
                        tn.wait_ge(s_ox.h, bS[s_ox] + 16 * ei + ddx - 1)
                    slot = w2b[(ei * 16 + ddx) % 2]
                    dst = P[ddx % 2]
                    for k in range(NI):
                        for hh in range(2):
                            cl = slice(hh * 512, (hh + 1) * 512)
                            ins = tn.matmul(dst[:, cl], slot[:, k * 128:(k + 1) * 128],
                                            hb[k][:, cl], start=(k == 0), stop=(k == NI - 1))
                    ins.then_inc(s_o.inc(), 1)
                    jg += 1

        @blk.scalar
        def _(sc):
            for ei, e in enumerate(elist):
                for i in range(NI):
                    gi = 11 * ei + i
                    sc.wait_ge(s_wf13.h, bS[s_wf13] + 22 * ei + 2 * i + 1)
                    if gi >= 2:
                        sc.wait_ge(s_tt1.h, bS[s_tt1] + gi - 1)
                    sc.activation(th[gi % 2][:], P[2 * (i % 2)][:], AF.Silu) \
                        .then_inc(s_sl.inc(), 1)

        @blk.vector
        def _(v):
            for ei, e in enumerate(elist):
                routed = e < 8
                if routed:
                    v.wait_ge(s_cwm.h, bS[s_cwm] + nrouted_before[ei] + 1)
                    v.tensor_copy(cwb_sb[:], P[2][:]).then_inc(s_cwb.inc(), 1)
                for i in range(NI):
                    gi = 11 * ei + i
                    v.wait_ge(s_sl.h, bS[s_sl] + gi + 1)
                    v.wait_ge(s_wf13.h, bS[s_wf13] + 22 * ei + 2 * i + 2)
                    v.tensor_tensor(hbt[:], th[gi % 2][:], P[2 * (i % 2) + 1][:],
                                    ALU.mult).then_inc(s_tt1.inc(), 1)
                    if routed:
                        v.tensor_tensor(hb[i][:], hbt[:], cwb_sb[:], ALU.mult) \
                            .then_inc(s_hb.inc(), 1)
                    else:
                        v.tensor_scalar(hb[i][:], hbt[:], 0.5, None, ALU.mult) \
                            .then_inc(s_hb.inc(), 1)
                for ddx in range(NCH):
                    v.wait_ge(s_o.h, bS[s_o] + 16 * ei + ddx + 1)
                    v.tensor_tensor(X[ddx][:], X[ddx][:].bitcast(F32), P[ddx % 2][:],
                                    ALU.add).then_inc(s_ox.inc(), 1)

    es.close()
    return [(s_ox, s_ox.n)]



def emit_l1attn(C, X, dd, pfx):
    nc = C.nc
    P = C.P
    es = contextlib.ExitStack()
    sbs = lambda n, shp, dt: sb_scoped(es, nc, pfx + n, shp, dt)

    ar_in = nc.dram_tensor(pfx + "ar_in", [128, 1040], F32)
    ar_out = nc.dram_tensor(pfx + "ar_out", [128, 1040], F32)

    qe = [sbs(f"qe{j}", [128, 1024], BF16) for j in range(8)]
    ke = [sbs(f"ke{j}", [128, 1024], BF16) for j in range(8)]
    kv_sb = sbs("kv", [128, 1024], F32R)
    ks_bd = sbs("ksbd", [128, 16], F32R)
    kst = sbs("kst", [128, 8], F32)
    rcp_t = sbs("rcpt", [1, 1024], F32)
    rcp = sbs("rcp", [1, 1024], F32R)
    wmq_sb = sbs("wmq", [128, 64], BF16)
    wmk_sb = sbs("wmk", [128, 64], BF16)
    bq_sb = sbs("bqs", [128, 16], F32)
    bk_sb = sbs("bks", [128, 16], F32)
    bo_sb = sbs("bos", [128, 16], F32)
    bv_bf = sbs("bvbf", [1, D], BF16)
    orow_bf = sbs("orowbf", [1, 128], BF16)
    ident_bf = sbs("identbf", [128, 128], BF16)

    s_ld = C.sem("a_ld")
    s_w = C.sem("a_w")
    s_pj = C.sem("a_pj"); s_bh = C.sem("a_bh")
    s_qm = C.sem("a_qm"); s_e1 = C.sem("a_e1"); s_e2 = C.sem("a_e2")
    s_elu = C.sem("a_elu")
    s_vm = C.sem("a_vm"); s_ve = C.sem("a_ve"); s_vw = C.sem("a_vw")
    s_tp = C.sem("a_tp"); s_te = C.sem("a_te")
    s_cc = C.sem("a_cc"); s_ar = C.sem("a_ar")
    s_nm = C.sem("a_nm"); s_nr = C.sem("a_nr")
    s_am = C.sem("a_am"); s_ae = C.sem("a_ae"); s_ad = C.sem("a_ad")
    s_om = C.sem("a_om"); s_oe = C.sem("a_oe"); s_od = C.sem("a_od")
    bA = {s: s.n for s in (s_vm, s_ve, s_vw, s_tp, s_te, s_cc, s_nm, s_nr,
                           s_am, s_ae, s_ad, s_om, s_oe, s_od)}

    with nc.Block() as blk:
        @blk.sync
        def _(sy):
            for dst, srcd in ((bq_sb, dd["bq"]), (bk_sb, dd["bk"]), (bo_sb, dd["bo"])):
                dmai(sy, dst[:], srcd, s_ld)
            sy.wait_ge(s_ld.h, s_ld.n)

        @blk.gpsimd
        def _(g):
            g.dma_start(out=wmq_sb[:], in_=dd["wmqT"]).then_inc(s_ld.inc(16), 16)
            g.dma_start(out=wmk_sb[:], in_=dd["wmkT"]).then_inc(s_ld.inc(16), 16)
            g.dma_start(out=bv_bf[:], in_=dd["bvR"]).then_inc(s_ld.inc(16), 16)
            g.wait_ge(s_ld.h, s_ld.n)

    with nc.Block() as blk:
        @blk.gpsimd
        def _(g):
            g.dma_start(out=orow_bf[:], in_=C.consts["ones_row"][:].bitcast(F32)) \
                .then_inc(s_ld.inc(16), 16)
            g.dma_start(out=ident_bf[:], in_=C.consts["ident"][:]).then_inc(s_ld.inc(16), 16)
            g.wait_ge(s_ld.h, s_ld.n)

        @blk.vector
        def _(v):
            v.memset(C.jC[:], 0.0)
            v.memset(C.jC[:], 0.0)
            v.tensor_scalar(ks_bd[:], C.jC[:, 0:16], 0.0, None, ALU.mult)

    es_xn = contextlib.ExitStack()
    XNb = [sb_scoped(es_xn, nc, f"{pfx}xnb{c}", [128, 1024], BF16) for c in range(NCH)]

    emit_ln(C, X, XNb, pfx + "ln")

    es_qk = contextlib.ExitStack()
    wr = [sb_scoped(es_qk, nc, f"{pfx}wr{i}", [128, 16 * 128], BF16) for i in range(4)]
    qh = sb_scoped(es_qk, nc, pfx + "qh", [128, 1024], BF16)
    el1 = sb_scoped(es_qk, nc, pfx + "el1", [128, 1024], F32)
    el2 = sb_scoped(es_qk, nc, pfx + "el2", [128, 1024], F32)

    for which, d_w, wmap, bias_sb, out_tiles in (
            ("q", dd["wqT"], wmq_sb, bq_sb, qe), ("k", dd["wkT"], wmk_sb, bk_sb, ke)):
        b_w = s_w.n
        b_pj, b_bh, b_qm = s_pj.n, s_bh.n, s_qm.n
        b_e1, b_e2, b_el = s_e1.n, s_e2.n, s_elu.n

        with nc.Block() as blk:
            @blk.gpsimd
            def _(g):
                for f in range(NCH):
                    if f >= 4:
                        g.wait_ge(s_pj.h, b_pj + f - 3)
                    g.dma_start(out=wr[f % 4][:].rearrange("p (c m) -> p c m", c=16),
                                in_=d_w[:, f * 128:(f + 1) * 128]
                                .rearrange("(c p) m -> p c m", p=128)) \
                        .then_inc(s_w.inc(16), 16)

            @blk.tensor
            def _(tn):
                for f in range(NCH):
                    j = f // 2
                    tn.wait_ge(s_w.h, b_w + 16 * (f + 1))
                    if f >= 2:
                        tn.wait_ge(s_bh.h, b_bh + f - 1)
                    for k in range(NCH):
                        for hh in range(2):
                            cl = slice(hh * 512, (hh + 1) * 512)
                            ins = tn.matmul(P[f % 2][:, cl],
                                            wr[f % 4][:, k * 128:(k + 1) * 128],
                                            XNb[k][:, cl], start=(k == 0), stop=(k == NCH - 1))
                    ins.then_inc(s_pj.inc(), 1)
                    tn.wait_ge(s_bh.h, b_bh + f + 1)
                    if f >= 4:
                        tn.wait_ge(s_elu.h, b_el + j - 1)
                    rows = slice(64 * (f % 2), 64 * (f % 2) + 64)
                    for hh in range(2):
                        cl = slice(hh * 512, (hh + 1) * 512)
                        ins = tn.matmul(P[2 + (j % 2)][rows, cl], wmap[:], qh[:, cl],
                                        start=True, stop=True)
                    ins.then_inc(s_qm.inc(), 1)

            @blk.scalar
            def _(sc):
                for f in range(NCH):
                    j = f // 2
                    sc.wait_ge(s_pj.h, b_pj + f + 1)
                    if f >= 1:
                        sc.wait_ge(s_qm.h, b_qm + f)
                    sc.activation(qh[:], P[f % 2][:], AF.Identity,
                                  bias=bias_sb[:, f:f + 1]).then_inc(s_bh.inc(), 1)
                    if f % 2 == 1:
                        sc.wait_ge(s_e1.h, b_e1 + j + 1)
                        if j >= 1:
                            sc.wait_ge(s_elu.h, b_el + j)
                        sc.activation(el2[:], el1[:], AF.Exp).then_inc(s_e2.inc(), 1)

            @blk.vector
            def _(v):
                s_hz = C.sem("hz")
                for j in range(8):
                    v.wait_ge(s_qm.h, b_qm + 2 * j + 2)
                    if j >= 1:
                        v.wait_ge(s_e2.h, b_e2 + j)
                    v.tensor_scalar(el1[:], P[2 + (j % 2)][:], 0.0, None, ALU.min) \
                        .then_inc(s_e1.inc(), 1)
                    v.wait_ge(s_e2.h, b_e2 + j + 1)
                    v.scalar_tensor_tensor(out_tiles[j][:], P[2 + (j % 2)][:], 0.0,
                                           el2[:], ALU.max, ALU.add) \
                        .then_inc(s_elu.inc(), 1)
                    if which == "k":
                        v.reduce_sum(kst[:, j:j + 1], out_tiles[j][:], axis=AX) \
                            .then_inc(s_hz.inc(), 1)
                v.memset(C.jC[:], 0.0)

    es_qk.close()

    with nc.Block() as blk:
        @blk.vector
        def _(v):
            v.wait_ge(C.sem("hz").h, C.sem("hz").n)
            for j in range(8):
                v.tensor_copy(ks_bd[0:64, 2 * j:2 * j + 1], kst[0:64, j:j + 1])
                v.tensor_copy(ks_bd[64:128, 2 * j + 1:2 * j + 2], kst[64:128, j:j + 1])
            v.memset(C.jC[:], 0.0)

    es_v = contextlib.ExitStack()
    vwb = [sb_scoped(es_v, nc, f"{pfx}vw{i}", [128, 16 * 512], BF16)
           for i in range(2)]
    vT = [sb_scoped(es_v, nc, f"{pfx}vT{t}", [128, D], BF16) for t in range(4)]
    emkT = [sb_scoped(es_v, nc, f"{pfx}emkT{t}", [128, 1024], BF16) for t in range(2)]

    for sblk in range(2):
        with nc.Block() as blk:
            @blk.gpsimd
            def _(g):
                for nq in range(4):
                    bb = sblk * 4 + nq
                    if bb >= 2:
                        g.wait_ge(s_vm.h, bA[s_vm] + 4 * (bb - 1))
                    for k in range(NCH):
                        g.dma_start(out=vwb[bb % 2][:, k * 512:(k + 1) * 512],
                                    in_=dd["wvT"][k * 128:(k + 1) * 128,
                                                  nq * 512:(nq + 1) * 512]) \
                            .then_inc(s_vw.inc(16), 16)

            @blk.tensor
            def _(tn):
                for nq in range(4):
                    bb = sblk * 4 + nq
                    tn.wait_ge(s_vw.h, bA[s_vw] + 16 * 16 * (bb + 1))
                    for tb in range(4):
                        tbg = sblk * 4 + tb
                        it = bb * 4 + tb
                        slot = it % 4
                        dst = P[slot // 2][:, (slot % 2) * 512:(slot % 2) * 512 + 512]
                        if it >= 4:
                            tn.wait_ge(s_ve.h, bA[s_ve] + it - 3)
                        for k in range(NCH):
                            tn.matmul(dst, XNb[k][:, tbg * 128:(tbg + 1) * 128],
                                      vwb[bb % 2][:, k * 512:(k + 1) * 512],
                                      start=(k == 0), stop=False)
                        ins = tn.matmul(dst, orow_bf[:],
                                        bv_bf[:, nq * 512:(nq + 1) * 512],
                                        start=False, stop=True)
                        ins.then_inc(s_vm.inc(), 1)

            @blk.scalar
            def _(sc):
                for nq in range(4):
                    for tb in range(4):
                        it = (sblk * 4 + nq) * 4 + tb
                        slot = it % 4
                        srcp = P[slot // 2][:, (slot % 2) * 512:(slot % 2) * 512 + 512]
                        sc.wait_ge(s_vm.h, bA[s_vm] + it + 1)
                        sc.activation(vT[tb][:, nq * 512:(nq + 1) * 512], srcp,
                                      AF.Identity).then_inc(s_ve.inc(), 1)

        with nc.Block() as blk:
            @blk.tensor
            def _(tn):
                for tb in range(4):
                    tbg = sblk * 4 + tb
                    for h in range(16):
                        it = (sblk * 4 + tb) * 16 + h
                        if it >= 2:
                            tn.wait_ge(s_te.h, bA[s_te] + it - 1)
                        j = h // 2
                        rows = slice(64 * (h % 2), 64 * (h % 2) + 64)
                        ib = 64 * (h % 2)
                        sl_c = (it % 2) * 1024
                        tn.transpose(P[2][:].bitcast(BF16)[:, sl_c:sl_c + 64],
                                     ke[j][rows, tbg * 128:(tbg + 1) * 128],
                                     ident_bf[ib:ib + 64, ib:ib + 64]).then_inc(s_tp.inc(), 1)
                    for h in range(16):
                        rows = slice(64 * (h % 2), 64 * (h % 2) + 64)
                        tn.wait_ge(s_te.h, bA[s_te] + (sblk * 4 + tb) * 16 + h + 1)
                        tn.matmul(P[3][rows, (h // 2) * 128:(h // 2) * 128 + 128],
                                  emkT[tb % 2][:, h * 64:(h + 1) * 64],
                                  vT[tb][:, h * 128:(h + 1) * 128],
                                  start=(tbg == 0), stop=(tbg == 7))

            @blk.vector
            def _(v):
                for tb in range(4):
                    for h in range(16):
                        it = (sblk * 4 + tb) * 16 + h
                        v.wait_ge(s_tp.h, bA[s_tp] + it + 1)
                        v.tensor_copy(emkT[tb % 2][:, h * 64:(h + 1) * 64],
                                      P[2][:].bitcast(BF16)[:, (it % 2) * 1024:(it % 2) * 1024 + 64]) \
                            .then_inc(s_te.inc(), 1)

    with nc.Block() as blk:
        @blk.vector
        def _(v):
            v.tensor_copy(kv_sb[:], P[3][:])
            v.memset(C.jC[:], 0.0)

    es_v.close()
    es_xn.close()

    with nc.Block() as blk:
        @blk.sync
        def _(sy):
            dmai(sy, ar_in.ap()[:, 0:1024], kv_sb[:].bitcast(F32), s_ar)
            dmai(sy, ar_in.ap()[:, 1024:1040], ks_bd[:].bitcast(F32), s_ar)

        @blk.gpsimd
        def _(g):
            g.wait_ge(s_ar.h, s_ar.n)
            g.collective_compute(
                "AllReduce", ALU.add,
                replica_groups=[[0, 1, 2, 3], [4, 5, 6, 7]],
                ins=[ar_in.ap()], outs=[ar_out.ap()],
            ).then_inc(s_cc.inc(), 1)
            g.wait_ge(s_cc.h, bA[s_cc] + 1)

    es_att = contextlib.ExitStack()
    att = [sb_scoped(es_att, nc, f"{pfx}att{h}", [128, 1024], BF16) for h in range(NCH)]
    at_sb = sb_scoped(es_att, nc, pfx + "at", [128, 1024], F32)
    kv_bf = sb_scoped(es_att, nc, pfx + "kvbf", [128, 1024], BF16)
    ksb_pad = sb_scoped(es_att, nc, pfx + "ksbp", [128, 32], BF16)

    with nc.Block() as blk:
        @blk.gpsimd
        def _(g):
            g.dma_start(out=kv_bf[:], in_=ar_out.ap()[:, 0:1024]) \
                .then_inc(s_ar.inc(16), 16)
            g.wait_ge(s_ar.h, s_ar.n)

        @blk.sync
        def _(sy):
            dmai(sy, ks_bd[:], ar_out.ap()[:, 1024:1040].bitcast(F32R), s_ar)
            sy.wait_ge(s_ar.h, s_ar.n)

    with nc.Block() as blk:
        @blk.vector
        def _(v):
            v.tensor_copy(ksb_pad[:].rearrange("p (h two) -> p h two", two=2)
                          [:, :, 0:1], ks_bd[:].bitcast(F32)
                          .rearrange("p (h o) -> p h o", o=1))
            v.memset(C.jC[:], 0.0)

    with nc.Block() as blk:
        @blk.tensor
        def _(tn):
            for h in range(16):
                if h >= 2:
                    tn.wait_ge(s_ad.h, bA[s_ad] + h - 1)
                for hh in range(2):
                    cl = slice(hh * 512, (hh + 1) * 512)
                    ins = tn.matmul(P[2 + (h % 2)][0:1, cl],
                                    ksb_pad[:, 2 * h:2 * h + 1],
                                    qe[h // 2][:, cl], start=True, stop=True)
                ins.then_inc(s_nm.inc(), 1)
                if h >= 2:
                    tn.wait_ge(s_ae.h, bA[s_ae] + h - 1)
                rows = slice(64 * (h % 2), 64 * (h % 2) + 64)
                for hh in range(2):
                    cl = slice(hh * 512, (hh + 1) * 512)
                    tn.matmul(P[h % 2][:, cl],
                              kv_bf[rows, (h // 2) * 128:(h // 2) * 128 + 128],
                              qe[h // 2][rows, cl], start=True, stop=True)
                tn.wait_ge(s_nr.h, bA[s_nr] + h + 1)
                for hh in range(2):
                    cl = slice(hh * 512, (hh + 1) * 512)
                    ins = tn.matmul(P[2 + (h % 2)][:, cl], C.consts["ones_row"][:],
                                    rcp[0:1, cl], start=True, stop=True)
                ins.then_inc(s_am.inc(), 1)

        @blk.vector
        def _(v):
            s_hz = C.sem("hz")
            for h in range(16):
                v.wait_ge(s_nm.h, bA[s_nm] + h + 1)
                v.tensor_scalar(rcp_t[:], P[2 + (h % 2)][0:1, :], 1e-6, None, ALU.add) \
                    .then_inc(s_hz.inc(), 1)
                v.wait_ge(s_hz.h, s_hz.n)
                v.reciprocal(rcp[:], rcp_t[:]).then_inc(s_nr.inc(), 1)
                v.wait_ge(s_ae.h, bA[s_ae] + h + 1)
                v.tensor_tensor(att[h][:], at_sb[:], P[2 + (h % 2)][:], ALU.mult) \
                    .then_inc(s_ad.inc(), 1)

        @blk.scalar
        def _(sc):
            for h in range(16):
                sc.wait_ge(s_am.h, bA[s_am] + h + 1)
                if h >= 1:
                    sc.wait_ge(s_ad.h, bA[s_ad] + h)
                sc.activation(at_sb[:], P[h % 2][:], AF.Identity).then_inc(s_ae.inc(), 1)

    wr2 = [sb_scoped(es_att, nc, f"{pfx}wr2_{i}", [128, 16 * 128], BF16) for i in range(4)]
    ot = sb_scoped(es_att, nc, pfx + "ot", [128, 1024], F32)
    with nc.Block() as blk:
        @blk.gpsimd
        def _(g):
            for ddx in range(NCH):
                if ddx >= 4:
                    g.wait_ge(s_om.h, bA[s_om] + ddx - 3)
                g.dma_start(out=wr2[ddx % 4][:].rearrange("p (c m) -> p c m", c=16),
                            in_=dd["woT"][:, ddx * 128:(ddx + 1) * 128]
                            .rearrange("(c p) m -> p c m", p=128)) \
                    .then_inc(s_w.inc(16), 16)

        @blk.tensor
        def _(tn):
            b_w = s_w.n - 16 * NCH
            for ddx in range(NCH):
                tn.wait_ge(s_w.h, b_w + 16 * (ddx + 1))
                if ddx >= 2:
                    tn.wait_ge(s_oe.h, bA[s_oe] + ddx - 1)
                for k in range(NCH):
                    for hh in range(2):
                        cl = slice(hh * 512, (hh + 1) * 512)
                        ins = tn.matmul(P[ddx % 2][:, cl], wr2[ddx % 4][:, k * 128:(k + 1) * 128],
                                        att[k][:, cl], start=(k == 0), stop=(k == NCH - 1))
                ins.then_inc(s_om.inc(), 1)

        @blk.scalar
        def _(sc):
            for ddx in range(NCH):
                sc.wait_ge(s_om.h, bA[s_om] + ddx + 1)
                if ddx >= 1:
                    sc.wait_ge(s_od.h, bA[s_od] + ddx)
                sc.activation(ot[:], P[ddx % 2][:], AF.Identity,
                              bias=bo_sb[:, ddx:ddx + 1]).then_inc(s_oe.inc(), 1)

        @blk.vector
        def _(v):
            for ddx in range(NCH):
                v.wait_ge(s_oe.h, bA[s_oe] + ddx + 1)
                v.tensor_tensor(X[ddx][:], X[ddx][:].bitcast(F32), ot[:], ALU.add) \
                    .then_inc(s_od.inc(), 1)

    es_att.close()
    es.close()
    return [(s_od, s_od.n)]



_EMIT = {"l0ffn": emit_l0ffn, "l0attn": emit_l0attn,
         "l1ffn": emit_l1ffn, "l1attn": emit_l1attn}
_SEQ = ["l0ffn", "l0attn", "l0ffn", "l1ffn", "l1attn", "l1ffn"]


def _declare(nc, keys):
    dd = {}
    for k in keys:
        shp, dt = _DRAM_SPECS[k]
        dd[k] = nc.dram_tensor(k, shp, dt, kind="ExternalInput").ap()
    return dd


def build_single(which):
    nc = bass.Bass("TRN2", target_bir_lowering=False)
    C = Ctx(nc)
    keys = ["xT"] + _NEED[which] + _CONST_KEYS
    dd = _declare(nc, keys)
    d_y = nc.dram_tensor("yT", [D, TOK], F32, kind="ExternalOutput").ap()
    X = [nc.alloc_sbuf_tensor(f"x{c}", [128, 1024], F32R) for c in range(NCH)]
    s_ld = C.sem("io_ld")
    s_st = C.sem("io_st")
    with nc.Block() as blk:
        @blk.sync
        def _(sy):
            emit_load_stream(C, X, dd["xT"], sy, s_ld)
            emit_load_consts(C, dd, sy, s_ld)
            sy.wait_ge(s_ld.h, s_ld.n)
    waits = _EMIT[which](C, X, dd, which[:2] + "_")
    emit_store_stream(C, X, d_y, s_st, waits=waits)
    return nc, keys


def build_fused():
    nc = bass.Bass("TRN2", target_bir_lowering=False)
    C = Ctx(nc)
    keys = sorted({k for ks in _NEED.values() for k in ks}) + ["xT"] + _CONST_KEYS
    dd = _declare(nc, keys)
    d_y = nc.dram_tensor("yT", [D, TOK], F32, kind="ExternalOutput").ap()
    X = [nc.alloc_sbuf_tensor(f"x{c}", [128, 1024], F32R) for c in range(NCH)]
    s_ld = C.sem("io_ld")
    s_st = C.sem("io_st")
    with nc.Block() as blk:
        @blk.sync
        def _(sy):
            emit_load_stream(C, X, dd["xT"], sy, s_ld)
            emit_load_consts(C, dd, sy, s_ld)
            sy.wait_ge(s_ld.h, s_ld.n)
    waits = []
    for idx, which in enumerate(_SEQ):
        with nc.named_scope(f"s{idx}_{which}"):
            waits = _EMIT[which](C, X, dd, f"s{idx}_")
    emit_store_stream(C, X, d_y, s_st, waits=waits)
    return nc, keys



_CACHE = {}


def kernel(**inputs):
    h = _prep(inputs)
    if "fused" not in _CACHE:
        _CACHE["fused"] = build_fused()
    nc, keys = _CACHE["fused"]
    in_maps = []
    for c in range(NCORE):
        m = {}
        for k in keys:
            if k == "xT":
                m[k] = np.ascontiguousarray(h["xT"][:, c * TOK:(c + 1) * TOK])
            elif k in _PERCORE:
                m[k] = h[k][c]
            else:
                m[k] = h[k]
        in_maps.append(m)
    res = run_bass_kernel_spmd(nc, in_maps, core_ids=list(range(NCORE)))
    full = np.concatenate([res.results[c]["yT"] for c in range(NCORE)], axis=1)
    return np.ascontiguousarray(full.T).reshape(2, 4096, D).astype(np.float32)
